# version 19
# baseline (speedup 1.0000x reference)
import numpy as np
import ml_dtypes
import concourse.bass as bass
import concourse.mybir as mybir
from concourse.bass import IndirectOffsetOnAxis
from concourse.bass_utils import run_bass_kernel_spmd

F32 = mybir.dt.float32
BF16 = mybir.dt.bfloat16
U32 = mybir.dt.uint32
AF = mybir.ActivationFunctionType
ALU = mybir.AluOpType
AX = mybir.AxisListType
BF = ml_dtypes.bfloat16

T, D, NH, HS, L, QL, E = 2048, 1024, 16, 64, 128, 128, 8
CAP = 2457
NT = 16
NCK = 4
NFILL = 23
EPS = 1e-5



class Prog:
    def __init__(self):
        self.ops = []
        self.cnt = {}

    SELF_SEM = {"ve": "ve", "sc": "sc"}

    def emit(self, eng, fn, waits=(), inc=None):
        if inc is not None and inc[0] in ("ve", "sc", "gpc"):
            waits = tuple(waits) + ((inc[0], self.cnt.get(inc[0], 0)),)
        import sys
        fr = sys._getframe(1)
        tag = []
        for _ in range(4):
            if fr is None:
                break
            tag.append(fr.f_lineno)
            fr = fr.f_back
        self._tag = tuple(tag)
        ws = {}
        for s, v in waits:
            if v and v > ws.get(s, 0):
                ws[s] = v
        if inc is not None:
            sem, n = inc
            self.cnt[sem] = self.cnt.get(sem, 0) + n
        self.ops.append((eng, fn, tuple(ws.items()), inc, self._tag))

    def val(self, sem):
        return self.cnt.get(sem, 0)

    def replay(self, eng_name, engine, sems):
        for eng, fn, waits, inc, tag in self.ops:
            if eng != eng_name:
                continue
            for sem, v in waits:
                engine.wait_ge(sems[sem], v)
            try:
                inst = fn(engine)
            except Exception as ex:
                raise RuntimeError(f"op failed, emitted at lines {tag}: {ex}") from ex
            if inc is not None:
                sem, n = inc
                inst.then_inc(sems[sem], n)


class Res:

    def __init__(self):
        self.wsem = None
        self.readers = []

    def for_write(self):
        ws = list(self.readers)
        if self.wsem:
            ws.append(self.wsem)
        return ws

    def wrote(self, sem, val):
        self.wsem = (sem, val)
        self.readers = []

    def for_read(self):
        return [self.wsem] if self.wsem else []

    def read(self, sem, val):
        self.readers.append((sem, val))


class Ctx:
    pass



def host_consts():
    c = {}
    c["identbf"] = np.eye(128, dtype=BF)
    c["identf"] = np.eye(128, dtype=np.float32)
    c["triu"] = np.triu(np.ones((128, 128), np.float32))
    c["sl8"] = np.triu(np.ones((8, 8), np.float32), 1)
    c["sl16"] = np.triu(np.ones((16, 16), np.float32), 1)
    c["ones128"] = np.ones((1, 128), np.float32)
    c["onesrowbf"] = np.ones((1, 128), BF)
    c["onescolbf"] = np.ones((128, 1), BF)
    c["ones16"] = np.ones((1, 16), np.float32)
    c["iota8"] = np.tile(np.arange(8, dtype=np.float32), (128, 1))
    c["iota8m"] = (c["iota8"] - 999.0).astype(np.float32)
    c["iotap"] = np.arange(128, dtype=np.float32).reshape(128, 1)
    c["iota23"] = (128.0 * np.arange(NFILL, dtype=np.float32)).reshape(1, NFILL)
    c["thr16"] = (128.0 * np.arange(16, dtype=np.float32)).reshape(16, 1) * np.ones((1, 8), np.float32)
    p = np.arange(128).reshape(128, 1)
    f = np.arange(512).reshape(1, 512)
    m = np.zeros((128, 2048), np.float32)
    for j in range(4):
        m[:, 512 * j:512 * j + 512] = (f >= (128 * j + p)).astype(np.float32)
    c["maskc"] = m.astype(BF)
    c["ones2048bf"] = np.ones((128, 2048), BF)
    sel = np.zeros((16, 4, 512), np.float32)
    for tc in range(4):
        sel[4 * (tc + 1):, tc, :] = 1.0
    c["selbf"] = sel.reshape(16, 2048).astype(BF)
    oh16 = np.zeros((128, 256), np.float32)
    ohr = np.zeros((128, 256), np.float32)
    for h in range(16):
        oh16[:, 16 * h + h] = 1.0
        ohr[64, 16 * h + h] = 1.0
    c["oh16bf"] = oh16.astype(BF)
    c["ohrow16"] = ohr
    return c



def build(stage="full"):
    nc = bass.Bass("TRN2", target_bir_lowering=False)
    g = Ctx()
    g.nc = nc
    g.stage = stage
    g.P = Prog()

    di = lambda n, sh, dt=F32: nc.dram_tensor(n, sh, dt, kind="ExternalInput").ap()
    do = lambda n, sh, dt=F32: nc.dram_tensor(n, sh, dt, kind="ExternalOutput").ap()
    dint = lambda n, sh, dt=F32: nc.dram_tensor(n, sh, dt, kind="Internal").ap()

    g.x = di("x", [T, D])
    g.wdq = di("wdq", [QL, D]); g.wuq = di("wuq", [D, QL])
    g.wdkv = di("wdkv", [L, D]); g.wuk = di("wuk", [D, L]); g.wuv = di("wuv", [D, L])
    g.wo = di("wo", [D, D])
    g.wq = di("wq", [D, D]); g.wk = di("wk", [D, D]); g.wv = di("wv", [D, D])
    g.bq = di("bq", [1, D]); g.bk = di("bk", [1, D]); g.bv = di("bv", [1, D])
    g.we = di("we", [E * D, D]); g.be = di("be", [E, D])
    g.wsw = di("wsw", [E, D]); g.bsw = di("bsw", [1, E])
    g.gamma = di("gamma", [1, D]); g.beta = di("beta", [1, D])
    g.rankmask = di("rankmask", [8, 1])
    g.hc = host_consts()
    for k, v in g.hc.items():
        setattr(g, "c_" + k, di(k, list(v.shape), BF16 if v.dtype == BF else F32))
    g.out = do("out", [T, D])
    if stage == "x1":
        g.dbg = do("dbg_x1", [T, D], F32)
    elif stage == "xy":
        g.dbg = do("dbg_xy", [T, D], F32)
    elif stage == "ckv":
        g.dbg = do("dbg_ckv", [128, 2048], BF16)
        g.dbg5 = do("dbg_xT", [128, 16384], BF16)
        g.dbg2 = do("dbg_qt", [128, 16384], BF16)
        g.dbg3 = do("dbg_keff", [128, 1024], BF16)
        g.dbg4 = do("dbg_veff", [128, 1024], BF16)
    elif stage == "hid":
        g.dbg = do("dbg_hid", [T, D], BF16)
    elif stage == "route":
        g.dbg = do("dbg_route", [128, 16 * 8], F32)
    elif stage == "full2":
        g.dbg = do("dbg_route", [128, 16 * 8], F32)
        g.dbg6 = do("dbg_eo", [T, D], F32)
        g.dbg7 = do("dbg_tail", [128, 5 * D], F32)
    elif stage == "tl0":
        g.dbg6 = do("dbg_eo", [T, D], F32)

    g.d_xbf = dint("d_xbf", [T, D], BF16)
    g.d_x1bf = dint("d_x1bf", [T, D], BF16)
    g.d_hidbf = dint("d_hidbf", [T, D], BF16)
    g.d_x1f = dint("d_x1f", [T, D], F32)
    g.d_webf = dint("d_webf", [E * D, D], BF16)
    g.d_wt = dint("d_wt", [E * 128, 9 * 1024], BF16)
    g.d_wqkvbf = dint("d_wqkvbf", [3 * D, D], BF16)
    g.d_wobf = dint("d_wobf", [D, D], BF16)
    g.d_smallbf = dint("d_smallbf", [3 * 128, D], BF16)
    g.d_srt = dint("d_srt", [NFILL * 128, D], BF16)
    g.d_srto = dint("d_srto", [NFILL * 128, D], F32)
    g.d_ccin = dint("d_ccin", [1, 8])
    g.d_ccout = dint("d_ccout", [8, 8])

    from contextlib import ExitStack
    with ExitStack() as ctx:
        S = lambda n, sh, dt=F32: ctx.enter_context(nc.sbuf_tensor(n, sh, dt))
        Ps = lambda n, sh: ctx.enter_context(nc.psum_tensor(n, sh, F32))

        g.fmA = S("fmA", [128, 16384], BF16)
        g.fmB = S("fmB", [128, 32768], BF16)
        g.x_tok = g.fmB[:].bitcast(F32)
        g.vs65 = S("vs65", [128, NH * 16 * 65], BF16)
        g.ctxall = S("ctxall", [128, 16 * 512], BF16)
        g.attb = S("attb", [128, 2 * 2048], BF16)
        g.wqkvT = g.ctxall
        g.qlb = S("qlb", [128, 1024], BF16)
        g.mlakv = S("mlakv", [128, 4096], BF16)
        g.ckvT = g.mlakv[:, 0:2048]
        g.ckvS = g.mlakv[:, 2048:4096]
        g.q2T = g.attb[:, 0:2048]
        g.gbc = g.mlakv[:].bitcast(F32)
        g.wsmall = S("wsmall", [128, 3 * 1024], BF16)
        g.prepw = S("prepw", [128, 2048], BF16)
        g.wuk_s = g.prepw[:, 0:1024]
        g.wuv_s = g.prepw[:, 1024:2048]
        g.keff = S("keff", [128, 1024], BF16)
        g.veff = S("veff", [128, 1024], BF16)
        g.wtmps = S("wtmps", [128, 2048], BF16)
        g.wtmp = g.wtmps[:, 0:1024]
        g.wtmp2 = g.wtmps[:, 1024:2048]
        g.wswT = S("wswT", [128, 64], BF16)
        g.bqkv = S("bqkv", [128, 24])
        g.u65 = S("u65", [128, 512], BF16)
        g.den_s = S("den_s", [128, 512])
        g.r16 = S("r16", [128, 512])
        g.rtok = S("rtok", [128, 64])
        g.stat = S("stat", [128, 48])
        g.lg = S("lg", [128, 128])
        g.route = S("route", [128, 16])
        g.rp = S("rp", [128, 16])
        g.onehot = S("onehot", [128, 128])
        g.kept = S("kept", [128, 16])
        g.posl = S("posl", [128, 16])
        g.dest32 = S("dest32", [128, NT], U32)
        g.widx32 = S("widx32", [128, NFILL], U32)
        g.cntrow = S("cntrow", [1, 8 * 17])
        g.adjrow = S("adjrow", [1, 128])
        g.segrow = S("segrow", [1, 8])
        g.corow = S("corow", [1, 8])
        g.gab = S("gab", [8, 8])
        g.small = S("small", [128, 64])
        g.svT = S("svT", [128, NH * 65], BF16)
        g.sv_s = S("sv_s", [128, 16])
        g.bcast8 = S("bcast8", [128, 32])
        g.stage_f = S("stage_f", [128, 2048])
        g.stage_b = S("stage_b", [128, 2048], BF16)
        g.stage_w = None
        g.bswr = S("bswr", [1, 8])
        g.rmask = S("rmask", [8, 1])
        g.onescol_f = S("onescol_f", [128, 1])
        _deferred = ("maskc", "ones2048bf", "selbf")
        for n in g.hc:
            if n in _deferred:
                continue
            setattr(g, n, S("s_" + n, list(g.hc[n].shape), BF16 if g.hc[n].dtype == BF else F32))
        g.maskc = g.wsmall[:, 0:2048]
        g.ones2048bf = g.prepw[:, 0:2048]
        g.selbf = g.wtmps[0:16, 0:2048]

        g.p_sc = Ps("p_sc", [128, 2048])
        g.p_ctx = Ps("p_ctx", [128, 512])
        g.p_den = Ps("p_den", [128, 512])
        g.p_m1 = Ps("p_m1", [128, 512])
        g.p_m2 = Ps("p_m2", [128, 512])

        for r in ["r_psc", "r_pscB", "r_pctx", "r_pden", "r_pm1", "r_pm2", "r_attb0", "r_attb1",
                  "r_attb2", "r_attb3",
                  "r_qlb", "r_qlb2", "r_u65", "r_dens", "r_r16", "r_rtok", "r_ctxall", "r_stagef",
                  "r_stageb", "r_fmA", "r_fmB", "r_vs65", "r_wtmp",
                  "r_wtmp2", "r_small", "r_svs", "r_sv", "r_mlakv"]:
            setattr(g, r, Res())
        g.r_xtok = g.r_fmB
        g.r_wqkvT = g.r_ctxall
        g.abres = [g.r_attb0, g.r_attb1, g.r_attb2, g.r_attb3]
        g.r_pscs = [g.r_psc, g.r_pscB]
        g.stage_w = g.stage_b[:, 1024:2048]
        g.r_stagew = g.r_stageb

        sems = {}
        for n in ["sy", "gp", "scq", "mm", "ve", "sc", "gpc"]:
            sems[n] = ctx.enter_context(nc.semaphore(n))
        g.sems = sems

        emit_program(g)

        block = ctx.enter_context(nc.Block())

        @block.sync
        def _(e):
            g.P.replay("sy", e, sems)

        @block.scalar
        def _(e):
            g.P.replay("sc", e, sems)

        @block.tensor
        def _(e):
            g.P.replay("te", e, sems)

        @block.vector
        def _(e):
            g.P.replay("ve", e, sems)

        @block.gpsimd
        def _(e):
            g.P.replay("gp", e, sems)

    return nc



def dma(g, q, out, in_, waits=(), transpose=False):
    eng = {"sy": "sy", "gp": "gp", "scq": "sc"}[q]
    sem = {"sy": "sy", "gp": "gp", "scq": "scq"}[q]
    g.P.emit(eng, (lambda o, i, t: lambda e: e.dma_start(out=o, in_=i, transpose=t))(out, in_, transpose),
             waits=waits, inc=(sem, 16))
    return g.P.val(sem)


def mm(g, out, lhsT, rhs, start, stop, waits=()):
    g.P.emit("te", (lambda o, l, r, s0, s1: lambda e: e.matmul(o, l, r, start=s0, stop=s1))(
        out, lhsT, rhs, start, stop), waits=waits, inc=("mm", 1))
    return g.P.val("mm")


def act(g, out, in_, func=AF.Copy, bias=0.0, scale=1.0, accum_out=None, waits=()):
    g.P.emit("sc", (lambda o, i, f, b, s, a: lambda e: e.activation(o, i, f, bias=b, scale=s, accum_out=a))(
        out, in_, func, bias, scale, accum_out), waits=waits, inc=("sc", 1))
    return g.P.val("sc")


def ve(g, name, *args, waits=(), **kw):
    g.P.emit("ve", (lambda n, a, k: lambda e: getattr(e, n)(*a, **k))(name, args, kw),
             waits=waits, inc=("ve", 1))
    return g.P.val("ve")


def gp_op(g, name, *args, waits=(), **kw):
    g.P.emit("gp", (lambda n, a, k: lambda e: getattr(e, n)(*a, **k))(name, args, kw),
             waits=waits, inc=("gpc", 1))
    return g.P.val("gpc")


def tr_mm(g, out_psum, in_sbuf, ident, waits=()):
    k = in_sbuf.shape[0]
    b = in_sbuf.base_partition()
    return mm(g, out_psum, in_sbuf, ident[b:b + k, b:b + k], True, True, waits=waits)



def emit_program(g):
    emit_consts(g)
    emit_x_path(g)
    emit_small_weights(g)
    emit_weprep(g)
    emit_mla(g)
    if g.stage == "ckv":
        return
    if g.stage == "xy":
        emit_dbg_tiles(g, g.dbg, lambda i: g.x_tok[:, 1024 * i:1024 * i + 1024], g.r_xtok)
        return
    emit_postln1(g)
    if g.stage == "x1":
        emit_dbg_tiles(g, g.dbg, lambda i: g.x_tok[:, 1024 * i:1024 * i + 1024], g.r_xtok)
        return
    emit_ffn_attn(g)
    if g.stage == "hid":
        emit_dbg_tiles(g, g.dbg, lambda i: g.fmA[:, 1024 * i:1024 * i + 1024], g.r_fmA)
        return
    emit_router(g)
    if g.stage == "route":
        emit_kept(g)
        emit_dbg_route(g)
        return
    emit_experts(g)
    if g.stage == "full2":
        for i in range(NT):
            dma(g, "sy", g.dbg6[128 * i:128 * i + 128, :], g.d_srto[128 * i:128 * i + 128, :],
                waits=(("scq", g.w_eo),) if i == 0 else ())
    emit_tail(g)
    if g.stage == "full2":
        emit_dbg_route(g)


def emit_dbg_tiles(g, dst, src_fn, res):
    w = 0
    for i in range(NT):
        w = dma(g, "sy", dst[128 * i:128 * i + 128, :], src_fn(i),
                waits=res.for_read() if i == 0 else ())
    g.P.emit("ve", (lambda: lambda e: e.memset(g.stat[:, 15:16], 0.0))(),
             waits=(("sy", w),), inc=("ve", 1))


def emit_dbg_route(g):
    v = ve(g, "tensor_copy", out=g.lg[:, 0:16], in_=g.route[:, 0:16])
    v = ve(g, "tensor_copy", out=g.lg[:, 16:32], in_=g.rp[:, 0:16])
    v = ve(g, "tensor_copy", out=g.lg[:, 32:48], in_=g.kept[:, 0:16])
    v = ve(g, "tensor_copy", out=g.lg[:, 48:64], in_=g.posl[:, 0:16])
    v = ve(g, "tensor_copy", out=g.lg[:, 64:80], in_=g.dest32[:, 0:16])
    w = dma(g, "sy", g.dbg, g.lg[:, 0:128], waits=(("ve", v),))
    g.P.emit("ve", (lambda: lambda e: e.memset(g.stat[:, 15:16], 0.0))(),
             waits=(("sy", w),), inc=("ve", 1))



def emit_consts(g):
    for n in g.hc:
        if n in ("maskc", "ones2048bf", "selbf"):
            continue
        dma(g, "sy", getattr(g, n)[:], getattr(g, "c_" + n))
    dma(g, "sy", g.rmask[:], g.rankmask)
    dma(g, "sy", g.bswr[:], g.bsw)
    g.w_consts = g.P.val("sy")
    ve(g, "memset", g.onescol_f[:], 1.0)
    g.v_ones = g.P.val("ve")



def emit_x_path(g):
    g.w_x = []
    for i in range(NT):
        g.w_x.append(dma(g, "scq", g.x_tok[:, 1024 * i:1024 * i + 1024],
                         g.x[128 * i:128 * i + 128, :]))
    g.r_xtok.wrote("scq", g.w_x[-1])
    wc = dma(g, "gp", g.d_xbf, g.x)
    for c in range(8):
        w = dma(g, "sy", g.fmA[:, 2048 * c:2048 * c + 2048], g.d_xbf[:, 128 * c:128 * c + 128],
                transpose=True, waits=(("gp", wc),) if c == 0 else ())
    g.w_xT = g.P.val("sy")
    g.r_fmA.wrote("sy", g.w_xT)



def emit_small_weights(g):
    P = g.P
    dma(g, "gp", g.d_smallbf[0:128, :], g.wdq)
    dma(g, "gp", g.d_smallbf[128:256, :], g.wdkv)
    wuq_flat = g.d_smallbf[256:384, :].rearrange("p f -> (p f)").rearrange("(a b) -> a b", a=1024)
    dma(g, "gp", wuq_flat, g.wuq)
    g.w_cast_small = g.P.val("gp")
    dma(g, "gp", g.wuk_s.rearrange("p (a b) -> p a b", a=8),
        g.wuk.rearrange("(a p) b -> p a b", p=128))
    dma(g, "gp", g.wuv_s.rearrange("p (a b) -> p a b", a=8),
        g.wuv.rearrange("(a p) b -> p a b", p=128))
    g.w_uvk = P.val("gp")
    dma(g, "gp", g.d_wobf, g.wo)
    g.w_cast_wo = P.val("gp")
    dma(g, "gp", g.d_wqkvbf[0:D, :], g.wq)
    dma(g, "gp", g.d_wqkvbf[D:2 * D, :], g.wk)
    dma(g, "gp", g.d_wqkvbf[2 * D:3 * D, :], g.wv)
    dma(g, "gp", g.d_webf, g.we)
    g.w_cast = P.val("gp")
    for c in range(8):
        dma(g, "sy", g.wsmall[:, 128 * c:128 * c + 128],
            g.d_smallbf[0:128, 128 * c:128 * c + 128], transpose=True,
            waits=(("gp", g.w_cast_small),) if c == 0 else ())
    for c in range(8):
        dma(g, "sy", g.wsmall[:, 1024 + 128 * c:1024 + 128 * c + 128],
            g.d_smallbf[128:256, 128 * c:128 * c + 128], transpose=True)
    for c in range(8):
        dma(g, "sy", g.wsmall[:, 2048 + 128 * c:2048 + 128 * c + 128],
            wuq_flat[128 * c:128 * c + 128, 0:128], transpose=True)
    g.w_small = P.val("sy")
    w1 = dma(g, "sy", g.wtmp.rearrange("p (a b) -> p a b", a=8),
             wuq_flat.rearrange("(a p) b -> p a b", p=128))
    w2 = dma(g, "sy", g.wtmp2[:, 0:1024], g.d_smallbf[0:128, :], waits=(("gp", g.w_cast_small),))
    g.r_wtmp.wrote("sy", w1)
    g.r_wtmp2.wrote("sy", w2)

    for c in range(8):
        m = mm(g, g.p_m1[:, 0:128], g.wtmp[:, 128 * c:128 * c + 128],
               g.wuk_s[:, 128 * c:128 * c + 128], c == 0, c == 7,
               waits=g.r_wtmp.for_read() + [("gp", g.w_uvk)] + g.r_pm1.for_write() if c == 0 else ())
    g.r_pm1.wrote("mm", m)
    g.r_wtmp.read("mm", m)
    v = ve(g, "tensor_copy", out=g.u65[:, 0:128], in_=g.p_m1[:, 0:128],
           waits=g.r_pm1.for_read() + g.r_u65.for_write())
    g.r_pm1.read("ve", v)
    g.r_u65.wrote("ve", v)
    for c in range(8):
        m = mm(g, g.p_m2[:, 0:128], g.wtmp2[:, 128 * c:128 * c + 128], g.u65[:, 0:128],
               True, True,
               waits=(g.r_wtmp2.for_read() + g.r_u65.for_read() + g.r_pm2.for_write()) if c == 0 else g.r_pm2.for_write())
        g.r_pm2.wrote("mm", m)
        v = ve(g, "tensor_copy", out=g.keff[:, 128 * c:128 * c + 128], in_=g.p_m2[:, 0:128],
               waits=g.r_pm2.for_read())
        g.r_pm2.read("ve", v)
    g.r_wtmp2.read("mm", m)
    g.r_u65.read("mm", m)
    g.v_keff = v

    for c in range(8):
        w = dma(g, "sy", g.wtmp[:, 0:1024], g.d_wobf[0:1024, 128 * c:128 * c + 128],
                transpose=True, waits=g.r_wtmp.for_write() + ([("gp", g.w_cast_wo)] if c == 0 else []))
        g.r_wtmp.wrote("sy", w)
        for nk in range(2):
            pdst = g.p_m1 if nk == 0 else g.p_m2
            rres = g.r_pm1 if nk == 0 else g.r_pm2
            m = mm(g, pdst[:, 0:512], g.wuv_s[:, 128 * c:128 * c + 128],
                   g.wtmp[:, 512 * nk:512 * nk + 512], c == 0, c == 7,
                   waits=([("sy", w)] + rres.for_write()) if c == 0 else [("sy", w)])
            if c == 0:
                rres.wrote("mm", m)
        g.r_wtmp.read("mm", m)
    g.r_pm1.wrote("mm", m)
    g.r_pm2.wrote("mm", m)
    for nk in range(2):
        pdst = g.p_m1 if nk == 0 else g.p_m2
        rres = g.r_pm1 if nk == 0 else g.r_pm2
        v = ve(g, "tensor_copy", out=g.veff[:, 512 * nk:512 * nk + 512], in_=pdst[:, 0:512],
               waits=rres.for_read())
        rres.read("ve", v)
    g.v_veff = v

    w = dma(g, "gp", g.stage_b[0:8, 0:1024], g.wsw, waits=g.r_stageb.for_write())
    g.r_stageb.wrote("gp", w)
    for c in range(8):
        m = tr_mm(g, g.p_m2[:, 0:8], g.stage_b[0:8, 128 * c:128 * c + 128], g.identbf,
                  waits=g.r_stageb.for_read() + g.r_pm2.for_write())
        g.r_pm2.wrote("mm", m)
        v = ve(g, "tensor_copy", out=g.wswT[:, 8 * c:8 * c + 8], in_=g.p_m2[:, 0:8],
               waits=g.r_pm2.for_read())
        g.r_pm2.read("ve", v)
    g.r_stageb.read("mm", m)
    g.v_wsw = v

    for pi, b in enumerate([g.bq, g.bk, g.bv]):
        for mt in range(8):
            dma(g, "sy", g.bqkv[:, 8 * pi + mt:8 * pi + mt + 1],
                b[0:1, 128 * mt:128 * mt + 128].rearrange("a p -> (a p)").unsqueeze(1))
    g.w_bqkv = P.val("sy")
    m = mm(g, g.p_m2[:, 0:8], g.ones128[:], g.bswr[:], True, True,
           waits=[("sy", g.w_consts)] + g.r_pm2.for_write())
    g.r_pm2.wrote("mm", m)
    v = ve(g, "tensor_copy", out=g.bcast8[:, 0:8], in_=g.p_m2[:, 0:8], waits=g.r_pm2.for_read())
    g.r_pm2.read("ve", v)
    g.v_bsw = v



def emit_weprep(g):
    P = g.P
    v0 = ve(g, "memset", g.stage_w[:, 0:1024], 0.0, waits=g.r_stagew.for_write())
    g.r_stagew.wrote("ve", v0)
    for e in range(E):
        w = dma(g, "sy", g.d_wt[128 * e:128 * e + 128, 8192:9216], g.stage_w[:, 0:1024],
                waits=g.r_stagew.for_read() if e == 0 else ())
        g.r_stagew.read("sy", w)
    for e in range(E):
        dma(g, "gp", g.d_wt[128 * e:128 * e + 1, 8192:9216], g.be[e:e + 1, :],
            waits=(("sy", P.val("sy")),) if e == 0 else ())
    for e in range(E):
        for j in range(8):
            w1 = dma(g, "sy", g.stage_w[:, 0:1024],
                     g.d_webf[1024 * e:1024 * e + 1024, 128 * j:128 * j + 128], transpose=True,
                     waits=g.r_stagew.for_write() + ([("gp", g.w_cast)] if (e == 0 and j == 0) else []))
            g.r_stagew.wrote("sy", w1)
            w2 = dma(g, "sy", g.d_wt[128 * e:128 * e + 128, 1024 * j:1024 * j + 1024],
                     g.stage_w[:, 0:1024])
            g.r_stagew.read("sy", w2)
    g.w_wt = P.val("sy")



def emit_mla(g):
    P = g.P
    xT = lambda c: g.fmA[:, 2048 * c:2048 * c + 2048]
    dump_ckv = (g.stage == "ckv")
    if dump_ckv:
        dma(g, "sy", g.dbg5, g.fmA[:, 0:16384], waits=(("sy", g.w_xT),))

    for nk in range(4):
        for c in range(8):
            m = mm(g, g.p_sc[:, 512 * nk:512 * nk + 512],
                   g.wsmall[:, 1024 + 128 * c:1024 + 128 * c + 128],
                   xT(c)[:, 512 * nk:512 * nk + 512], c == 0, c == 7,
                   waits=([("sy", g.w_xT), ("sy", g.w_small)] + g.r_psc.for_write()) if (nk == 0 and c == 0) else ())
        s = act(g, g.ckvT[:, 512 * nk:512 * nk + 512], g.p_sc[:, 512 * nk:512 * nk + 512],
                AF.Copy, waits=(("mm", m),))
        g.r_psc.read("sc", s)
    g.r_psc.wrote("mm", m)
    s_ckv = s
    for nk in range(4):
        for c in range(8):
            m = mm(g, g.p_ctx[:, 0:512], g.wsmall[:, 128 * c:128 * c + 128],
                   xT(c)[:, 512 * nk:512 * nk + 512], c == 0, c == 7,
                   waits=g.r_pctx.for_write() if c == 0 else ())
        g.r_pctx.wrote("mm", m)
        s = act(g, g.q2T[:, 512 * nk:512 * nk + 512], g.p_ctx[:, 0:512], AF.Copy,
                waits=g.r_pctx.for_read())
        g.r_pctx.read("sc", s)
    s_q2 = s
    g.r_fmA.read("mm", m)
    for si in range(16):
        m = tr_mm(g, g.p_m1[:, 0:128], g.ckvT[:, 128 * si:128 * si + 128], g.identbf,
                  waits=([("sc", s_ckv)] if si == 0 else []) + g.r_pm1.for_write())
        g.r_pm1.wrote("mm", m)
        s = act(g, g.ckvS[:, 128 * si:128 * si + 128], g.p_m1[:, 0:128], AF.Copy,
                waits=g.r_pm1.for_read())
        g.r_pm1.read("sc", s)
    s_ckvs = s
    for mt in range(8):
        for nk in range(4):
            m = mm(g, g.p_m2[:, 0:512], g.wsmall[:, 2048 + 128 * mt:2048 + 128 * mt + 128],
                   g.q2T[:, 512 * nk:512 * nk + 512], True, True,
                   waits=([("sc", s_q2)] if (mt == 0 and nk == 0) else []) + g.r_pm2.for_write()
                   + (g.r_fmA.for_write() if (mt == 0 and nk == 0) else []))
            g.r_pm2.wrote("mm", m)
            s = act(g, g.fmA[:, 2048 * mt + 512 * nk:2048 * mt + 512 * nk + 512],
                    g.p_m2[:, 0:512], AF.Copy, waits=g.r_pm2.for_read())
            g.r_pm2.read("sc", s)
    g.r_fmA.wrote("sc", s)
    s_qT = s
    if dump_ckv:
        w = dma(g, "sy", g.dbg, g.ckvT, waits=(("sc", s_ckv), ("ve", g.v_keff), ("ve", g.v_veff)))
        w = dma(g, "sy", g.dbg2, g.fmA[:, 0:16384], waits=(("sc", s_qT),))
        w = dma(g, "sy", g.dbg3, g.keff[:, 0:1024])
        w = dma(g, "sy", g.dbg4, g.veff[:, 0:1024])
        g.P.emit("ve", (lambda: lambda e: e.memset(g.stat[:, 15:16], 0.0))(),
                 waits=(("sy", w),), inc=("ve", 1))
        return
    g.w_maskc_q = dma(g, "scq", g.maskc, g.c_maskc, waits=(("mm", P.val("mm")),))

    gctr_box = [0]

    def ql_res(h):
        return g.r_qlb if h % 2 == 0 else g.r_qlb2

    def ql_ap(h):
        return g.qlb[:, 512 * (h % 2):512 * (h % 2) + 512]

    def emit_ql(tc, h, first=False):
        hb = 64 * (h % 2)
        lhsT_ql = g.keff[hb:hb + 64, 128 * (h // 2):128 * (h // 2) + 128]
        rhs_q = g.fmA[hb:hb + 64, 2048 * (h // 2) + 512 * tc:2048 * (h // 2) + 512 * tc + 512]
        m = mm(g, g.p_m2[:, 0:512], lhsT_ql, rhs_q, True, True,
               waits=([("ve", g.v_keff), ("sc", s_qT)] if first else [])
               + g.r_pm2.for_write() + (g.r_fmA.for_read() if first else []))
        g.r_pm2.wrote("mm", m)
        r = ql_res(h)
        v = ve(g, "tensor_copy", out=ql_ap(h), in_=g.p_m2[:, 0:512],
               waits=g.r_pm2.for_read() + r.for_write())
        g.r_pm2.read("ve", v)
        r.wrote("ve", v)

    def emit_scores(tc, h, gi):
        reg = gctr_box[0] % 2
        psc = g.p_sc[:, 1024 * reg:1024 * reg + 1024]
        rsc = g.r_pscs[reg]
        abres = g.abres[gctr_box[0] % 4]
        ab = g.attb[:, 1024 * (gctr_box[0] % 4):1024 * (gctr_box[0] % 4) + 1024]
        gctr_box[0] += 1
        r = ql_res(h)
        for j in range(2):
            si = 2 * gi + j
            m = mm(g, psc[:, 512 * j:512 * j + 512],
                   g.ckvT[:, 128 * si:128 * si + 128], ql_ap(h), True, True,
                   waits=(r.for_read() + rsc.for_write()) if j == 0 else ())
        rsc.wrote("mm", m)
        r.read("mm", m)
        s = act(g, ab, psc, AF.Exp, scale=0.125,
                waits=rsc.for_read() + abres.for_write())
        rsc.read("sc", s)
        abres.wrote("sc", s)
        if gi >= 2 * tc:
            v = ve(g, "tensor_mul", out=ab, in0=ab,
                   in1=g.maskc[:, 1024 * (gi - 2 * tc):1024 * (gi - 2 * tc) + 1024],
                   waits=abres.for_read() + [("scq", g.w_maskc_q)])
            abres.wrote("ve", v)
        return (tc, h, gi, ab, abres)

    def emit_ctxden(p):
        tc, h, gi, ab, abres = p
        ngi = 2 * (tc + 1)
        for j in range(2):
            si = 2 * gi + j
            wts = []
            wts2 = []
            if j == 0:
                wts += abres.for_read()
                if gi == 0:
                    wts += [("sc", s_ckvs)]
                    wts += g.r_pctx.for_write()
                if h == 0 and gi == 0:
                    wts2 += g.r_pden.for_write()
            m1 = mm(g, g.p_ctx[:, 0:512], g.ckvS[:, 128 * si:128 * si + 128],
                    ab[:, 512 * j:512 * j + 512], (gi == 0 and j == 0),
                    (gi == ngi - 1 and j == 1), waits=wts)
            m2 = mm(g, g.p_den[0:16, 0:512], g.oh16bf[:, 16 * h:16 * h + 16],
                    ab[:, 512 * j:512 * j + 512], (h == 0 and gi == 0 and j == 0),
                    (h == NH - 1 and gi == ngi - 1 and j == 1), waits=wts2)
        abres.read("mm", m2)
        if gi == ngi - 1:
            g.r_pctx.wrote("mm", m1)
            v = ve(g, "tensor_copy", out=g.ctxall[:, 512 * h:512 * h + 512],
                   in_=g.p_ctx[:, 0:512],
                   waits=g.r_pctx.for_read() + (g.r_ctxall.for_write() if h == 0 else []))
            g.r_pctx.read("ve", v)
        if h == NH - 1 and gi == ngi - 1:
            g.r_pden.wrote("mm", m2)

    def emit_epilogue(tc):
        v = ve(g, "reciprocal", out=g.r16[0:16, 0:512], in_=g.p_den[0:16, 0:512],
               waits=g.r_pden.for_read() + g.r_r16.for_write())
        g.r_pden.read("ve", v)
        g.r_r16.wrote("ve", v)
        for sub in range(4):
            m = tr_mm(g, g.p_m1[:, 0:16], g.r16[0:16, 128 * sub:128 * sub + 128], g.identf,
                      waits=g.r_r16.for_read() + g.r_pm1.for_write())
            g.r_pm1.wrote("mm", m)
            v = ve(g, "tensor_copy", out=g.rtok[:, 16 * sub:16 * sub + 16], in_=g.p_m1[:, 0:16],
                   waits=g.r_pm1.for_read() + (g.r_rtok.for_write() if sub == 0 else []))
            g.r_pm1.read("ve", v)
        g.r_r16.read("mm", m)
        g.r_rtok.wrote("ve", v)
        vv = v
        for pr in range(8):
            for hh in range(2):
                h = 2 * pr + hh
                m = mm(g, g.p_m2[64 * hh:64 * hh + 64, 0:512], g.veff[:, 64 * h:64 * h + 64],
                       g.ctxall[:, 512 * h:512 * h + 512], True, True,
                       waits=(g.r_ctxall.for_read() + [("ve", g.v_veff)] + g.r_pm2.for_write()) if hh == 0 else ())
            g.r_pm2.wrote("mm", m)
            v = ve(g, "tensor_copy", out=g.u65[:, 0:512], in_=g.p_m2[:, 0:512],
                   waits=g.r_pm2.for_read() + g.r_u65.for_write())
            g.r_pm2.read("ve", v)
            g.r_u65.wrote("ve", v)
            for sub in range(4):
                m = tr_mm(g, g.p_m1[:, 0:128], g.u65[:, 128 * sub:128 * sub + 128], g.identbf,
                          waits=g.r_u65.for_read() + g.r_pm1.for_write())
                g.r_pm1.wrote("mm", m)
                ti = 4 * tc + sub
                for hh in range(2):
                    h = 2 * pr + hh
                    xs = g.x_tok[:, 1024 * ti + 64 * h:1024 * ti + 64 * h + 64]
                    g.P.emit("ve", (lambda o, i0, s1, i1: lambda e: e.scalar_tensor_tensor(
                        out=o, in0=i0, scalar=s1, in1=i1, op0=ALU.mult, op1=ALU.add))(
                        xs, g.p_m1[0:128, 64 * hh:64 * hh + 64],
                        g.rtok[:, 16 * sub + h:16 * sub + h + 1], xs),
                        waits=tuple(g.r_pm1.for_read() + (g.r_rtok.for_read() if hh == 0 else [])
                                    + ([("scq", g.w_x[ti])] if (tc == 0 and pr == 0) else [])),
                        inc=("ve", 1))
                vv = g.P.val("ve")
                g.r_pm1.read("ve", vv)
                g.r_u65.read("mm", m)
        g.r_rtok.read("ve", vv)
        g.r_ctxall.read("mm", g.P.val("mm"))
        if g.stage not in ("xy", "ckv"):
            for sub in range(4):
                ti = 4 * tc + sub
                xs_t = g.x_tok[:, 1024 * ti:1024 * ti + 1024]
                last = emit_ln_once(g, xs_t, xs_t, None, affine=False,
                                    waits0=((("ve", vv),) if sub == 0 else ()))
                w = dma(g, "sy", g.d_x1f[128 * ti:128 * ti + 128, :], xs_t, waits=[last])
                g.r_xtok.read("sy", w)
                wb = dma(g, "gp", g.d_x1bf[128 * ti:128 * ti + 128, :], xs_t, waits=[last])
                g.r_xtok.read("gp", wb)
            g.w_x1spill = w
            g.w_x1bf = wb
        return g.P.val("ve")

    all_items = [(tc, h, gi) for tc in range(NCK) for h in range(NH)
                 for gi in range(2 * (tc + 1))]
    emit_ql(0, 0, first=True)
    pq = []
    due_epi = None

    def flush_one():
        nonlocal due_epi
        if due_epi is not None:
            emit_epilogue(due_epi)
            due_epi = None
        p = pq.pop(0)
        emit_ctxden(p)
        ptc, ph, pgi = p[0], p[1], p[2]
        if ph == NH - 1 and pgi == 2 * (ptc + 1) - 1:
            due_epi = ptc

    for tc, h, gi in all_items:
        pq.append(emit_scores(tc, h, gi))
        if gi == 0:
            if h + 1 < NH:
                emit_ql(tc, h + 1)
            elif tc + 1 < NCK:
                emit_ql(tc + 1, 0)
        if len(pq) > 2:
            flush_one()
    while pq:
        flush_one()
    vv_last = emit_epilogue(NCK - 1)
    g.r_fmA.read("mm", g.P.val("mm"))
    g.r_xtok.wrote("ve", vv_last)
    g.v_mla = vv_last



def emit_ln_once(g, src, dst, sq_scratch, affine, waits0=(), so=0):
    st = g.stat[:, so:so + 16]
    ve(g, "bn_stats", out=st[:, 0:6], in_=src[:, 0:512], waits=waits0)
    ve(g, "bn_stats", out=st[:, 6:12], in_=src[:, 512:1024], waits=waits0)
    ve(g, "bn_aggr", out=st[:, 12:14], in_=st[:, 0:12])
    v5 = ve(g, "tensor_scalar", out=st[:, 14:15], in0=st[:, 13:14], scalar1=EPS,
            scalar2=None, op0=ALU.add)
    s2 = act(g, st[:, 15:16], st[:, 14:15], AF.Sqrt, waits=(("ve", v5),))
    v6 = ve(g, "reciprocal", out=st[:, 0:1], in_=st[:, 15:16], waits=(("sc", s2),))
    v7 = ve(g, "tensor_mul", out=st[:, 1:2], in0=st[:, 12:13], in1=st[:, 0:1])
    v8 = ve(g, "tensor_scalar", out=st[:, 2:3], in0=st[:, 1:2], scalar1=-1.0,
            scalar2=None, op0=ALU.mult)
    s3 = ve(g, "tensor_scalar", out=dst, in0=src, scalar1=st[:, 0:1],
            scalar2=st[:, 2:3], op0=ALU.mult, op1=ALU.add)
    if affine:
        v9 = ve(g, "tensor_mul", out=dst, in0=dst, in1=g.gbc[:, 0:1024],
                waits=(("ve", g.v_gbc),))
        v9 = ve(g, "tensor_add", out=dst, in0=dst, in1=g.gbc[:, 1024:2048])
        return ("ve", v9)
    return ("ve", s3)


def emit_postln1(g):
    P = g.P
    w2 = 0
    for c in range(8):
        w2 = dma(g, "sy", g.fmA[:, 2048 * c:2048 * c + 2048], g.d_x1bf[:, 128 * c:128 * c + 128],
                 transpose=True, waits=([("gp", g.w_x1bf)] + g.r_fmA.for_write() + g.r_fmA.for_read()) if c == 0 else ())
    g.r_fmA.wrote("sy", w2)
    g.w_x1T = w2



def emit_ffn_attn(g):
    P = g.P
    g.w_ffnc_q = dma(g, "scq", g.ones2048bf, g.c_ones2048bf, waits=(("mm", P.val("mm")),))
    g.w_ffnc_q = dma(g, "scq", g.selbf, g.c_selbf)
    x1T = lambda c: g.fmA[:, 2048 * c:2048 * c + 2048]
    v_ones = ve(g, "memset",
                g.vs65[:, :].rearrange("p (a b) -> p a b", b=65)[:, :, 64:65], 1.0,
                waits=g.r_vs65.for_write())
    g.r_vs65.wrote("ve", v_ones)

    for proj in range(3):
        wsrc = g.d_wqkvbf[1024 * proj:1024 * proj + 1024, :]
        wch = []
        for kc in range(8):
            w = dma(g, "sy", g.wqkvT[:, 1024 * kc:1024 * kc + 1024],
                    wsrc[0:1024, 128 * kc:128 * kc + 128], transpose=True,
                    waits=(g.r_wqkvT.for_write() + [("gp", g.w_cast)]) if kc == 0 else ())
            wch.append(w)
        g.r_wqkvT.wrote("sy", w)
        for mt in range(8):
            for nk in range(4):
                pp, rr = (g.p_m2, g.r_pm2) if ((mt * 4 + nk) % 2 == 0) else (g.p_den, g.r_pden)
                m = mm(g, pp[:, 0:512], g.wqkvT[:, 128 * mt:128 * mt + 128],
                       x1T(0)[:, 512 * nk:512 * nk + 512], True, False,
                       waits=([("sy", wch[0]), ("sy", g.w_x1T), ("sy", g.w_bqkv)]
                              + rr.for_write()) if (mt == 0 and nk == 0) else rr.for_write())
                for kc in range(1, 8):
                    m = mm(g, pp[:, 0:512], g.wqkvT[:, 1024 * kc + 128 * mt:1024 * kc + 128 * mt + 128],
                           x1T(kc)[:, 512 * nk:512 * nk + 512], False, kc == 7,
                           waits=[("sy", wch[kc])] if (mt == 0 and nk == 0) else ())
                rr.wrote("mm", m)
                if proj < 2:
                    dst = g.fmB[:, 16384 * proj + 2048 * mt + 512 * nk:
                                16384 * proj + 2048 * mt + 512 * nk + 512]
                    s = ve(g, "tensor_scalar", out=dst, in0=pp[:, 0:512],
                           scalar1=g.bqkv[:, 8 * proj + mt:8 * proj + mt + 1],
                           scalar2=None, op0=ALU.add,
                           waits=rr.for_read() + (g.r_fmB.for_write() if (proj == 0 and mt == 0 and nk == 0) else []))
                    rr.read("ve", s)
                else:
                    abslot = (mt * 4 + nk) % 4
                    ares = g.abres[abslot]
                    abv = g.attb[:, 1024 * abslot:1024 * abslot + 512]
                    s = ve(g, "tensor_scalar", out=abv, in0=pp[:, 0:512],
                           scalar1=g.bqkv[:, 16 + mt:16 + mt + 1],
                           scalar2=None, op0=ALU.add,
                           waits=rr.for_read() + ares.for_write())
                    rr.read("ve", s)
                    ares.wrote("ve", s)
                    for sub in range(4):
                        si = 4 * nk + sub
                        for hh in range(2):
                            h = 2 * mt + hh
                            pt, rt = (g.p_m1, g.r_pm1) if ((sub * 2 + hh) % 2 == 0) else (g.p_ctx, g.r_pctx)
                            m2 = tr_mm(g, pt[:, 0:64],
                                       abv[64 * hh:64 * hh + 64, 128 * sub:128 * sub + 128],
                                       g.identbf,
                                       waits=(ares.for_read() if (sub == 0 and hh == 0) else []) + rt.for_write())
                            rt.wrote("mm", m2)
                            ares.read("mm", m2)
                            v = ve(g, "tensor_copy",
                                   out=g.vs65[:, 65 * (16 * h + si):65 * (16 * h + si) + 64],
                                   in_=pt[:, 0:64], waits=rt.for_read())
                            rt.read("ve", v)
        g.r_wqkvT.read("mm", g.P.val("mm"))
    g.r_fmB.wrote("sc", g.P.val("sc"))
    g.r_vs65.wrote("ve", g.P.val("ve"))
    s_qk = g.P.val("sc")

    for h in range(NH):
        for si in range(16):
            m = mm(g, g.p_m2[0:65, si:si + 1], g.vs65[:, 65 * (16 * h + si):65 * (16 * h + si) + 65],
                   g.onescolbf[:], True, True,
                   waits=(g.r_vs65.for_read() + [("ve", g.v_ones), ("sy", g.w_consts)]
                          + g.r_pm2.for_write()) if si == 0 else ())
        g.r_pm2.wrote("mm", m)
        v = ve(g, "tensor_copy", out=g.sv_s[0:65, 0:16],
               in_=g.p_m2[0:65, 0:16], waits=g.r_pm2.for_read() + g.r_svs.for_write())
        g.r_pm2.read("ve", v)
        g.r_svs.wrote("ve", v)
        s = act(g, g.stage_b[0:65, 0:16], g.sv_s[0:65, 0:16], AF.Copy,
                waits=g.r_svs.for_read() + g.r_stageb.for_write())
        g.r_svs.read("sc", s)
        g.r_stageb.wrote("sc", s)
        m = tr_mm(g, g.p_m1[0:16, 0:65], g.stage_b[0:65, 0:16], g.identbf,
                  waits=g.r_stageb.for_read() + g.r_pm1.for_write())
        g.r_stageb.read("mm", m)
        g.r_pm1.wrote("mm", m)
        v = ve(g, "tensor_copy", out=g.svT[0:16, 65 * h:65 * h + 65], in_=g.p_m1[0:16, 0:65],
               waits=g.r_pm1.for_read() + (g.r_sv.for_write() if h == 0 else []))
        g.r_pm1.read("ve", v)
    g.r_sv.wrote("ve", v)
    v_sv = v

    qTs = lambda h, t0: g.fmB[64 * (h % 2):64 * (h % 2) + 64,
                              2048 * (h // 2) + t0:2048 * (h // 2) + t0 + 512]
    kTs = lambda h, si: g.fmB[64 * (h % 2):64 * (h % 2) + 64,
                              16384 + 2048 * (h // 2) + 128 * si:16384 + 2048 * (h // 2) + 128 * si + 128]
    gctr_box = [0]

    def emit_scores(tc, h, gi):
        t0 = 512 * tc
        reg = gctr_box[0] % 2
        psc = g.p_sc[:, 1024 * reg:1024 * reg + 1024]
        rsc = g.r_pscs[reg]
        abres = g.abres[gctr_box[0] % 4]
        ab = g.attb[:, 1024 * (gctr_box[0] % 4):1024 * (gctr_box[0] % 4) + 1024]
        gctr_box[0] += 1
        for j in range(2):
            si = 2 * gi + j
            m = mm(g, psc[:, 512 * j:512 * j + 512], kTs(h, si), qTs(h, t0),
                   True, True,
                   waits=(rsc.for_write() + (g.r_fmB.for_read() if (tc == 0 and h == 0 and gi == 0) else [])
                          + ([("ve", v_sv)] if (tc == 0 and h == 0 and gi == 0) else [])) if j == 0 else ())
        rsc.wrote("mm", m)
        s = act(g, ab, psc, AF.Exp, scale=1.0 / 32.0,
                waits=rsc.for_read() + abres.for_write())
        rsc.read("sc", s)
        abres.wrote("sc", s)
        if gi >= 2 * tc:
            mk = g.maskc[:, 1024 * (gi - 2 * tc):1024 * (gi - 2 * tc) + 1024]
            v = ve(g, "select", out=ab, mask=mk.bitcast(mybir.dt.uint16), on_true=ab,
                   on_false=g.ones2048bf[:, 0:1024],
                   waits=abres.for_read() + [("scq", g.w_ffnc_q)])
            abres.wrote("ve", v)
        return (tc, h, gi, ab, abres)

    def emit_ctx(p):
        tc, h, gi, ab, abres = p
        ngi = 2 * (tc + 1)
        pc, rc = (g.p_ctx, g.r_pctx) if h % 2 == 0 else (g.p_m2, g.r_pm2)
        for j in range(2):
            si = 2 * gi + j
            wts = list(abres.for_read()) if j == 0 else []
            if gi == 0 and j == 0:
                wts += rc.for_write()
            m1 = mm(g, pc[0:65, 0:512],
                    g.vs65[:, 65 * (16 * h + si):65 * (16 * h + si) + 65],
                    ab[:, 512 * j:512 * j + 512], (gi == 0 and j == 0), False, waits=wts)
        abres.read("mm", m1)
        if gi != ngi - 1:
            return
        m1 = mm(g, pc[0:65, 0:512], g.svT[0:16, 65 * h:65 * h + 65],
                g.selbf[:, 512 * tc:512 * tc + 512], False, True,
                waits=g.r_sv.for_read() + [("scq", g.w_ffnc_q)])
        rc.wrote("mm", m1)
        g.r_sv.read("mm", m1)
        stg, rs = (g.den_s, g.r_dens) if h % 2 == 0 else (g.r16, g.r_r16)
        v_u = ve(g, "tensor_copy", out=stg[0:65, 0:512], in_=pc[0:65, 0:512],
                 waits=rc.for_read() + rs.for_write())
        rs.wrote("ve", v_u)
        rc.read("ve", v_u)
        return (tc, h, stg, rs)

    def emit_ctx_tail(p):
        tc, h, stg, rs = p
        for sub in range(4):
            trp, rt = (g.p_m1, g.r_pm1) if sub % 2 == 0 else (g.p_den, g.r_pden)
            m2 = tr_mm(g, trp[:, 0:65], stg[0:65, 128 * sub:128 * sub + 128],
                       g.identf, waits=rs.for_read() + rt.for_write())
            rt.wrote("mm", m2)
            rs.read("mm", m2)
            v = ve(g, "tensor_copy",
                   out=g.ctxall[:, 512 * h + 64 * sub:512 * h + 64 * sub + 64],
                   in_=trp[:, 0:64],
                   waits=rt.for_read() + (g.r_ctxall.for_write() if (h == 0 and sub == 0) else []))
            v2 = ve(g, "tensor_copy", out=g.rtok[:, 16 * sub + h:16 * sub + h + 1],
                    in_=trp[:, 64:65],
                    waits=(g.r_rtok.for_write() if (h == 0 and sub == 0) else ()))
            rt.read("ve", v2)
        if h == NH - 1:
            g.r_rtok.wrote("ve", v2)

    def emit_fepi(tc):
        v = ve(g, "reciprocal", out=g.rtok[:, 0:64], in_=g.rtok[:, 0:64])
        g.r_rtok.wrote("ve", v)
        for h in range(NH):
            for sub in range(4):
                ti = 4 * tc + sub
                g.P.emit("ve", (lambda o, i0, s1: lambda e: e.tensor_scalar(
                    out=o, in0=i0, scalar1=s1, scalar2=None, op0=ALU.mult))(
                    g.fmA[:, 1024 * ti + 64 * h:1024 * ti + 64 * h + 64],
                    g.ctxall[:, 512 * h + 64 * sub:512 * h + 64 * sub + 64],
                    g.rtok[:, 16 * sub + h:16 * sub + h + 1]),
                    waits=tuple(g.r_rtok.for_read() + g.r_ctxall.for_read()
                                + (g.r_fmA.for_write() if (tc == 0 and h == 0 and sub == 0) else [])),
                    inc=("ve", 1))
        vv = g.P.val("ve")
        g.r_ctxall.read("ve", vv)
        g.r_rtok.read("ve", vv)
        for sub in range(4):
            ti = 4 * tc + sub
            wh = dma(g, "sy", g.d_hidbf[128 * ti:128 * ti + 128, :],
                     g.fmA[:, 1024 * ti:1024 * ti + 1024], waits=(("ve", vv),) if sub == 0 else ())
            g.r_fmA.read("sy", wh)
        g.w_hidst = wh
        return vv

    all_items = [(tc, h, gi) for tc in range(NCK) for h in range(NH)
                 for gi in range(2 * (tc + 1))]
    pq = []
    due_epi = None
    due_tail = None

    def flush_one():
        nonlocal due_tail, due_epi
        if due_tail is not None:
            emit_ctx_tail(due_tail)
            due_tail = None
        if due_epi is not None:
            emit_fepi(due_epi)
            due_epi = None
        p = pq.pop(0)
        t = emit_ctx(p)
        if t is not None:
            due_tail = t
        ptc, ph, pgi = p[0], p[1], p[2]
        if ph == NH - 1 and pgi == 2 * (ptc + 1) - 1:
            due_epi = ptc

    for tc, h, gi in all_items:
        pq.append(emit_scores(tc, h, gi))
        if len(pq) > 2:
            flush_one()
    while pq:
        flush_one()
    if due_tail is not None:
        emit_ctx_tail(due_tail)
    vv = emit_fepi(NCK - 1)
    g.r_fmB.read("mm", g.P.val("mm"))
    g.r_vs65.read("mm", g.P.val("mm"))
    g.r_fmA.wrote("ve", vv)
    g.v_hid = vv



def emit_gbc(g):
    w = dma(g, "sy", g.stage_f[0:1, 0:1024], g.gamma, waits=g.r_stagef.for_write())
    w = dma(g, "sy", g.stage_f[0:1, 1024:2048], g.beta)
    g.r_stagef.wrote("sy", w)
    m = 0
    for r in range(2):
        for nk in range(2):
            src = g.stage_f[0:1, 1024 * r + 512 * nk:1024 * r + 512 * nk + 512]
            m = mm(g, g.p_m1[:, 0:512], g.ones128[:], src,
                   True, True, waits=g.r_stagef.for_read() + g.r_pm1.for_write()
                   + (g.r_mlakv.for_write() if (r == 0 and nk == 0) else []))
            g.r_pm1.wrote("mm", m)
            v = ve(g, "tensor_copy", out=g.gbc[:, 1024 * r + 512 * nk:1024 * r + 512 * nk + 512],
                   in_=g.p_m1[:, 0:512], waits=g.r_pm1.for_read())
            g.r_pm1.read("ve", v)
    g.r_stagef.read("mm", m)
    g.r_mlakv.wrote("ve", v)
    g.v_gbc = v


def emit_router(g):
    P = g.P
    emit_gbc(g)
    for c in range(8):
        w2 = dma(g, "sy", g.fmB[:, 2048 * c:2048 * c + 2048], g.d_hidbf[:, 128 * c:128 * c + 128],
                 transpose=True, waits=([("sy", g.w_hidst)] + g.r_fmB.for_write()) if c == 0 else ())
    g.r_fmB.wrote("sy", w2)
    for tc in range(NCK):
        for c in range(8):
            m = mm(g, g.p_m2[0:8, 0:512], g.wswT[:, 8 * c:8 * c + 8],
                   g.fmB[:, 2048 * c + 512 * tc:2048 * c + 512 * tc + 512], c == 0, c == 7,
                   waits=(g.r_fmB.for_read() + [("ve", g.v_wsw)] + g.r_pm2.for_write()) if c == 0 else ())
        g.r_pm2.wrote("mm", m)
        if tc == NCK - 1:
            g.r_fmB.read("mm", m)
        v = ve(g, "tensor_copy", out=g.den_s[0:8, 0:512], in_=g.p_m2[0:8, 0:512],
               waits=g.r_pm2.for_read() + g.r_dens.for_write())
        g.r_pm2.read("ve", v)
        g.r_dens.wrote("ve", v)
        for sub in range(4):
            ti = 4 * tc + sub
            m = tr_mm(g, g.p_m1[:, 0:8], g.den_s[0:8, 128 * sub:128 * sub + 128], g.identf,
                      waits=g.r_dens.for_read() + g.r_pm1.for_write())
            g.r_pm1.wrote("mm", m)
            v = ve(g, "tensor_copy", out=g.lg[:, 8 * ti:8 * ti + 8], in_=g.p_m1[:, 0:8],
                   waits=g.r_pm1.for_read())
            g.r_pm1.read("ve", v)
        g.r_dens.read("mm", m)
    for ti in range(NT):
        lg = g.lg[:, 8 * ti:8 * ti + 8]
        ve(g, "tensor_add", out=lg, in0=lg, in1=g.bcast8[:, 0:8],
           waits=(("ve", g.v_bsw),) if ti == 0 else ())
        v2 = ve(g, "reduce_max", out=g.stat[:, 0:1], in_=lg, axis=AX.X)
        vneg = ve(g, "tensor_scalar", out=g.stat[:, 1:2], in0=g.stat[:, 0:1], scalar1=-1.0,
                  scalar2=None, op0=ALU.mult)
        s = act(g, g.stage_f[:, 0:8], lg, AF.Exp, bias=g.stat[:, 1:2],
                waits=[("ve", vneg)] + g.r_stagef.for_write())
        g.r_stagef.wrote("sc", s)
        v3 = ve(g, "reduce_sum", out=g.stat[:, 2:3], in_=g.stage_f[:, 0:8], axis=AX.X,
                waits=g.r_stagef.for_read())
        g.r_stagef.read("ve", v3)
        ve(g, "reciprocal", out=g.rp[:, ti:ti + 1], in_=g.stat[:, 2:3])
        v4 = ve(g, "tensor_scalar", out=g.stage_f[:, 8:16], in0=lg, scalar1=g.stat[:, 0:1],
                scalar2=None, op0=ALU.is_ge)
        v5 = ve(g, "tensor_mul", out=g.stage_f[:, 16:24], in0=g.stage_f[:, 8:16], in1=g.iota8m[:])
        v6 = ve(g, "tensor_scalar", out=g.stage_f[:, 24:32], in0=g.stage_f[:, 16:24],
                scalar1=999.0, scalar2=None, op0=ALU.add)
        g.P.emit("ve", (lambda o, i: lambda e: e.tensor_reduce(
            out=o, in_=i, axis=AX.X, op=ALU.min))(g.route[:, ti:ti + 1], g.stage_f[:, 24:32]),
            waits=(("ve", v6),), inc=("ve", 1))
        ve(g, "tensor_scalar", out=g.onehot[:, 8 * ti:8 * ti + 8], in0=g.iota8[:],
           scalar1=g.route[:, ti:ti + 1], scalar2=None, op0=ALU.is_equal)
    v_route = g.P.val("ve")

    for ti in range(NT):
        m = mm(g, g.p_m2[0:8, ti:ti + 1], g.onehot[:, 8 * ti:8 * ti + 8], g.onescol_f[:],
               True, True,
               waits=([("ve", v_route), ("ve", g.v_ones)] + g.r_pm2.for_write()) if ti == 0 else ())
    g.r_pm2.wrote("mm", m)
    v = ve(g, "tensor_copy", out=g.small[0:8, 0:16], in_=g.p_m2[0:8, 0:16],
           waits=g.r_pm2.for_read() + g.r_small.for_write())
    g.r_pm2.read("ve", v)
    v_tot = ve(g, "reduce_sum", out=g.small[0:8, 48:49], in_=g.small[0:8, 0:16], axis=AX.X)
    g.r_small.wrote("ve", v_tot)
    m = tr_mm(g, g.p_m1[0:16, 0:8], g.small[0:8, 0:16], g.identf,
              waits=g.r_small.for_read() + g.r_pm1.for_write())
    g.r_pm1.wrote("mm", m)
    g.r_small.read("mm", m)
    v = ve(g, "tensor_copy", out=g.small[0:16, 16:24], in_=g.p_m1[0:16, 0:8],
           waits=g.r_pm1.for_read())
    g.r_pm1.read("ve", v)
    m = mm(g, g.p_m1[0:16, 0:8], g.sl16[:], g.small[0:16, 16:24], True, True,
           waits=[("ve", v), ("sy", g.w_consts)] + g.r_pm1.for_write())
    g.r_pm1.wrote("mm", m)
    v = ve(g, "tensor_copy", out=g.small[0:16, 24:32], in_=g.p_m1[0:16, 0:8],
           waits=g.r_pm1.for_read())
    g.r_pm1.read("ve", v)
    m = tr_mm(g, g.p_m1[0:8, 0:16], g.small[0:16, 24:32], g.identf,
              waits=[("ve", v)] + g.r_pm1.for_write())
    g.r_pm1.wrote("mm", m)
    v_excl = ve(g, "tensor_copy", out=g.small[0:8, 32:48], in_=g.p_m1[0:8, 0:16],
                waits=g.r_pm1.for_read())
    g.r_pm1.read("ve", v_excl)
    m = tr_mm(g, g.p_m1[0:1, 0:8], g.small[0:8, 48:49], g.identf,
              waits=[("ve", v_tot)] + g.r_pm1.for_write())
    g.r_pm1.wrote("mm", m)
    v = ve(g, "tensor_copy", out=g.cntrow[:, 0:8], in_=g.p_m1[0:1, 0:8], waits=g.r_pm1.for_read())
    g.r_pm1.read("ve", v)
    w1 = dma(g, "sy", g.d_ccin, g.cntrow[:, 0:8], waits=(("ve", v),))
    g.P.emit("gp", (lambda: lambda e: e.collective_compute(
        "AllGather", ALU.bypass, replica_groups=[[0, 1, 2, 3, 4, 5, 6, 7]],
        ins=[g.d_ccin.opt()], outs=[g.d_ccout.opt()]))(),
        waits=(("sy", w1),), inc=("gpc", 1))
    g.c_ccdone = g.P.val("gpc")

    m = mm(g, g.p_m1[0:16, 0:8], g.ones16[:], g.cntrow[:, 0:8], True, True,
           waits=[("ve", v)] + g.r_pm1.for_write())
    g.r_pm1.wrote("mm", m)
    v = ve(g, "tensor_copy", out=g.small[0:16, 50:58], in_=g.p_m1[0:16, 0:8],
           waits=g.r_pm1.for_read())
    g.r_pm1.read("ve", v)
    v = ve(g, "tensor_tensor", out=g.small[0:16, 50:58], in0=g.small[0:16, 50:58],
           in1=g.thr16[:], op=ALU.is_gt)
    m = mm(g, g.p_m1[0:8, 0:1], g.small[0:16, 50:58], g.onescol_f[0:16, 0:1], True, True,
           waits=[("ve", v), ("ve", g.v_ones)] + g.r_pm1.for_write())
    g.r_pm1.wrote("mm", m)
    v = ve(g, "tensor_scalar", out=g.small[0:8, 58:59], in0=g.p_m1[0:8, 0:1], scalar1=128.0,
           scalar2=None, op0=ALU.mult, waits=g.r_pm1.for_read())
    g.r_pm1.read("ve", v)
    m = mm(g, g.p_m2[0:8, 0:1], g.sl8[:], g.small[0:8, 58:59], True, True,
           waits=[("ve", v)] + g.r_pm2.for_write())
    g.r_pm2.wrote("mm", m)
    v_seg = ve(g, "tensor_copy", out=g.small[0:8, 59:60], in_=g.p_m2[0:8, 0:1],
               waits=g.r_pm2.for_read())
    g.r_pm2.read("ve", v_seg)
    m = tr_mm(g, g.p_m1[0:1, 0:8], g.small[0:8, 59:60], g.identf,
              waits=[("ve", v_seg)] + g.r_pm1.for_write())
    g.r_pm1.wrote("mm", m)
    v_segr = ve(g, "tensor_copy", out=g.segrow[:], in_=g.p_m1[0:1, 0:8], waits=g.r_pm1.for_read())
    g.r_pm1.read("ve", v_segr)
    m = mm(g, g.p_m1[:, 0:8], g.ones128[:], g.segrow[:], True, True,
           waits=[("ve", v_segr)] + g.r_pm1.for_write())
    g.r_pm1.wrote("mm", m)
    v_segb = ve(g, "tensor_copy", out=g.bcast8[:, 16:24], in_=g.p_m1[:, 0:8],
                waits=g.r_pm1.for_read())
    g.r_pm1.read("ve", v_segb)

    v_rpl = ve(g, "tensor_scalar", out=g.small[0:8, 32:48], in0=g.small[0:8, 32:48],
               scalar1=-1.0, scalar2=None, op0=ALU.add, waits=(("ve", v_excl),))
    pools = [(g.p_m2, g.r_pm2), (g.p_den, g.r_pden)]
    tpools = [(g.p_m1, g.r_pm1), (g.p_ctx, g.r_pctx)]
    for ti in range(NT):
        psu, rsu = pools[ti % 2]
        ptr_, rtr = tpools[ti % 2]
        m = mm(g, psu[:, 0:8], g.triu[:], g.onehot[:, 8 * ti:8 * ti + 8], True, True,
               waits=([("sy", g.w_consts)] if ti == 0 else []) + rsu.for_write())
        rsu.wrote("mm", m)
        m2 = tr_mm(g, ptr_[0:8, 0:128], g.onehot[:, 8 * ti:8 * ti + 8], g.identf,
                   waits=rtr.for_write())
        rtr.wrote("mm", m2)
        v = ve(g, "tensor_copy", out=g.den_s[0:8, 128 * (ti % 4):128 * (ti % 4) + 128],
               in_=ptr_[0:8, 0:128], waits=rtr.for_read() + (g.r_dens.for_write() if ti < 4 else []))
        rtr.read("ve", v)
        v = ve(g, "tensor_mul", out=g.stage_f[:, 0:8], in0=psu[:, 0:8],
               in1=g.onehot[:, 8 * ti:8 * ti + 8], waits=rsu.for_read() + g.r_stagef.for_write())
        rsu.read("ve", v)
        g.r_stagef.wrote("ve", v)
        v_par = ve(g, "reduce_sum", out=g.stat[:, 0:1], in_=g.stage_f[:, 0:8], axis=AX.X)
        g.r_stagef.read("ve", v_par)
        m3 = mm(g, ptr_[0:128, 8:9], g.den_s[0:8, 128 * (ti % 4):128 * (ti % 4) + 128],
                g.small[0:8, 32 + ti:32 + ti + 1], True, True,
                waits=[("ve", v_rpl), ("ve", g.P.val("ve"))] + rtr.for_write())
        rtr.wrote("mm", m3)
        v = ve(g, "tensor_add", out=g.posl[:, ti:ti + 1], in0=g.stat[:, 0:1],
               in1=ptr_[0:128, 8:9], waits=rtr.for_read())
        rtr.read("ve", v)
        v = ve(g, "tensor_mul", out=g.stage_f[:, 8:16], in0=g.onehot[:, 8 * ti:8 * ti + 8],
               in1=g.bcast8[:, 16:24], waits=(("ve", v_segb),) if ti == 0 else ())
        v = ve(g, "reduce_sum", out=g.stat[:, 1:2], in_=g.stage_f[:, 8:16], axis=AX.X)
        v = ve(g, "tensor_add", out=g.stat[:, 2:3], in0=g.stat[:, 1:2], in1=g.posl[:, ti:ti + 1])
        ve(g, "tensor_copy", out=g.dest32[:, ti:ti + 1], in_=g.stat[:, 2:3])
    g.v_dest = g.P.val("ve")

    v = ve(g, "memset", g.adjrow[:, 0:23], 0.0)
    for e in range(8):
        v1 = ve(g, "tensor_scalar", out=g.adjrow[:, 32:55], in0=g.iota23[:],
                scalar1=g.segrow[0:1, e:e + 1], scalar2=None, op0=ALU.is_ge,
                waits=(("ve", v_segr),) if e == 0 else ())
        v = ve(g, "tensor_add", out=g.adjrow[:, 0:23], in0=g.adjrow[:, 0:23],
               in1=g.adjrow[:, 32:55])
    v = ve(g, "tensor_scalar", out=g.adjrow[:, 0:23], in0=g.adjrow[:, 0:23], scalar1=128.0,
           scalar2=-128.0, op0=ALU.mult, op1=ALU.add)
    m = mm(g, g.p_m1[:, 0:23], g.ones128[:], g.adjrow[:, 0:23], True, True,
           waits=[("ve", v)] + g.r_pm1.for_write())
    g.r_pm1.wrote("mm", m)
    g.P.emit("ve", (lambda o, a, b: lambda e: e.scalar_tensor_tensor(
        out=o, in0=a, scalar=1.0, in1=b, op0=ALU.mult, op1=ALU.add))(
        g.small[:, 40:63], g.p_m1[:, 0:23], g.iotap[:].broadcast_to([128, 23])),
        waits=tuple(g.r_pm1.for_read() + g.r_small.for_write() + [("sy", g.w_consts),
                    ("ve", g.P.val("ve"))]),
        inc=("ve", 1))
    v = g.P.val("ve")
    g.r_pm1.read("ve", v)
    g.r_small.wrote("ve", v)
    g.v_widx = ve(g, "tensor_copy", out=g.widx32[:, 0:23], in_=g.small[:, 40:63],
                  waits=g.r_small.for_read())
    g.r_small.read("ve", g.v_widx)


def emit_kept(g):
    w2 = dma(g, "sy", g.gab[:], g.d_ccout, waits=(("gpc", g.c_ccdone),))
    m = mm(g, g.p_m1[0:1, 0:8], g.rmask[:], g.gab[:], True, True,
           waits=[("sy", w2), ("sy", g.w_consts)] + g.r_pm1.for_write())
    g.r_pm1.wrote("mm", m)
    v_co = ve(g, "tensor_copy", out=g.corow[:], in_=g.p_m1[0:1, 0:8], waits=g.r_pm1.for_read())
    g.r_pm1.read("ve", v_co)
    m = mm(g, g.p_m1[:, 0:8], g.ones128[:], g.corow[:], True, True,
           waits=[("ve", v_co)] + g.r_pm1.for_write())
    g.r_pm1.wrote("mm", m)
    v_cob = ve(g, "tensor_copy", out=g.bcast8[:, 8:16], in_=g.p_m1[:, 0:8],
               waits=g.r_pm1.for_read())
    g.r_pm1.read("ve", v_cob)
    for ti in range(NT):
        v = ve(g, "tensor_mul", out=g.stage_f[:, 0:8], in0=g.onehot[:, 8 * ti:8 * ti + 8],
               in1=g.bcast8[:, 8:16],
               waits=([("ve", v_cob)] + g.r_stagef.for_write()) if ti == 0 else ())
        v = ve(g, "reduce_sum", out=g.stat[:, 0:1], in_=g.stage_f[:, 0:8], axis=AX.X)
        v = ve(g, "tensor_add", out=g.stat[:, 1:2], in0=g.stat[:, 0:1], in1=g.posl[:, ti:ti + 1])
        ve(g, "tensor_scalar", out=g.kept[:, ti:ti + 1], in0=g.stat[:, 1:2],
           scalar1=float(CAP), scalar2=None, op0=ALU.is_lt)
    g.r_stagef.wrote("ve", g.P.val("ve"))
    g.v_kept = g.P.val("ve")



def emit_experts(g):
    P = g.P
    w_scat = 0
    for ti in range(NT):
        g.P.emit("gp", (lambda o, i, ix: lambda e: e.indirect_dma_start(
            out=o, out_offset=IndirectOffsetOnAxis(ap=ix, axis=0), in_=i, in_offset=None))(
            g.d_srt, g.fmA[:, 1024 * ti:1024 * ti + 1024], g.dest32[:, ti:ti + 1]),
            waits=((("ve", g.v_dest),) + tuple(g.r_fmA.for_read())) if ti == 0 else (),
            inc=("gp", 16))
        w_scat = g.P.val("gp")
    g.r_fmA.read("gp", w_scat)

    last_v_evac = [0, 0]
    last_st = [0, 0]
    last_m_ps = [0, 0]
    last_m_wb = [0, 0, 0]
    last_m_srt = [0, 0, 0, 0]
    last_m_ld = [0, 0, 0, 0]
    for f in range(NFILL):
        p2, p3, p4 = f % 2, f % 3, f % 4
        wb = g.fmB[:, 9216 * p3:9216 * p3 + 9216]
        g.P.emit("gp", (lambda o, ix: lambda e: e.indirect_dma_start(
            out=o, out_offset=None, in_=g.d_wt,
            in_offset=IndirectOffsetOnAxis(ap=ix, axis=0)))(wb, g.widx32[:, f:f + 1]),
            waits=((("ve", g.v_widx), ("sy", g.w_wt)) + tuple(g.r_fmB.for_read() + g.r_fmB.for_write())
                   if f < 3 else (("mm", last_m_wb[p3]),)),
            inc=("gp", 16))
        w_w = g.P.val("gp")
        sslot = g.ctxall[:, 1024 * p4:1024 * p4 + 1024]
        w_s = dma(g, "sy", sslot, g.d_srt[128 * f:128 * f + 128, :],
                  waits=([("gp", w_scat)] + g.r_ctxall.for_read() + g.r_ctxall.for_write())
                  if f < 4 else [("mm", last_m_ld[p4])])
        srtT = g.vs65[:, 1024 * p4:1024 * p4 + 1024]
        for kc in range(8):
            m = tr_mm(g, g.p_m1[:, 0:128], sslot[:, 128 * kc:128 * kc + 128],
                      g.identbf,
                      waits=([("sy", w_s)] + g.r_pm1.for_write()
                             + (g.r_vs65.for_write() if f < 4 else [("mm", last_m_srt[p4])]))
                      if kc == 0 else g.r_pm1.for_write())
            g.r_pm1.wrote("mm", m)
            if kc == 7:
                last_m_ld[p4] = m
            v = ve(g, "tensor_copy", out=srtT[:, 128 * kc:128 * kc + 128], in_=g.p_m1[:, 0:128],
                   waits=g.r_pm1.for_read())
            g.r_pm1.read("ve", v)
        v_t = v
        psl = g.p_sc[:, 1024 * p2:1024 * p2 + 1024]
        for nk in range(2):
            for kc in range(8):
                m = mm(g, psl[:, 512 * nk:512 * nk + 512], srtT[:, 128 * kc:128 * kc + 128],
                       wb[:, 1024 * kc + 512 * nk:1024 * kc + 512 * nk + 512], kc == 0, False,
                       waits=([("ve", v_t), ("gp", w_w), ("ve", last_v_evac[p2])]
                              + (g.r_psc.for_write() if f < 2 else [])) if (nk == 0 and kc == 0) else ())
            m = mm(g, psl[:, 512 * nk:512 * nk + 512], g.onesrowbf[:],
                   wb[0:1, 8192 + 512 * nk:8192 + 512 * nk + 512], False, True)
        last_m_ps[p2] = m
        last_m_wb[p3] = m
        last_m_srt[p4] = m
        v = ve(g, "tensor_copy", out=g.stage_f[:, 1024 * p2:1024 * p2 + 1024], in_=psl,
               waits=[("mm", m)] + (g.r_stagef.for_write() if f < 2
                                    else [("scq", last_st[p2])]))
        last_v_evac[p2] = v
        last_st[p2] = dma(g, "scq", g.d_srto[128 * f:128 * f + 128, :],
                          g.stage_f[:, 1024 * p2:1024 * p2 + 1024], waits=(("ve", v),))
    g.w_eo = P.val("scq")
    g.r_psc.wrote("mm", max(last_m_ps))
    g.r_stagef.wrote("ve", max(last_v_evac))
    g.r_fmB.wrote("gp", g.P.val("gp"))
    g.r_fmB.read("mm", max(last_m_ps))
    g.r_fmB.read("ve", max(last_v_evac))
    g.r_ctxall.wrote("sy", g.P.val("sy"))
    g.r_ctxall.read("mm", max(last_m_ps))
    g.r_vs65.wrote("ve", g.P.val("ve"))
    g.r_vs65.read("mm", max(last_m_ps))



def emit_tail(g):
    P = g.P
    emit_kept(g)
    slots = [g.attb[:, 0:2048].bitcast(F32), g.attb[:, 2048:4096].bitcast(F32),
             g.stage_f[:, 0:1024], g.stage_f[:, 1024:2048]]
    slot_res = [Res(), Res(), Res(), Res()]
    ybufs = [g.vs65[:, 4096 + 2048 * p:4096 + 2048 * p + 2048].bitcast(F32) for p in range(2)]
    ybuf_res = [Res(), Res()]
    w_gs = [0] * NT
    w_x1s = [0] * NT
    for ti in range(NT):
        sp = ti % 4
        g.P.emit("gp", (lambda o, ix: lambda e: e.indirect_dma_start(
            out=o, out_offset=None, in_=g.d_srto,
            in_offset=IndirectOffsetOnAxis(ap=ix, axis=0)))(
            slots[sp], g.dest32[:, ti:ti + 1]),
            waits=((("scq", g.w_eo), ("ve", g.v_kept)) + tuple(g.r_stagef.for_write() + g.r_attb0.for_write()
                   + g.r_attb1.for_write() + g.r_attb2.for_write() + g.r_attb3.for_write())
                   if ti == 0 else tuple(slot_res[sp].for_write())),
            inc=("gp", 16))
        w_gs[ti] = g.P.val("gp")
        slot_res[sp].wrote("gp", w_gs[ti])
        w_x1s[ti] = dma(g, "sy", g.x_tok[:, 1024 * ti:1024 * ti + 1024],
                        g.d_x1f[128 * ti:128 * ti + 128, :],
                        waits=g.r_xtok.for_write() if ti == 0 else ())
        if ti < 3:
            continue
        emit_tail_tile(g, ti - 3, slots, slot_res, ybufs, ybuf_res, w_gs, w_x1s)
    for ti in range(NT - 3, NT):
        emit_tail_tile(g, ti, slots, slot_res, ybufs, ybuf_res, w_gs, w_x1s)
    w_out = g.P.val("scq")
    g.P.emit("ve", (lambda: lambda e: e.memset(g.stat[:, 15:16], 0.0))(),
             waits=(("scq", w_out),), inc=("ve", 1))


def emit_tail_tile(g, ti, slots, slot_res, ybufs, ybuf_res, w_gs, w_x1s):
    sp = ti % 4
    yp = ti % 2
    so = 16 * yp
    eo = slots[sp]
    yb = ybufs[yp]
    x2 = yb[:, 0:1024]
    x3 = yb[:, 0:1024]
    v1 = ve(g, "tensor_mul", out=g.stat[:, 32 + 2 * yp:33 + 2 * yp], in0=g.rp[:, ti:ti + 1],
            in1=g.kept[:, ti:ti + 1])
    v2 = ve(g, "tensor_scalar", out=g.stat[:, 33 + 2 * yp:34 + 2 * yp],
            in0=g.stat[:, 32 + 2 * yp:33 + 2 * yp], scalar1=-1.0, scalar2=None,
            op0=ALU.mult)
    v2 = ve(g, "tensor_add", out=g.stat[:, 33 + 2 * yp:34 + 2 * yp],
            in0=g.stat[:, 33 + 2 * yp:34 + 2 * yp], in1=g.rp[:, ti:ti + 1])
    v3 = ve(g, "tensor_scalar", out=x2, in0=eo, scalar1=g.stat[:, 32 + 2 * yp:33 + 2 * yp],
            scalar2=None, op0=ALU.mult,
            waits=[("gp", w_gs[ti]), ("ve", v1)] + ybuf_res[yp].for_write())
    slot_res[sp].read("ve", v3)
    g.P.emit("ve", (lambda o, i0, s1, i1: lambda e: e.scalar_tensor_tensor(
        out=o, in0=i0, scalar=s1, in1=i1, op0=ALU.mult, op1=ALU.add))(
        x2, g.fmA[:, 1024 * ti:1024 * ti + 1024], g.stat[:, 33 + 2 * yp:34 + 2 * yp],
        x2), waits=(("ve", v3), ("ve", v2)), inc=("ve", 1))
    if g.stage == "tl0" and ti == 0:
        wd = dma(g, "sy", g.dbg6[0:128, :], eo, waits=(("gp", w_gs[0]),))
        wd = dma(g, "sy", g.dbg6[128:256, :], g.x_tok[:, 0:1024], waits=(("sy", w_x1s[0]),))
        wd = dma(g, "sy", g.dbg6[256:384, :], x2, waits=(("ve", g.P.val("ve")),))
    v4 = ve(g, "tensor_add", out=x2, in0=x2,
            in1=g.x_tok[:, 1024 * ti:1024 * ti + 1024], waits=(("sy", w_x1s[ti]),))
    if g.stage == "tl0" and ti == 0:
        wd = dma(g, "sy", g.dbg6[384:512, :], x2, waits=(("ve", v4),))
    emit_ln_once(g, x2, eo, None, affine=True, waits0=(("ve", v4),), so=so)
    if g.stage == "tl0" and ti == 0:
        wd = dma(g, "sy", g.dbg6[512:640, :], eo, waits=(("ve", g.P.val("ve")),))
    v5 = ve(g, "tensor_add", out=x3, in0=eo,
            in1=g.x_tok[:, 1024 * ti:1024 * ti + 1024])
    if g.stage == "tl0" and ti == 0:
        wd = dma(g, "sy", g.dbg6[640:768, :], x3, waits=(("ve", v5),))
        wd = dma(g, "sy", g.dbg6[768:769, 0:40], g.stat[0:1, 0:40], waits=(("ve", v5),))
        wd = dma(g, "sy", g.dbg6[769:897, 0:16], g.kept[:, 0:16], waits=(("ve", v5),))
        wd = dma(g, "sy", g.dbg6[897:1025, 0:16], g.rp[:, 0:16], waits=(("ve", v5),))
    slot_res[sp].read("ve", v5)
    last = emit_ln_once(g, x3, g.x_tok[:, 1024 * ti:1024 * ti + 1024], None, affine=False,
                        waits0=(("ve", v5),), so=so)
    vz = g.P.val("ve")
    ybuf_res[yp].wrote("ve", vz)
    ybuf_res[yp].read("ve", vz)
    w = dma(g, "scq", g.out[128 * ti:128 * ti + 128, :], g.x_tok[:, 1024 * ti:1024 * ti + 1024],
            waits=[last])
    g.r_xtok.read("scq", w)



def shard_inputs(inputs, core, consts):
    z = inputs
    m = {
        "x": np.ascontiguousarray(z["L0_attn_x"][core]),
        "wdq": z["L0_attn_W_dq"], "wuq": z["L0_attn_W_uq"], "wdkv": z["L0_attn_W_dkv"],
        "wuk": z["L0_attn_W_uk"], "wuv": z["L0_attn_W_uv"], "wo": z["L0_attn_W_o"],
        "wq": np.ascontiguousarray(z["L0_ffn_Wq"].reshape(D, D)),
        "wk": np.ascontiguousarray(z["L0_ffn_Wk"].reshape(D, D)),
        "wv": np.ascontiguousarray(z["L0_ffn_Wv"].reshape(D, D)),
        "bq": np.ascontiguousarray(z["L0_ffn_bq"].reshape(1, D)),
        "bk": np.ascontiguousarray(z["L0_ffn_bk"].reshape(1, D)),
        "bv": np.ascontiguousarray(z["L0_ffn_bv"].reshape(1, D)),
        "we": np.ascontiguousarray(z["L0_ffn_We"].reshape(E * D, D)),
        "be": np.ascontiguousarray(z["L0_ffn_be"]),
        "wsw": z["L0_ffn_Wsw"],
        "bsw": np.ascontiguousarray(z["L0_ffn_bsw"].reshape(1, E)),
        "gamma": np.ascontiguousarray(z["L0_ffn_gamma"].reshape(1, D)),
        "beta": np.ascontiguousarray(z["L0_ffn_beta"].reshape(1, D)),
        "rankmask": (np.arange(8) < core).astype(np.float32).reshape(8, 1),
    }
    m.update(consts)
    return m


_NC_CACHE = {}


def run(inputs, stage="full"):
    if stage not in _NC_CACHE:
        _NC_CACHE[stage] = build(stage)
    nc = _NC_CACHE[stage]
    consts = host_consts()
    in_maps = [shard_inputs(inputs, c, consts) for c in range(8)]
    return run_bass_kernel_spmd(nc, in_maps, core_ids=list(range(8)))


def kernel(**inputs):
    res = run(inputs, "full")
    return np.stack([res.results[c]["out"] for c in range(8)], axis=0)


# version 20
# speedup vs baseline: 1.0151x; 1.0151x over previous
import numpy as np
import ml_dtypes
import concourse.bass as bass
import concourse.mybir as mybir
from concourse.bass import IndirectOffsetOnAxis
from concourse.bass_utils import run_bass_kernel_spmd

F32 = mybir.dt.float32
BF16 = mybir.dt.bfloat16
U32 = mybir.dt.uint32
AF = mybir.ActivationFunctionType
ALU = mybir.AluOpType
AX = mybir.AxisListType
BF = ml_dtypes.bfloat16

T, D, NH, HS, L, QL, E = 2048, 1024, 16, 64, 128, 128, 8
CAP = 2457
NT = 16
NCK = 4
NFILL = 23
EPS = 1e-5



class Prog:
    def __init__(self):
        self.ops = []
        self.cnt = {}

    SELF_SEM = {"ve": "ve", "sc": "sc"}

    def emit(self, eng, fn, waits=(), inc=None):
        if inc is not None and inc[0] in ("ve", "sc", "gpc"):
            waits = tuple(waits) + ((inc[0], self.cnt.get(inc[0], 0)),)
        import sys
        fr = sys._getframe(1)
        tag = []
        for _ in range(4):
            if fr is None:
                break
            tag.append(fr.f_lineno)
            fr = fr.f_back
        self._tag = tuple(tag)
        ws = {}
        for s, v in waits:
            if v and v > ws.get(s, 0):
                ws[s] = v
        if inc is not None:
            sem, n = inc
            self.cnt[sem] = self.cnt.get(sem, 0) + n
        self.ops.append((eng, fn, tuple(ws.items()), inc, self._tag))

    def val(self, sem):
        return self.cnt.get(sem, 0)

    def replay(self, eng_name, engine, sems):
        for eng, fn, waits, inc, tag in self.ops:
            if eng != eng_name:
                continue
            for sem, v in waits:
                engine.wait_ge(sems[sem], v)
            try:
                inst = fn(engine)
            except Exception as ex:
                raise RuntimeError(f"op failed, emitted at lines {tag}: {ex}") from ex
            if inc is not None:
                sem, n = inc
                inst.then_inc(sems[sem], n)


class Res:

    def __init__(self):
        self.wsem = None
        self.readers = []

    def for_write(self):
        ws = list(self.readers)
        if self.wsem:
            ws.append(self.wsem)
        return ws

    def wrote(self, sem, val):
        self.wsem = (sem, val)
        self.readers = []

    def for_read(self):
        return [self.wsem] if self.wsem else []

    def read(self, sem, val):
        self.readers.append((sem, val))


class Ctx:
    pass



def host_consts():
    c = {}
    c["identbf"] = np.eye(128, dtype=BF)
    c["identf"] = np.eye(128, dtype=np.float32)
    c["triu"] = np.triu(np.ones((128, 128), np.float32))
    c["sl8"] = np.triu(np.ones((8, 8), np.float32), 1)
    c["sl16"] = np.triu(np.ones((16, 16), np.float32), 1)
    c["ones128"] = np.ones((1, 128), np.float32)
    c["onesrowbf"] = np.ones((1, 128), BF)
    c["onescolbf"] = np.ones((128, 1), BF)
    c["ones16"] = np.ones((1, 16), np.float32)
    c["iota8"] = np.tile(np.arange(8, dtype=np.float32), (128, 1))
    c["iota8m"] = (c["iota8"] - 999.0).astype(np.float32)
    c["iotap"] = np.arange(128, dtype=np.float32).reshape(128, 1)
    c["iota23"] = (128.0 * np.arange(NFILL, dtype=np.float32)).reshape(1, NFILL)
    c["thr16"] = (128.0 * np.arange(16, dtype=np.float32)).reshape(16, 1) * np.ones((1, 8), np.float32)
    p = np.arange(128).reshape(128, 1)
    f = np.arange(512).reshape(1, 512)
    m = np.zeros((128, 2048), np.float32)
    for j in range(4):
        m[:, 512 * j:512 * j + 512] = (f >= (128 * j + p)).astype(np.float32)
    c["maskc"] = m.astype(BF)
    c["ones2048bf"] = np.ones((128, 2048), BF)
    sel = np.zeros((16, 4, 512), np.float32)
    for tc in range(4):
        sel[4 * (tc + 1):, tc, :] = 1.0
    c["selbf"] = sel.reshape(16, 2048).astype(BF)
    oh16 = np.zeros((128, 256), np.float32)
    ohr = np.zeros((128, 256), np.float32)
    for h in range(16):
        oh16[:, 16 * h + h] = 1.0
        ohr[64, 16 * h + h] = 1.0
    c["oh16bf"] = oh16.astype(BF)
    c["ohrow16"] = ohr
    return c



def build(stage="full"):
    nc = bass.Bass("TRN2", target_bir_lowering=False)
    g = Ctx()
    g.nc = nc
    g.stage = stage
    g.P = Prog()

    di = lambda n, sh, dt=F32: nc.dram_tensor(n, sh, dt, kind="ExternalInput").ap()
    do = lambda n, sh, dt=F32: nc.dram_tensor(n, sh, dt, kind="ExternalOutput").ap()
    dint = lambda n, sh, dt=F32: nc.dram_tensor(n, sh, dt, kind="Internal").ap()

    g.x = di("x", [T, D])
    g.wdq = di("wdq", [QL, D]); g.wuq = di("wuq", [D, QL])
    g.wdkv = di("wdkv", [L, D]); g.wuk = di("wuk", [D, L]); g.wuv = di("wuv", [D, L])
    g.wo = di("wo", [D, D])
    g.wq = di("wq", [D, D]); g.wk = di("wk", [D, D]); g.wv = di("wv", [D, D])
    g.bq = di("bq", [1, D]); g.bk = di("bk", [1, D]); g.bv = di("bv", [1, D])
    g.we = di("we", [E * D, D]); g.be = di("be", [E, D])
    g.wsw = di("wsw", [E, D]); g.bsw = di("bsw", [1, E])
    g.gamma = di("gamma", [1, D]); g.beta = di("beta", [1, D])
    g.rankmask = di("rankmask", [8, 1])
    g.hc = host_consts()
    for k, v in g.hc.items():
        setattr(g, "c_" + k, di(k, list(v.shape), BF16 if v.dtype == BF else F32))
    g.out = do("out", [T, D])
    if stage == "x1":
        g.dbg = do("dbg_x1", [T, D], F32)
    elif stage == "xy":
        g.dbg = do("dbg_xy", [T, D], F32)
    elif stage == "ckv":
        g.dbg = do("dbg_ckv", [128, 2048], BF16)
        g.dbg5 = do("dbg_xT", [128, 16384], BF16)
        g.dbg2 = do("dbg_qt", [128, 16384], BF16)
        g.dbg3 = do("dbg_keff", [128, 1024], BF16)
        g.dbg4 = do("dbg_veff", [128, 1024], BF16)
    elif stage == "hid":
        g.dbg = do("dbg_hid", [T, D], BF16)
    elif stage == "route":
        g.dbg = do("dbg_route", [128, 16 * 8], F32)
    elif stage == "full2":
        g.dbg = do("dbg_route", [128, 16 * 8], F32)
        g.dbg6 = do("dbg_eo", [T, D], F32)
        g.dbg7 = do("dbg_tail", [128, 5 * D], F32)
    elif stage == "tl0":
        g.dbg6 = do("dbg_eo", [T, D], F32)

    g.d_xbf = dint("d_xbf", [T, D], BF16)
    g.d_x1bf = dint("d_x1bf", [T, D], BF16)
    g.d_hidbf = dint("d_hidbf", [T, D], BF16)
    g.d_x1f = dint("d_x1f", [T, D], F32)
    g.d_webf = dint("d_webf", [E * D, D], BF16)
    g.d_wt = dint("d_wt", [E * 128, 9 * 1024], BF16)
    g.d_wqkvbf = dint("d_wqkvbf", [3 * D, D], BF16)
    g.d_wobf = dint("d_wobf", [D, D], BF16)
    g.d_smallbf = dint("d_smallbf", [3 * 128, D], BF16)
    g.d_srt = dint("d_srt", [NFILL * 128, D], BF16)
    g.d_srto = dint("d_srto", [NFILL * 128, D], F32)
    g.d_ccin = dint("d_ccin", [1, 8])
    g.d_ccout = dint("d_ccout", [8, 8])

    from contextlib import ExitStack
    with ExitStack() as ctx:
        S = lambda n, sh, dt=F32: ctx.enter_context(nc.sbuf_tensor(n, sh, dt))
        Ps = lambda n, sh: ctx.enter_context(nc.psum_tensor(n, sh, F32))

        g.fmA = S("fmA", [128, 16384], BF16)
        g.fmB = S("fmB", [128, 32768], BF16)
        g.x_tok = g.fmB[:].bitcast(F32)
        g.vs65 = S("vs65", [128, NH * 16 * 65], BF16)
        g.ctxall = S("ctxall", [128, 16 * 512], BF16)
        g.attb = S("attb", [128, 2 * 2048], BF16)
        g.wqkvT = g.ctxall
        g.qlb = S("qlb", [128, 1024], BF16)
        g.mlakv = S("mlakv", [128, 4096], BF16)
        g.ckvT = g.mlakv[:, 0:2048]
        g.ckvS = g.mlakv[:, 2048:4096]
        g.q2T = g.attb[:, 0:2048]
        g.gbc = g.mlakv[:].bitcast(F32)
        g.wsmall = S("wsmall", [128, 3 * 1024], BF16)
        g.prepw = S("prepw", [128, 2048], BF16)
        g.wuk_s = g.prepw[:, 0:1024]
        g.wuv_s = g.prepw[:, 1024:2048]
        g.keff = S("keff", [128, 1024], BF16)
        g.veff = S("veff", [128, 1024], BF16)
        g.wtmps = S("wtmps", [128, 2048], BF16)
        g.wtmp = g.wtmps[:, 0:1024]
        g.wtmp2 = g.wtmps[:, 1024:2048]
        g.wswT = S("wswT", [128, 64], BF16)
        g.bqkv = S("bqkv", [128, 24])
        g.u65 = S("u65", [128, 512], BF16)
        g.den_s = S("den_s", [128, 512])
        g.r16 = S("r16", [128, 512])
        g.rtok = S("rtok", [128, 64])
        g.stat = S("stat", [128, 48])
        g.lg = S("lg", [128, 128])
        g.route = S("route", [128, 16])
        g.rp = S("rp", [128, 16])
        g.onehot = S("onehot", [128, 128])
        g.kept = S("kept", [128, 16])
        g.posl = S("posl", [128, 16])
        g.dest32 = S("dest32", [128, NT], U32)
        g.widx32 = S("widx32", [128, NFILL], U32)
        g.cntrow = S("cntrow", [1, 8 * 17])
        g.adjrow = S("adjrow", [1, 128])
        g.segrow = S("segrow", [1, 8])
        g.corow = S("corow", [1, 8])
        g.gab = S("gab", [8, 8])
        g.small = S("small", [128, 64])
        g.svT = S("svT", [128, NH * 65], BF16)
        g.sv_s = S("sv_s", [128, 16])
        g.bcast8 = S("bcast8", [128, 32])
        g.stage_f = S("stage_f", [128, 2048])
        g.stage_b = S("stage_b", [128, 2048], BF16)
        g.stage_w = None
        g.bswr = S("bswr", [1, 8])
        g.rmask = S("rmask", [8, 1])
        g.onescol_f = S("onescol_f", [128, 1])
        _deferred = ("maskc", "ones2048bf", "selbf")
        for n in g.hc:
            if n in _deferred:
                continue
            setattr(g, n, S("s_" + n, list(g.hc[n].shape), BF16 if g.hc[n].dtype == BF else F32))
        g.maskc = g.wsmall[:, 0:2048]
        g.ones2048bf = g.prepw[:, 0:2048]
        g.selbf = g.wtmps[0:16, 0:2048]

        g.p_sc = Ps("p_sc", [128, 2048])
        g.p_ctx = Ps("p_ctx", [128, 512])
        g.p_den = Ps("p_den", [128, 512])
        g.p_m1 = Ps("p_m1", [128, 512])
        g.p_m2 = Ps("p_m2", [128, 512])

        for r in ["r_psc", "r_pscB", "r_pctx", "r_pden", "r_pm1", "r_pm2", "r_attb0", "r_attb1",
                  "r_attb2", "r_attb3",
                  "r_qlb", "r_qlb2", "r_u65", "r_dens", "r_r16", "r_rtok", "r_ctxall", "r_stagef",
                  "r_stageb", "r_fmA", "r_fmB", "r_vs65", "r_wtmp",
                  "r_wtmp2", "r_small", "r_svs", "r_sv", "r_mlakv"]:
            setattr(g, r, Res())
        g.r_xtok = g.r_fmB
        g.r_wqkvT = g.r_ctxall
        g.abres = [g.r_attb0, g.r_attb1, g.r_attb2, g.r_attb3]
        g.r_pscs = [g.r_psc, g.r_pscB]
        g.stage_w = g.stage_b[:, 1024:2048]
        g.r_stagew = g.r_stageb

        sems = {}
        for n in ["sy", "gp", "scq", "mm", "ve", "sc", "gpc"]:
            sems[n] = ctx.enter_context(nc.semaphore(n))
        g.sems = sems

        emit_program(g)

        block = ctx.enter_context(nc.Block())

        @block.sync
        def _(e):
            g.P.replay("sy", e, sems)

        @block.scalar
        def _(e):
            g.P.replay("sc", e, sems)

        @block.tensor
        def _(e):
            g.P.replay("te", e, sems)

        @block.vector
        def _(e):
            g.P.replay("ve", e, sems)

        @block.gpsimd
        def _(e):
            g.P.replay("gp", e, sems)

    return nc



def dma(g, q, out, in_, waits=(), transpose=False):
    eng = {"sy": "sy", "gp": "gp", "scq": "sc"}[q]
    sem = {"sy": "sy", "gp": "gp", "scq": "scq"}[q]
    g.P.emit(eng, (lambda o, i, t: lambda e: e.dma_start(out=o, in_=i, transpose=t))(out, in_, transpose),
             waits=waits, inc=(sem, 16))
    return g.P.val(sem)


def mm(g, out, lhsT, rhs, start, stop, waits=()):
    g.P.emit("te", (lambda o, l, r, s0, s1: lambda e: e.matmul(o, l, r, start=s0, stop=s1))(
        out, lhsT, rhs, start, stop), waits=waits, inc=("mm", 1))
    return g.P.val("mm")


def act(g, out, in_, func=AF.Copy, bias=0.0, scale=1.0, accum_out=None, waits=()):
    g.P.emit("sc", (lambda o, i, f, b, s, a: lambda e: e.activation(o, i, f, bias=b, scale=s, accum_out=a))(
        out, in_, func, bias, scale, accum_out), waits=waits, inc=("sc", 1))
    return g.P.val("sc")


def ve(g, name, *args, waits=(), **kw):
    g.P.emit("ve", (lambda n, a, k: lambda e: getattr(e, n)(*a, **k))(name, args, kw),
             waits=waits, inc=("ve", 1))
    return g.P.val("ve")


def gp_op(g, name, *args, waits=(), **kw):
    g.P.emit("gp", (lambda n, a, k: lambda e: getattr(e, n)(*a, **k))(name, args, kw),
             waits=waits, inc=("gpc", 1))
    return g.P.val("gpc")


def tr_mm(g, out_psum, in_sbuf, ident, waits=()):
    k = in_sbuf.shape[0]
    b = in_sbuf.base_partition()
    return mm(g, out_psum, in_sbuf, ident[b:b + k, b:b + k], True, True, waits=waits)



def emit_program(g):
    emit_consts(g)
    emit_x_path(g)
    emit_small_weights(g)
    emit_weprep(g)
    emit_mla(g)
    if g.stage == "ckv":
        return
    if g.stage == "xy":
        emit_dbg_tiles(g, g.dbg, lambda i: g.x_tok[:, 1024 * i:1024 * i + 1024], g.r_xtok)
        return
    emit_postln1(g)
    if g.stage == "x1":
        emit_dbg_tiles(g, g.dbg, lambda i: g.x_tok[:, 1024 * i:1024 * i + 1024], g.r_xtok)
        return
    emit_ffn_attn(g)
    if g.stage == "hid":
        emit_dbg_tiles(g, g.dbg, lambda i: g.fmA[:, 1024 * i:1024 * i + 1024], g.r_fmA)
        return
    emit_router(g)
    if g.stage == "route":
        emit_kept(g)
        emit_dbg_route(g)
        return
    emit_experts(g)
    if g.stage == "full2":
        for i in range(NT):
            dma(g, "sy", g.dbg6[128 * i:128 * i + 128, :], g.d_srto[128 * i:128 * i + 128, :],
                waits=(("scq", g.w_eo),) if i == 0 else ())
    emit_tail(g)
    if g.stage == "full2":
        emit_dbg_route(g)


def emit_dbg_tiles(g, dst, src_fn, res):
    w = 0
    for i in range(NT):
        w = dma(g, "sy", dst[128 * i:128 * i + 128, :], src_fn(i),
                waits=res.for_read() if i == 0 else ())
    g.P.emit("ve", (lambda: lambda e: e.memset(g.stat[:, 15:16], 0.0))(),
             waits=(("sy", w),), inc=("ve", 1))


def emit_dbg_route(g):
    v = ve(g, "tensor_copy", out=g.lg[:, 0:16], in_=g.route[:, 0:16])
    v = ve(g, "tensor_copy", out=g.lg[:, 16:32], in_=g.rp[:, 0:16])
    v = ve(g, "tensor_copy", out=g.lg[:, 32:48], in_=g.kept[:, 0:16])
    v = ve(g, "tensor_copy", out=g.lg[:, 48:64], in_=g.posl[:, 0:16])
    v = ve(g, "tensor_copy", out=g.lg[:, 64:80], in_=g.dest32[:, 0:16])
    w = dma(g, "sy", g.dbg, g.lg[:, 0:128], waits=(("ve", v),))
    g.P.emit("ve", (lambda: lambda e: e.memset(g.stat[:, 15:16], 0.0))(),
             waits=(("sy", w),), inc=("ve", 1))



def emit_consts(g):
    for n in g.hc:
        if n in ("maskc", "ones2048bf", "selbf"):
            continue
        dma(g, "sy", getattr(g, n)[:], getattr(g, "c_" + n))
    dma(g, "sy", g.rmask[:], g.rankmask)
    dma(g, "sy", g.bswr[:], g.bsw)
    g.w_consts = g.P.val("sy")
    ve(g, "memset", g.onescol_f[:], 1.0)
    g.v_ones = g.P.val("ve")



def emit_x_path(g):
    g.w_x = []
    for i in range(NT):
        g.w_x.append(dma(g, "scq", g.x_tok[:, 1024 * i:1024 * i + 1024],
                         g.x[128 * i:128 * i + 128, :]))
    g.r_xtok.wrote("scq", g.w_x[-1])
    wc = dma(g, "gp", g.d_xbf, g.x)
    for c in range(8):
        w = dma(g, "sy", g.fmA[:, 2048 * c:2048 * c + 2048], g.d_xbf[:, 128 * c:128 * c + 128],
                transpose=True, waits=(("gp", wc),) if c == 0 else ())
    g.w_xT = g.P.val("sy")
    g.r_fmA.wrote("sy", g.w_xT)



def emit_small_weights(g):
    P = g.P
    dma(g, "gp", g.d_smallbf[0:128, :], g.wdq)
    dma(g, "gp", g.d_smallbf[128:256, :], g.wdkv)
    wuq_flat = g.d_smallbf[256:384, :].rearrange("p f -> (p f)").rearrange("(a b) -> a b", a=1024)
    dma(g, "gp", wuq_flat, g.wuq)
    g.w_cast_small = g.P.val("gp")
    dma(g, "gp", g.wuk_s.rearrange("p (a b) -> p a b", a=8),
        g.wuk.rearrange("(a p) b -> p a b", p=128))
    dma(g, "gp", g.wuv_s.rearrange("p (a b) -> p a b", a=8),
        g.wuv.rearrange("(a p) b -> p a b", p=128))
    g.w_uvk = P.val("gp")
    dma(g, "gp", g.d_wobf, g.wo)
    g.w_cast_wo = P.val("gp")
    dma(g, "gp", g.d_wqkvbf[0:D, :], g.wq)
    dma(g, "gp", g.d_wqkvbf[D:2 * D, :], g.wk)
    dma(g, "gp", g.d_wqkvbf[2 * D:3 * D, :], g.wv)
    dma(g, "gp", g.d_webf, g.we)
    g.w_cast = P.val("gp")
    for c in range(8):
        dma(g, "sy", g.wsmall[:, 128 * c:128 * c + 128],
            g.d_smallbf[0:128, 128 * c:128 * c + 128], transpose=True,
            waits=(("gp", g.w_cast_small),) if c == 0 else ())
    for c in range(8):
        dma(g, "sy", g.wsmall[:, 1024 + 128 * c:1024 + 128 * c + 128],
            g.d_smallbf[128:256, 128 * c:128 * c + 128], transpose=True)
    for c in range(8):
        dma(g, "sy", g.wsmall[:, 2048 + 128 * c:2048 + 128 * c + 128],
            wuq_flat[128 * c:128 * c + 128, 0:128], transpose=True)
    g.w_small = P.val("sy")
    w1 = dma(g, "sy", g.wtmp.rearrange("p (a b) -> p a b", a=8),
             wuq_flat.rearrange("(a p) b -> p a b", p=128))
    w2 = dma(g, "sy", g.wtmp2[:, 0:1024], g.d_smallbf[0:128, :], waits=(("gp", g.w_cast_small),))
    g.r_wtmp.wrote("sy", w1)
    g.r_wtmp2.wrote("sy", w2)

    for c in range(8):
        m = mm(g, g.p_m1[:, 0:128], g.wtmp[:, 128 * c:128 * c + 128],
               g.wuk_s[:, 128 * c:128 * c + 128], c == 0, c == 7,
               waits=g.r_wtmp.for_read() + [("gp", g.w_uvk)] + g.r_pm1.for_write() if c == 0 else ())
    g.r_pm1.wrote("mm", m)
    g.r_wtmp.read("mm", m)
    v = ve(g, "tensor_copy", out=g.u65[:, 0:128], in_=g.p_m1[:, 0:128],
           waits=g.r_pm1.for_read() + g.r_u65.for_write())
    g.r_pm1.read("ve", v)
    g.r_u65.wrote("ve", v)
    for c in range(8):
        m = mm(g, g.p_m2[:, 0:128], g.wtmp2[:, 128 * c:128 * c + 128], g.u65[:, 0:128],
               True, True,
               waits=(g.r_wtmp2.for_read() + g.r_u65.for_read() + g.r_pm2.for_write()) if c == 0 else g.r_pm2.for_write())
        g.r_pm2.wrote("mm", m)
        v = ve(g, "tensor_copy", out=g.keff[:, 128 * c:128 * c + 128], in_=g.p_m2[:, 0:128],
               waits=g.r_pm2.for_read())
        g.r_pm2.read("ve", v)
    g.r_wtmp2.read("mm", m)
    g.r_u65.read("mm", m)
    g.v_keff = v

    for c in range(8):
        w = dma(g, "sy", g.wtmp[:, 0:1024], g.d_wobf[0:1024, 128 * c:128 * c + 128],
                transpose=True, waits=g.r_wtmp.for_write() + ([("gp", g.w_cast_wo)] if c == 0 else []))
        g.r_wtmp.wrote("sy", w)
        for nk in range(2):
            pdst = g.p_m1 if nk == 0 else g.p_m2
            rres = g.r_pm1 if nk == 0 else g.r_pm2
            m = mm(g, pdst[:, 0:512], g.wuv_s[:, 128 * c:128 * c + 128],
                   g.wtmp[:, 512 * nk:512 * nk + 512], c == 0, c == 7,
                   waits=([("sy", w)] + rres.for_write()) if c == 0 else [("sy", w)])
            if c == 0:
                rres.wrote("mm", m)
        g.r_wtmp.read("mm", m)
    g.r_pm1.wrote("mm", m)
    g.r_pm2.wrote("mm", m)
    for nk in range(2):
        pdst = g.p_m1 if nk == 0 else g.p_m2
        rres = g.r_pm1 if nk == 0 else g.r_pm2
        v = ve(g, "tensor_copy", out=g.veff[:, 512 * nk:512 * nk + 512], in_=pdst[:, 0:512],
               waits=rres.for_read())
        rres.read("ve", v)
    g.v_veff = v

    w = dma(g, "gp", g.stage_b[0:8, 0:1024], g.wsw, waits=g.r_stageb.for_write())
    g.r_stageb.wrote("gp", w)
    for c in range(8):
        m = tr_mm(g, g.p_m2[:, 0:8], g.stage_b[0:8, 128 * c:128 * c + 128], g.identbf,
                  waits=g.r_stageb.for_read() + g.r_pm2.for_write())
        g.r_pm2.wrote("mm", m)
        v = ve(g, "tensor_copy", out=g.wswT[:, 8 * c:8 * c + 8], in_=g.p_m2[:, 0:8],
               waits=g.r_pm2.for_read())
        g.r_pm2.read("ve", v)
    g.r_stageb.read("mm", m)
    g.v_wsw = v

    for pi, b in enumerate([g.bq, g.bk, g.bv]):
        for mt in range(8):
            dma(g, "sy", g.bqkv[:, 8 * pi + mt:8 * pi + mt + 1],
                b[0:1, 128 * mt:128 * mt + 128].rearrange("a p -> (a p)").unsqueeze(1))
    g.w_bqkv = P.val("sy")
    m = mm(g, g.p_m2[:, 0:8], g.ones128[:], g.bswr[:], True, True,
           waits=[("sy", g.w_consts)] + g.r_pm2.for_write())
    g.r_pm2.wrote("mm", m)
    v = ve(g, "tensor_copy", out=g.bcast8[:, 0:8], in_=g.p_m2[:, 0:8], waits=g.r_pm2.for_read())
    g.r_pm2.read("ve", v)
    g.v_bsw = v



def emit_weprep(g):
    P = g.P
    v0 = ve(g, "memset", g.stage_w[:, 0:1024], 0.0, waits=g.r_stagew.for_write())
    g.r_stagew.wrote("ve", v0)
    for e in range(E):
        w = dma(g, "sy", g.d_wt[128 * e:128 * e + 128, 8192:9216], g.stage_w[:, 0:1024],
                waits=g.r_stagew.for_read() if e == 0 else ())
        g.r_stagew.read("sy", w)
    for e in range(E):
        dma(g, "gp", g.d_wt[128 * e:128 * e + 1, 8192:9216], g.be[e:e + 1, :],
            waits=(("sy", P.val("sy")),) if e == 0 else ())
    for e in range(E):
        for j in range(8):
            w1 = dma(g, "sy", g.stage_w[:, 0:1024],
                     g.d_webf[1024 * e:1024 * e + 1024, 128 * j:128 * j + 128], transpose=True,
                     waits=g.r_stagew.for_write() + ([("gp", g.w_cast)] if (e == 0 and j == 0) else []))
            g.r_stagew.wrote("sy", w1)
            w2 = dma(g, "sy", g.d_wt[128 * e:128 * e + 128, 1024 * j:1024 * j + 1024],
                     g.stage_w[:, 0:1024])
            g.r_stagew.read("sy", w2)
    g.w_wt = P.val("sy")



def emit_mla(g):
    P = g.P
    xT = lambda c: g.fmA[:, 2048 * c:2048 * c + 2048]
    dump_ckv = (g.stage == "ckv")
    if dump_ckv:
        dma(g, "sy", g.dbg5, g.fmA[:, 0:16384], waits=(("sy", g.w_xT),))

    for nk in range(4):
        for c in range(8):
            m = mm(g, g.p_sc[:, 512 * nk:512 * nk + 512],
                   g.wsmall[:, 1024 + 128 * c:1024 + 128 * c + 128],
                   xT(c)[:, 512 * nk:512 * nk + 512], c == 0, c == 7,
                   waits=([("sy", g.w_xT), ("sy", g.w_small)] + g.r_psc.for_write()) if (nk == 0 and c == 0) else ())
        s = act(g, g.ckvT[:, 512 * nk:512 * nk + 512], g.p_sc[:, 512 * nk:512 * nk + 512],
                AF.Copy, waits=(("mm", m),))
        g.r_psc.read("sc", s)
    g.r_psc.wrote("mm", m)
    s_ckv = s
    for nk in range(4):
        for c in range(8):
            m = mm(g, g.p_ctx[:, 0:512], g.wsmall[:, 128 * c:128 * c + 128],
                   xT(c)[:, 512 * nk:512 * nk + 512], c == 0, c == 7,
                   waits=g.r_pctx.for_write() if c == 0 else ())
        g.r_pctx.wrote("mm", m)
        s = act(g, g.q2T[:, 512 * nk:512 * nk + 512], g.p_ctx[:, 0:512], AF.Copy,
                waits=g.r_pctx.for_read())
        g.r_pctx.read("sc", s)
    s_q2 = s
    g.r_fmA.read("mm", m)
    for si in range(16):
        m = tr_mm(g, g.p_m1[:, 0:128], g.ckvT[:, 128 * si:128 * si + 128], g.identbf,
                  waits=([("sc", s_ckv)] if si == 0 else []) + g.r_pm1.for_write())
        g.r_pm1.wrote("mm", m)
        s = act(g, g.ckvS[:, 128 * si:128 * si + 128], g.p_m1[:, 0:128], AF.Copy,
                waits=g.r_pm1.for_read())
        g.r_pm1.read("sc", s)
    s_ckvs = s
    for mt in range(8):
        for nk in range(4):
            m = mm(g, g.p_m2[:, 0:512], g.wsmall[:, 2048 + 128 * mt:2048 + 128 * mt + 128],
                   g.q2T[:, 512 * nk:512 * nk + 512], True, True,
                   waits=([("sc", s_q2)] if (mt == 0 and nk == 0) else []) + g.r_pm2.for_write()
                   + (g.r_fmA.for_write() if (mt == 0 and nk == 0) else []))
            g.r_pm2.wrote("mm", m)
            s = act(g, g.fmA[:, 2048 * mt + 512 * nk:2048 * mt + 512 * nk + 512],
                    g.p_m2[:, 0:512], AF.Copy, waits=g.r_pm2.for_read())
            g.r_pm2.read("sc", s)
    g.r_fmA.wrote("sc", s)
    s_qT = s
    if dump_ckv:
        w = dma(g, "sy", g.dbg, g.ckvT, waits=(("sc", s_ckv), ("ve", g.v_keff), ("ve", g.v_veff)))
        w = dma(g, "sy", g.dbg2, g.fmA[:, 0:16384], waits=(("sc", s_qT),))
        w = dma(g, "sy", g.dbg3, g.keff[:, 0:1024])
        w = dma(g, "sy", g.dbg4, g.veff[:, 0:1024])
        g.P.emit("ve", (lambda: lambda e: e.memset(g.stat[:, 15:16], 0.0))(),
                 waits=(("sy", w),), inc=("ve", 1))
        return
    g.w_maskc_q = dma(g, "scq", g.maskc, g.c_maskc, waits=(("mm", P.val("mm")),))

    gctr_box = [0]

    def ql_res(h):
        return g.r_qlb if h % 2 == 0 else g.r_qlb2

    def ql_ap(h):
        return g.qlb[:, 512 * (h % 2):512 * (h % 2) + 512]

    def emit_ql(tc, h, first=False):
        hb = 64 * (h % 2)
        lhsT_ql = g.keff[hb:hb + 64, 128 * (h // 2):128 * (h // 2) + 128]
        rhs_q = g.fmA[hb:hb + 64, 2048 * (h // 2) + 512 * tc:2048 * (h // 2) + 512 * tc + 512]
        m = mm(g, g.p_m2[:, 0:512], lhsT_ql, rhs_q, True, True,
               waits=([("ve", g.v_keff), ("sc", s_qT)] if first else [])
               + g.r_pm2.for_write() + (g.r_fmA.for_read() if first else []))
        g.r_pm2.wrote("mm", m)
        r = ql_res(h)
        v = ve(g, "tensor_copy", out=ql_ap(h), in_=g.p_m2[:, 0:512],
               waits=g.r_pm2.for_read() + r.for_write())
        g.r_pm2.read("ve", v)
        r.wrote("ve", v)

    def emit_scores(tc, h, gi):
        reg = gctr_box[0] % 2
        psc = g.p_sc[:, 1024 * reg:1024 * reg + 1024]
        rsc = g.r_pscs[reg]
        abres = g.abres[gctr_box[0] % 4]
        ab = g.attb[:, 1024 * (gctr_box[0] % 4):1024 * (gctr_box[0] % 4) + 1024]
        gctr_box[0] += 1
        r = ql_res(h)
        for j in range(2):
            si = 2 * gi + j
            m = mm(g, psc[:, 512 * j:512 * j + 512],
                   g.ckvT[:, 128 * si:128 * si + 128], ql_ap(h), True, True,
                   waits=(r.for_read() + rsc.for_write()) if j == 0 else ())
        rsc.wrote("mm", m)
        r.read("mm", m)
        s = act(g, ab, psc, AF.Exp, scale=0.125,
                waits=rsc.for_read() + abres.for_write())
        rsc.read("sc", s)
        abres.wrote("sc", s)
        if gi >= 2 * tc:
            v = ve(g, "tensor_mul", out=ab, in0=ab,
                   in1=g.maskc[:, 1024 * (gi - 2 * tc):1024 * (gi - 2 * tc) + 1024],
                   waits=abres.for_read() + [("scq", g.w_maskc_q)])
            abres.wrote("ve", v)
        return (tc, h, gi, ab, abres)

    def emit_ctxden(p):
        tc, h, gi, ab, abres = p
        ngi = 2 * (tc + 1)
        for j in range(2):
            si = 2 * gi + j
            wts = []
            wts2 = []
            if j == 0:
                wts += abres.for_read()
                if gi == 0:
                    wts += [("sc", s_ckvs)]
                    wts += g.r_pctx.for_write()
                if h == 0 and gi == 0:
                    wts2 += g.r_pden.for_write()
            m1 = mm(g, g.p_ctx[:, 0:512], g.ckvS[:, 128 * si:128 * si + 128],
                    ab[:, 512 * j:512 * j + 512], (gi == 0 and j == 0),
                    (gi == ngi - 1 and j == 1), waits=wts)
            m2 = mm(g, g.p_den[0:16, 0:512], g.oh16bf[:, 16 * h:16 * h + 16],
                    ab[:, 512 * j:512 * j + 512], (h == 0 and gi == 0 and j == 0),
                    (h == NH - 1 and gi == ngi - 1 and j == 1), waits=wts2)
        abres.read("mm", m2)
        if gi == ngi - 1:
            g.r_pctx.wrote("mm", m1)
            v = ve(g, "tensor_copy", out=g.ctxall[:, 512 * h:512 * h + 512],
                   in_=g.p_ctx[:, 0:512],
                   waits=g.r_pctx.for_read() + (g.r_ctxall.for_write() if h == 0 else []))
            g.r_pctx.read("ve", v)
        if h == NH - 1 and gi == ngi - 1:
            g.r_pden.wrote("mm", m2)

    def emit_epilogue(tc):
        v = ve(g, "reciprocal", out=g.r16[0:16, 0:512], in_=g.p_den[0:16, 0:512],
               waits=g.r_pden.for_read() + g.r_r16.for_write())
        g.r_pden.read("ve", v)
        g.r_r16.wrote("ve", v)
        for sub in range(4):
            m = tr_mm(g, g.p_m1[:, 0:16], g.r16[0:16, 128 * sub:128 * sub + 128], g.identf,
                      waits=g.r_r16.for_read() + g.r_pm1.for_write())
            g.r_pm1.wrote("mm", m)
            v = ve(g, "tensor_copy", out=g.rtok[:, 16 * sub:16 * sub + 16], in_=g.p_m1[:, 0:16],
                   waits=g.r_pm1.for_read() + (g.r_rtok.for_write() if sub == 0 else []))
            g.r_pm1.read("ve", v)
        g.r_r16.read("mm", m)
        g.r_rtok.wrote("ve", v)
        vv = v
        for pr in range(8):
            for hh in range(2):
                h = 2 * pr + hh
                m = mm(g, g.p_m2[64 * hh:64 * hh + 64, 0:512], g.veff[:, 64 * h:64 * h + 64],
                       g.ctxall[:, 512 * h:512 * h + 512], True, True,
                       waits=(g.r_ctxall.for_read() + [("ve", g.v_veff)] + g.r_pm2.for_write()) if hh == 0 else ())
            g.r_pm2.wrote("mm", m)
            v = ve(g, "tensor_copy", out=g.u65[:, 0:512], in_=g.p_m2[:, 0:512],
                   waits=g.r_pm2.for_read() + g.r_u65.for_write())
            g.r_pm2.read("ve", v)
            g.r_u65.wrote("ve", v)
            for sub in range(4):
                trp, rt = (g.p_m1, g.r_pm1) if sub % 2 == 0 else (g.p_den, g.r_pden)
                m = tr_mm(g, trp[:, 0:128], g.u65[:, 128 * sub:128 * sub + 128], g.identbf,
                          waits=g.r_u65.for_read() + rt.for_write())
                rt.wrote("mm", m)
                ti = 4 * tc + sub
                for hh in range(2):
                    h = 2 * pr + hh
                    xs = g.x_tok[:, 1024 * ti + 64 * h:1024 * ti + 64 * h + 64]
                    g.P.emit("ve", (lambda o, i0, s1, i1, pm: lambda e: e.scalar_tensor_tensor(
                        out=o, in0=i0, scalar=s1, in1=i1, op0=ALU.mult, op1=ALU.add))(
                        xs, trp[0:128, 64 * hh:64 * hh + 64],
                        g.rtok[:, 16 * sub + h:16 * sub + h + 1], xs, trp),
                        waits=tuple(rt.for_read() + (g.r_rtok.for_read() if hh == 0 else [])
                                    + ([("scq", g.w_x[ti])] if (tc == 0 and pr == 0) else [])),
                        inc=("ve", 1))
                vv = g.P.val("ve")
                rt.read("ve", vv)
                g.r_u65.read("mm", m)
        g.r_rtok.read("ve", vv)
        g.r_ctxall.read("mm", g.P.val("mm"))
        if g.stage not in ("xy", "ckv"):
            for sub in range(4):
                ti = 4 * tc + sub
                xs_t = g.x_tok[:, 1024 * ti:1024 * ti + 1024]
                last = emit_ln_once(g, xs_t, xs_t, None, affine=False,
                                    waits0=((("ve", vv),) if sub == 0 else ()))
                w = dma(g, "sy", g.d_x1f[128 * ti:128 * ti + 128, :], xs_t, waits=[last])
                g.r_xtok.read("sy", w)
                wb = dma(g, "gp", g.d_x1bf[128 * ti:128 * ti + 128, :], xs_t, waits=[last])
                g.r_xtok.read("gp", wb)
            g.w_x1spill = w
            g.w_x1bf = wb
        return g.P.val("ve")

    all_items = [(tc, h, gi) for tc in range(NCK) for h in range(NH)
                 for gi in range(2 * (tc + 1))]
    emit_ql(0, 0, first=True)
    pq = []
    due_epi = None

    def flush_one():
        nonlocal due_epi
        if due_epi is not None:
            emit_epilogue(due_epi)
            due_epi = None
        p = pq.pop(0)
        emit_ctxden(p)
        ptc, ph, pgi = p[0], p[1], p[2]
        if ph == NH - 1 and pgi == 2 * (ptc + 1) - 1:
            due_epi = ptc

    for tc, h, gi in all_items:
        pq.append(emit_scores(tc, h, gi))
        if gi == 0:
            if h + 1 < NH:
                emit_ql(tc, h + 1)
            elif tc + 1 < NCK:
                emit_ql(tc + 1, 0)
        if len(pq) > 2:
            flush_one()
    while pq:
        flush_one()
    vv_last = emit_epilogue(NCK - 1)
    g.r_fmA.read("mm", g.P.val("mm"))
    g.r_xtok.wrote("ve", vv_last)
    g.v_mla = vv_last



def emit_ln_once(g, src, dst, sq_scratch, affine, waits0=(), so=0):
    st = g.stat[:, so:so + 16]
    ve(g, "bn_stats", out=st[:, 0:6], in_=src[:, 0:512], waits=waits0)
    ve(g, "bn_stats", out=st[:, 6:12], in_=src[:, 512:1024], waits=waits0)
    ve(g, "bn_aggr", out=st[:, 12:14], in_=st[:, 0:12])
    v5 = ve(g, "tensor_scalar", out=st[:, 14:15], in0=st[:, 13:14], scalar1=EPS,
            scalar2=None, op0=ALU.add)
    s2 = act(g, st[:, 15:16], st[:, 14:15], AF.Sqrt, waits=(("ve", v5),))
    v6 = ve(g, "reciprocal", out=st[:, 0:1], in_=st[:, 15:16], waits=(("sc", s2),))
    v7 = ve(g, "tensor_mul", out=st[:, 1:2], in0=st[:, 12:13], in1=st[:, 0:1])
    v8 = ve(g, "tensor_scalar", out=st[:, 2:3], in0=st[:, 1:2], scalar1=-1.0,
            scalar2=None, op0=ALU.mult)
    s3 = ve(g, "tensor_scalar", out=dst, in0=src, scalar1=st[:, 0:1],
            scalar2=st[:, 2:3], op0=ALU.mult, op1=ALU.add)
    if affine:
        v9 = ve(g, "tensor_mul", out=dst, in0=dst, in1=g.gbc[:, 0:1024],
                waits=(("ve", g.v_gbc),))
        v9 = ve(g, "tensor_add", out=dst, in0=dst, in1=g.gbc[:, 1024:2048])
        return ("ve", v9)
    return ("ve", s3)


def emit_postln1(g):
    P = g.P
    w2 = 0
    for c in range(8):
        w2 = dma(g, "sy", g.fmA[:, 2048 * c:2048 * c + 2048], g.d_x1bf[:, 128 * c:128 * c + 128],
                 transpose=True, waits=([("gp", g.w_x1bf)] + g.r_fmA.for_write() + g.r_fmA.for_read()) if c == 0 else ())
    g.r_fmA.wrote("sy", w2)
    g.w_x1T = w2



def emit_ffn_attn(g):
    P = g.P
    g.w_ffnc_q = dma(g, "scq", g.ones2048bf, g.c_ones2048bf, waits=(("mm", P.val("mm")),))
    g.w_ffnc_q = dma(g, "scq", g.selbf, g.c_selbf)
    x1T = lambda c: g.fmA[:, 2048 * c:2048 * c + 2048]
    v_ones = ve(g, "memset",
                g.vs65[:, :].rearrange("p (a b) -> p a b", b=65)[:, :, 64:65], 1.0,
                waits=g.r_vs65.for_write())
    g.r_vs65.wrote("ve", v_ones)

    for proj in range(3):
        wsrc = g.d_wqkvbf[1024 * proj:1024 * proj + 1024, :]
        for kc in range(8):
            w = dma(g, "sy", g.wqkvT[:, 1024 * kc:1024 * kc + 1024],
                    wsrc[0:1024, 128 * kc:128 * kc + 128], transpose=True,
                    waits=(g.r_wqkvT.for_write() + [("gp", g.w_cast)]) if kc == 0 else ())
        g.r_wqkvT.wrote("sy", w)
        for mt in range(8):
            for nk in range(4):
                pp, rr = (g.p_m2, g.r_pm2) if ((mt * 4 + nk) % 2 == 0) else (g.p_den, g.r_pden)
                m = mm(g, pp[:, 0:512], g.wqkvT[:, 128 * mt:128 * mt + 128],
                       x1T(0)[:, 512 * nk:512 * nk + 512], True, False,
                       waits=(g.r_wqkvT.for_read() + [("sy", g.w_x1T), ("sy", g.w_bqkv)]
                              + rr.for_write()) if (mt == 0 and nk == 0) else rr.for_write())
                for kc in range(1, 8):
                    m = mm(g, pp[:, 0:512], g.wqkvT[:, 1024 * kc + 128 * mt:1024 * kc + 128 * mt + 128],
                           x1T(kc)[:, 512 * nk:512 * nk + 512], False, kc == 7)
                rr.wrote("mm", m)
                if proj < 2:
                    dst = g.fmB[:, 16384 * proj + 2048 * mt + 512 * nk:
                                16384 * proj + 2048 * mt + 512 * nk + 512]
                    s = ve(g, "tensor_scalar", out=dst, in0=pp[:, 0:512],
                           scalar1=g.bqkv[:, 8 * proj + mt:8 * proj + mt + 1],
                           scalar2=None, op0=ALU.add,
                           waits=rr.for_read() + (g.r_fmB.for_write() if (proj == 0 and mt == 0 and nk == 0) else []))
                    rr.read("ve", s)
                else:
                    abslot = (mt * 4 + nk) % 4
                    ares = g.abres[abslot]
                    abv = g.attb[:, 1024 * abslot:1024 * abslot + 512]
                    s = ve(g, "tensor_scalar", out=abv, in0=pp[:, 0:512],
                           scalar1=g.bqkv[:, 16 + mt:16 + mt + 1],
                           scalar2=None, op0=ALU.add,
                           waits=rr.for_read() + ares.for_write())
                    rr.read("ve", s)
                    ares.wrote("ve", s)
                    for sub in range(4):
                        si = 4 * nk + sub
                        for hh in range(2):
                            h = 2 * mt + hh
                            pt, rt = (g.p_m1, g.r_pm1) if ((sub * 2 + hh) % 2 == 0) else (g.p_ctx, g.r_pctx)
                            m2 = tr_mm(g, pt[:, 0:64],
                                       abv[64 * hh:64 * hh + 64, 128 * sub:128 * sub + 128],
                                       g.identbf,
                                       waits=(ares.for_read() if (sub == 0 and hh == 0) else []) + rt.for_write())
                            rt.wrote("mm", m2)
                            ares.read("mm", m2)
                            v = ve(g, "tensor_copy",
                                   out=g.vs65[:, 65 * (16 * h + si):65 * (16 * h + si) + 64],
                                   in_=pt[:, 0:64], waits=rt.for_read())
                            rt.read("ve", v)
        g.r_wqkvT.read("mm", g.P.val("mm"))
    g.r_fmB.wrote("sc", g.P.val("sc"))
    g.r_vs65.wrote("ve", g.P.val("ve"))
    s_qk = g.P.val("sc")

    for h in range(NH):
        for si in range(16):
            m = mm(g, g.p_m2[0:65, si:si + 1], g.vs65[:, 65 * (16 * h + si):65 * (16 * h + si) + 65],
                   g.onescolbf[:], True, True,
                   waits=(g.r_vs65.for_read() + [("ve", g.v_ones), ("sy", g.w_consts)]
                          + g.r_pm2.for_write()) if si == 0 else ())
        g.r_pm2.wrote("mm", m)
        v = ve(g, "tensor_copy", out=g.sv_s[0:65, 0:16],
               in_=g.p_m2[0:65, 0:16], waits=g.r_pm2.for_read() + g.r_svs.for_write())
        g.r_pm2.read("ve", v)
        g.r_svs.wrote("ve", v)
        s = act(g, g.stage_b[0:65, 0:16], g.sv_s[0:65, 0:16], AF.Copy,
                waits=g.r_svs.for_read() + g.r_stageb.for_write())
        g.r_svs.read("sc", s)
        g.r_stageb.wrote("sc", s)
        m = tr_mm(g, g.p_m1[0:16, 0:65], g.stage_b[0:65, 0:16], g.identbf,
                  waits=g.r_stageb.for_read() + g.r_pm1.for_write())
        g.r_stageb.read("mm", m)
        g.r_pm1.wrote("mm", m)
        v = ve(g, "tensor_copy", out=g.svT[0:16, 65 * h:65 * h + 65], in_=g.p_m1[0:16, 0:65],
               waits=g.r_pm1.for_read() + (g.r_sv.for_write() if h == 0 else []))
        g.r_pm1.read("ve", v)
    g.r_sv.wrote("ve", v)
    v_sv = v

    qTs = lambda h, t0: g.fmB[64 * (h % 2):64 * (h % 2) + 64,
                              2048 * (h // 2) + t0:2048 * (h // 2) + t0 + 512]
    kTs = lambda h, si: g.fmB[64 * (h % 2):64 * (h % 2) + 64,
                              16384 + 2048 * (h // 2) + 128 * si:16384 + 2048 * (h // 2) + 128 * si + 128]
    gctr_box = [0]

    def emit_scores(tc, h, gi):
        t0 = 512 * tc
        reg = gctr_box[0] % 2
        psc = g.p_sc[:, 1024 * reg:1024 * reg + 1024]
        rsc = g.r_pscs[reg]
        abres = g.abres[gctr_box[0] % 4]
        ab = g.attb[:, 1024 * (gctr_box[0] % 4):1024 * (gctr_box[0] % 4) + 1024]
        gctr_box[0] += 1
        for j in range(2):
            si = 2 * gi + j
            m = mm(g, psc[:, 512 * j:512 * j + 512], kTs(h, si), qTs(h, t0),
                   True, True,
                   waits=(rsc.for_write() + (g.r_fmB.for_read() if (tc == 0 and h == 0 and gi == 0) else [])
                          + ([("ve", v_sv)] if (tc == 0 and h == 0 and gi == 0) else [])) if j == 0 else ())
        rsc.wrote("mm", m)
        s = act(g, ab, psc, AF.Exp, scale=1.0 / 32.0,
                waits=rsc.for_read() + abres.for_write())
        rsc.read("sc", s)
        abres.wrote("sc", s)
        if gi >= 2 * tc:
            mk = g.maskc[:, 1024 * (gi - 2 * tc):1024 * (gi - 2 * tc) + 1024]
            v = ve(g, "select", out=ab, mask=mk.bitcast(mybir.dt.uint16), on_true=ab,
                   on_false=g.ones2048bf[:, 0:1024],
                   waits=abres.for_read() + [("scq", g.w_ffnc_q)])
            abres.wrote("ve", v)
        return (tc, h, gi, ab, abres)

    def emit_ctx(p):
        tc, h, gi, ab, abres = p
        ngi = 2 * (tc + 1)
        pc, rc = (g.p_ctx, g.r_pctx) if h % 2 == 0 else (g.p_m2, g.r_pm2)
        for j in range(2):
            si = 2 * gi + j
            wts = list(abres.for_read()) if j == 0 else []
            if gi == 0 and j == 0:
                wts += rc.for_write()
            m1 = mm(g, pc[0:65, 0:512],
                    g.vs65[:, 65 * (16 * h + si):65 * (16 * h + si) + 65],
                    ab[:, 512 * j:512 * j + 512], (gi == 0 and j == 0), False, waits=wts)
        abres.read("mm", m1)
        if gi != ngi - 1:
            return
        m1 = mm(g, pc[0:65, 0:512], g.svT[0:16, 65 * h:65 * h + 65],
                g.selbf[:, 512 * tc:512 * tc + 512], False, True,
                waits=g.r_sv.for_read() + [("scq", g.w_ffnc_q)])
        rc.wrote("mm", m1)
        g.r_sv.read("mm", m1)
        stg, rs = (g.den_s, g.r_dens) if h % 2 == 0 else (g.r16, g.r_r16)
        v_u = ve(g, "tensor_copy", out=stg[0:65, 0:512], in_=pc[0:65, 0:512],
                 waits=rc.for_read() + rs.for_write())
        rs.wrote("ve", v_u)
        rc.read("ve", v_u)
        return (tc, h, stg, rs)

    def emit_ctx_tail(p):
        tc, h, stg, rs = p
        for sub in range(4):
            trp, rt = (g.p_m1, g.r_pm1) if sub % 2 == 0 else (g.p_den, g.r_pden)
            m2 = tr_mm(g, trp[:, 0:65], stg[0:65, 128 * sub:128 * sub + 128],
                       g.identf, waits=rs.for_read() + rt.for_write())
            rt.wrote("mm", m2)
            rs.read("mm", m2)
            v = ve(g, "tensor_copy",
                   out=g.ctxall[:, 512 * h + 64 * sub:512 * h + 64 * sub + 64],
                   in_=trp[:, 0:64],
                   waits=rt.for_read() + (g.r_ctxall.for_write() if (h == 0 and sub == 0) else []))
            v2 = ve(g, "tensor_copy", out=g.rtok[:, 16 * sub + h:16 * sub + h + 1],
                    in_=trp[:, 64:65],
                    waits=(g.r_rtok.for_write() if (h == 0 and sub == 0) else ()))
            rt.read("ve", v2)
        if h == NH - 1:
            g.r_rtok.wrote("ve", v2)

    def emit_fepi(tc):
        v = ve(g, "reciprocal", out=g.rtok[:, 0:64], in_=g.rtok[:, 0:64])
        g.r_rtok.wrote("ve", v)
        for h in range(NH):
            for sub in range(4):
                ti = 4 * tc + sub
                g.P.emit("ve", (lambda o, i0, s1: lambda e: e.tensor_scalar(
                    out=o, in0=i0, scalar1=s1, scalar2=None, op0=ALU.mult))(
                    g.fmA[:, 1024 * ti + 64 * h:1024 * ti + 64 * h + 64],
                    g.ctxall[:, 512 * h + 64 * sub:512 * h + 64 * sub + 64],
                    g.rtok[:, 16 * sub + h:16 * sub + h + 1]),
                    waits=tuple(g.r_rtok.for_read() + g.r_ctxall.for_read()
                                + (g.r_fmA.for_write() if (tc == 0 and h == 0 and sub == 0) else [])),
                    inc=("ve", 1))
        vv = g.P.val("ve")
        g.r_ctxall.read("ve", vv)
        g.r_rtok.read("ve", vv)
        for sub in range(4):
            ti = 4 * tc + sub
            wh = dma(g, "sy", g.d_hidbf[128 * ti:128 * ti + 128, :],
                     g.fmA[:, 1024 * ti:1024 * ti + 1024], waits=(("ve", vv),) if sub == 0 else ())
            g.r_fmA.read("sy", wh)
        g.w_hidst = wh
        return vv

    all_items = [(tc, h, gi) for tc in range(NCK) for h in range(NH)
                 for gi in range(2 * (tc + 1))]
    pq = []
    due_epi = None
    due_tail = None

    def flush_one():
        nonlocal due_tail, due_epi
        if due_tail is not None:
            emit_ctx_tail(due_tail)
            due_tail = None
        if due_epi is not None:
            emit_fepi(due_epi)
            due_epi = None
        p = pq.pop(0)
        t = emit_ctx(p)
        if t is not None:
            due_tail = t
        ptc, ph, pgi = p[0], p[1], p[2]
        if ph == NH - 1 and pgi == 2 * (ptc + 1) - 1:
            due_epi = ptc

    for tc, h, gi in all_items:
        pq.append(emit_scores(tc, h, gi))
        if len(pq) > 2:
            flush_one()
    while pq:
        flush_one()
    if due_tail is not None:
        emit_ctx_tail(due_tail)
    vv = emit_fepi(NCK - 1)
    g.r_fmB.read("mm", g.P.val("mm"))
    g.r_vs65.read("mm", g.P.val("mm"))
    g.r_fmA.wrote("ve", vv)
    g.v_hid = vv



def emit_gbc(g):
    w = dma(g, "sy", g.stage_f[0:1, 0:1024], g.gamma, waits=g.r_stagef.for_write())
    w = dma(g, "sy", g.stage_f[0:1, 1024:2048], g.beta)
    g.r_stagef.wrote("sy", w)
    m = 0
    for r in range(2):
        for nk in range(2):
            src = g.stage_f[0:1, 1024 * r + 512 * nk:1024 * r + 512 * nk + 512]
            m = mm(g, g.p_m1[:, 0:512], g.ones128[:], src,
                   True, True, waits=g.r_stagef.for_read() + g.r_pm1.for_write()
                   + (g.r_mlakv.for_write() if (r == 0 and nk == 0) else []))
            g.r_pm1.wrote("mm", m)
            v = ve(g, "tensor_copy", out=g.gbc[:, 1024 * r + 512 * nk:1024 * r + 512 * nk + 512],
                   in_=g.p_m1[:, 0:512], waits=g.r_pm1.for_read())
            g.r_pm1.read("ve", v)
    g.r_stagef.read("mm", m)
    g.r_mlakv.wrote("ve", v)
    g.v_gbc = v


def emit_router(g):
    P = g.P
    emit_gbc(g)
    for c in range(8):
        w2 = dma(g, "sy", g.fmB[:, 2048 * c:2048 * c + 2048], g.d_hidbf[:, 128 * c:128 * c + 128],
                 transpose=True, waits=([("sy", g.w_hidst)] + g.r_fmB.for_write()) if c == 0 else ())
    g.r_fmB.wrote("sy", w2)
    for tc in range(NCK):
        for c in range(8):
            m = mm(g, g.p_m2[0:8, 0:512], g.wswT[:, 8 * c:8 * c + 8],
                   g.fmB[:, 2048 * c + 512 * tc:2048 * c + 512 * tc + 512], c == 0, c == 7,
                   waits=(g.r_fmB.for_read() + [("ve", g.v_wsw)] + g.r_pm2.for_write()) if c == 0 else ())
        g.r_pm2.wrote("mm", m)
        if tc == NCK - 1:
            g.r_fmB.read("mm", m)
        v = ve(g, "tensor_copy", out=g.den_s[0:8, 0:512], in_=g.p_m2[0:8, 0:512],
               waits=g.r_pm2.for_read() + g.r_dens.for_write())
        g.r_pm2.read("ve", v)
        g.r_dens.wrote("ve", v)
        for sub in range(4):
            ti = 4 * tc + sub
            m = tr_mm(g, g.p_m1[:, 0:8], g.den_s[0:8, 128 * sub:128 * sub + 128], g.identf,
                      waits=g.r_dens.for_read() + g.r_pm1.for_write())
            g.r_pm1.wrote("mm", m)
            v = ve(g, "tensor_copy", out=g.lg[:, 8 * ti:8 * ti + 8], in_=g.p_m1[:, 0:8],
                   waits=g.r_pm1.for_read())
            g.r_pm1.read("ve", v)
        g.r_dens.read("mm", m)
    for ti in range(NT):
        lg = g.lg[:, 8 * ti:8 * ti + 8]
        ve(g, "tensor_add", out=lg, in0=lg, in1=g.bcast8[:, 0:8],
           waits=(("ve", g.v_bsw),) if ti == 0 else ())
        v2 = ve(g, "reduce_max", out=g.stat[:, 0:1], in_=lg, axis=AX.X)
        vneg = ve(g, "tensor_scalar", out=g.stat[:, 1:2], in0=g.stat[:, 0:1], scalar1=-1.0,
                  scalar2=None, op0=ALU.mult)
        s = act(g, g.stage_f[:, 0:8], lg, AF.Exp, bias=g.stat[:, 1:2],
                waits=[("ve", vneg)] + g.r_stagef.for_write())
        g.r_stagef.wrote("sc", s)
        v3 = ve(g, "reduce_sum", out=g.stat[:, 2:3], in_=g.stage_f[:, 0:8], axis=AX.X,
                waits=g.r_stagef.for_read())
        g.r_stagef.read("ve", v3)
        ve(g, "reciprocal", out=g.rp[:, ti:ti + 1], in_=g.stat[:, 2:3])
        v4 = ve(g, "tensor_scalar", out=g.stage_f[:, 8:16], in0=lg, scalar1=g.stat[:, 0:1],
                scalar2=None, op0=ALU.is_ge)
        v5 = ve(g, "tensor_mul", out=g.stage_f[:, 16:24], in0=g.stage_f[:, 8:16], in1=g.iota8m[:])
        v6 = ve(g, "tensor_scalar", out=g.stage_f[:, 24:32], in0=g.stage_f[:, 16:24],
                scalar1=999.0, scalar2=None, op0=ALU.add)
        g.P.emit("ve", (lambda o, i: lambda e: e.tensor_reduce(
            out=o, in_=i, axis=AX.X, op=ALU.min))(g.route[:, ti:ti + 1], g.stage_f[:, 24:32]),
            waits=(("ve", v6),), inc=("ve", 1))
        ve(g, "tensor_scalar", out=g.onehot[:, 8 * ti:8 * ti + 8], in0=g.iota8[:],
           scalar1=g.route[:, ti:ti + 1], scalar2=None, op0=ALU.is_equal)
    v_route = g.P.val("ve")

    for ti in range(NT):
        m = mm(g, g.p_m2[0:8, ti:ti + 1], g.onehot[:, 8 * ti:8 * ti + 8], g.onescol_f[:],
               True, True,
               waits=([("ve", v_route), ("ve", g.v_ones)] + g.r_pm2.for_write()) if ti == 0 else ())
    g.r_pm2.wrote("mm", m)
    v = ve(g, "tensor_copy", out=g.small[0:8, 0:16], in_=g.p_m2[0:8, 0:16],
           waits=g.r_pm2.for_read() + g.r_small.for_write())
    g.r_pm2.read("ve", v)
    v_tot = ve(g, "reduce_sum", out=g.small[0:8, 48:49], in_=g.small[0:8, 0:16], axis=AX.X)
    g.r_small.wrote("ve", v_tot)
    m = tr_mm(g, g.p_m1[0:16, 0:8], g.small[0:8, 0:16], g.identf,
              waits=g.r_small.for_read() + g.r_pm1.for_write())
    g.r_pm1.wrote("mm", m)
    g.r_small.read("mm", m)
    v = ve(g, "tensor_copy", out=g.small[0:16, 16:24], in_=g.p_m1[0:16, 0:8],
           waits=g.r_pm1.for_read())
    g.r_pm1.read("ve", v)
    m = mm(g, g.p_m1[0:16, 0:8], g.sl16[:], g.small[0:16, 16:24], True, True,
           waits=[("ve", v), ("sy", g.w_consts)] + g.r_pm1.for_write())
    g.r_pm1.wrote("mm", m)
    v = ve(g, "tensor_copy", out=g.small[0:16, 24:32], in_=g.p_m1[0:16, 0:8],
           waits=g.r_pm1.for_read())
    g.r_pm1.read("ve", v)
    m = tr_mm(g, g.p_m1[0:8, 0:16], g.small[0:16, 24:32], g.identf,
              waits=[("ve", v)] + g.r_pm1.for_write())
    g.r_pm1.wrote("mm", m)
    v_excl = ve(g, "tensor_copy", out=g.small[0:8, 32:48], in_=g.p_m1[0:8, 0:16],
                waits=g.r_pm1.for_read())
    g.r_pm1.read("ve", v_excl)
    m = tr_mm(g, g.p_m1[0:1, 0:8], g.small[0:8, 48:49], g.identf,
              waits=[("ve", v_tot)] + g.r_pm1.for_write())
    g.r_pm1.wrote("mm", m)
    v = ve(g, "tensor_copy", out=g.cntrow[:, 0:8], in_=g.p_m1[0:1, 0:8], waits=g.r_pm1.for_read())
    g.r_pm1.read("ve", v)
    w1 = dma(g, "sy", g.d_ccin, g.cntrow[:, 0:8], waits=(("ve", v),))
    g.P.emit("gp", (lambda: lambda e: e.collective_compute(
        "AllGather", ALU.bypass, replica_groups=[[0, 1, 2, 3, 4, 5, 6, 7]],
        ins=[g.d_ccin.opt()], outs=[g.d_ccout.opt()]))(),
        waits=(("sy", w1),), inc=("gpc", 1))
    g.c_ccdone = g.P.val("gpc")

    m = mm(g, g.p_m1[0:16, 0:8], g.ones16[:], g.cntrow[:, 0:8], True, True,
           waits=[("ve", v)] + g.r_pm1.for_write())
    g.r_pm1.wrote("mm", m)
    v = ve(g, "tensor_copy", out=g.small[0:16, 50:58], in_=g.p_m1[0:16, 0:8],
           waits=g.r_pm1.for_read())
    g.r_pm1.read("ve", v)
    v = ve(g, "tensor_tensor", out=g.small[0:16, 50:58], in0=g.small[0:16, 50:58],
           in1=g.thr16[:], op=ALU.is_gt)
    m = mm(g, g.p_m1[0:8, 0:1], g.small[0:16, 50:58], g.onescol_f[0:16, 0:1], True, True,
           waits=[("ve", v), ("ve", g.v_ones)] + g.r_pm1.for_write())
    g.r_pm1.wrote("mm", m)
    v = ve(g, "tensor_scalar", out=g.small[0:8, 58:59], in0=g.p_m1[0:8, 0:1], scalar1=128.0,
           scalar2=None, op0=ALU.mult, waits=g.r_pm1.for_read())
    g.r_pm1.read("ve", v)
    m = mm(g, g.p_m2[0:8, 0:1], g.sl8[:], g.small[0:8, 58:59], True, True,
           waits=[("ve", v)] + g.r_pm2.for_write())
    g.r_pm2.wrote("mm", m)
    v_seg = ve(g, "tensor_copy", out=g.small[0:8, 59:60], in_=g.p_m2[0:8, 0:1],
               waits=g.r_pm2.for_read())
    g.r_pm2.read("ve", v_seg)
    m = tr_mm(g, g.p_m1[0:1, 0:8], g.small[0:8, 59:60], g.identf,
              waits=[("ve", v_seg)] + g.r_pm1.for_write())
    g.r_pm1.wrote("mm", m)
    v_segr = ve(g, "tensor_copy", out=g.segrow[:], in_=g.p_m1[0:1, 0:8], waits=g.r_pm1.for_read())
    g.r_pm1.read("ve", v_segr)
    m = mm(g, g.p_m1[:, 0:8], g.ones128[:], g.segrow[:], True, True,
           waits=[("ve", v_segr)] + g.r_pm1.for_write())
    g.r_pm1.wrote("mm", m)
    v_segb = ve(g, "tensor_copy", out=g.bcast8[:, 16:24], in_=g.p_m1[:, 0:8],
                waits=g.r_pm1.for_read())
    g.r_pm1.read("ve", v_segb)

    v_rpl = ve(g, "tensor_scalar", out=g.small[0:8, 32:48], in0=g.small[0:8, 32:48],
               scalar1=-1.0, scalar2=None, op0=ALU.add, waits=(("ve", v_excl),))
    pools = [(g.p_m2, g.r_pm2), (g.p_den, g.r_pden)]
    tpools = [(g.p_m1, g.r_pm1), (g.p_ctx, g.r_pctx)]
    for ti in range(NT):
        psu, rsu = pools[ti % 2]
        ptr_, rtr = tpools[ti % 2]
        m = mm(g, psu[:, 0:8], g.triu[:], g.onehot[:, 8 * ti:8 * ti + 8], True, True,
               waits=([("sy", g.w_consts)] if ti == 0 else []) + rsu.for_write())
        rsu.wrote("mm", m)
        m2 = tr_mm(g, ptr_[0:8, 0:128], g.onehot[:, 8 * ti:8 * ti + 8], g.identf,
                   waits=rtr.for_write())
        rtr.wrote("mm", m2)
        v = ve(g, "tensor_copy", out=g.den_s[0:8, 128 * (ti % 4):128 * (ti % 4) + 128],
               in_=ptr_[0:8, 0:128], waits=rtr.for_read() + (g.r_dens.for_write() if ti < 4 else []))
        rtr.read("ve", v)
        v = ve(g, "tensor_mul", out=g.stage_f[:, 0:8], in0=psu[:, 0:8],
               in1=g.onehot[:, 8 * ti:8 * ti + 8], waits=rsu.for_read() + g.r_stagef.for_write())
        rsu.read("ve", v)
        g.r_stagef.wrote("ve", v)
        v_par = ve(g, "reduce_sum", out=g.stat[:, 0:1], in_=g.stage_f[:, 0:8], axis=AX.X)
        g.r_stagef.read("ve", v_par)
        m3 = mm(g, ptr_[0:128, 8:9], g.den_s[0:8, 128 * (ti % 4):128 * (ti % 4) + 128],
                g.small[0:8, 32 + ti:32 + ti + 1], True, True,
                waits=[("ve", v_rpl), ("ve", g.P.val("ve"))] + rtr.for_write())
        rtr.wrote("mm", m3)
        v = ve(g, "tensor_add", out=g.posl[:, ti:ti + 1], in0=g.stat[:, 0:1],
               in1=ptr_[0:128, 8:9], waits=rtr.for_read())
        rtr.read("ve", v)
        v = ve(g, "tensor_mul", out=g.stage_f[:, 8:16], in0=g.onehot[:, 8 * ti:8 * ti + 8],
               in1=g.bcast8[:, 16:24], waits=(("ve", v_segb),) if ti == 0 else ())
        v = ve(g, "reduce_sum", out=g.stat[:, 1:2], in_=g.stage_f[:, 8:16], axis=AX.X)
        v = ve(g, "tensor_add", out=g.stat[:, 2:3], in0=g.stat[:, 1:2], in1=g.posl[:, ti:ti + 1])
        ve(g, "tensor_copy", out=g.dest32[:, ti:ti + 1], in_=g.stat[:, 2:3])
    g.v_dest = g.P.val("ve")

    v = ve(g, "memset", g.adjrow[:, 0:23], 0.0)
    for e in range(8):
        v1 = ve(g, "tensor_scalar", out=g.adjrow[:, 32:55], in0=g.iota23[:],
                scalar1=g.segrow[0:1, e:e + 1], scalar2=None, op0=ALU.is_ge,
                waits=(("ve", v_segr),) if e == 0 else ())
        v = ve(g, "tensor_add", out=g.adjrow[:, 0:23], in0=g.adjrow[:, 0:23],
               in1=g.adjrow[:, 32:55])
    v = ve(g, "tensor_scalar", out=g.adjrow[:, 0:23], in0=g.adjrow[:, 0:23], scalar1=128.0,
           scalar2=-128.0, op0=ALU.mult, op1=ALU.add)
    m = mm(g, g.p_m1[:, 0:23], g.ones128[:], g.adjrow[:, 0:23], True, True,
           waits=[("ve", v)] + g.r_pm1.for_write())
    g.r_pm1.wrote("mm", m)
    g.P.emit("ve", (lambda o, a, b: lambda e: e.scalar_tensor_tensor(
        out=o, in0=a, scalar=1.0, in1=b, op0=ALU.mult, op1=ALU.add))(
        g.small[:, 40:63], g.p_m1[:, 0:23], g.iotap[:].broadcast_to([128, 23])),
        waits=tuple(g.r_pm1.for_read() + g.r_small.for_write() + [("sy", g.w_consts),
                    ("ve", g.P.val("ve"))]),
        inc=("ve", 1))
    v = g.P.val("ve")
    g.r_pm1.read("ve", v)
    g.r_small.wrote("ve", v)
    g.v_widx = ve(g, "tensor_copy", out=g.widx32[:, 0:23], in_=g.small[:, 40:63],
                  waits=g.r_small.for_read())
    g.r_small.read("ve", g.v_widx)


def emit_kept(g):
    w2 = dma(g, "sy", g.gab[:], g.d_ccout, waits=(("gpc", g.c_ccdone),))
    m = mm(g, g.p_m1[0:1, 0:8], g.rmask[:], g.gab[:], True, True,
           waits=[("sy", w2), ("sy", g.w_consts)] + g.r_pm1.for_write())
    g.r_pm1.wrote("mm", m)
    v_co = ve(g, "tensor_copy", out=g.corow[:], in_=g.p_m1[0:1, 0:8], waits=g.r_pm1.for_read())
    g.r_pm1.read("ve", v_co)
    m = mm(g, g.p_m1[:, 0:8], g.ones128[:], g.corow[:], True, True,
           waits=[("ve", v_co)] + g.r_pm1.for_write())
    g.r_pm1.wrote("mm", m)
    v_cob = ve(g, "tensor_copy", out=g.bcast8[:, 8:16], in_=g.p_m1[:, 0:8],
               waits=g.r_pm1.for_read())
    g.r_pm1.read("ve", v_cob)
    for ti in range(NT):
        v = ve(g, "tensor_mul", out=g.stage_f[:, 0:8], in0=g.onehot[:, 8 * ti:8 * ti + 8],
               in1=g.bcast8[:, 8:16],
               waits=([("ve", v_cob)] + g.r_stagef.for_write()) if ti == 0 else ())
        v = ve(g, "reduce_sum", out=g.stat[:, 0:1], in_=g.stage_f[:, 0:8], axis=AX.X)
        v = ve(g, "tensor_add", out=g.stat[:, 1:2], in0=g.stat[:, 0:1], in1=g.posl[:, ti:ti + 1])
        ve(g, "tensor_scalar", out=g.kept[:, ti:ti + 1], in0=g.stat[:, 1:2],
           scalar1=float(CAP), scalar2=None, op0=ALU.is_lt)
    g.r_stagef.wrote("ve", g.P.val("ve"))
    g.v_kept = g.P.val("ve")



def emit_experts(g):
    P = g.P
    w_scat = 0
    for ti in range(NT):
        g.P.emit("gp", (lambda o, i, ix: lambda e: e.indirect_dma_start(
            out=o, out_offset=IndirectOffsetOnAxis(ap=ix, axis=0), in_=i, in_offset=None))(
            g.d_srt, g.fmA[:, 1024 * ti:1024 * ti + 1024], g.dest32[:, ti:ti + 1]),
            waits=((("ve", g.v_dest),) + tuple(g.r_fmA.for_read())) if ti == 0 else (),
            inc=("gp", 16))
        w_scat = g.P.val("gp")
    g.r_fmA.read("gp", w_scat)

    last_v_evac = [0, 0]
    last_st = [0, 0]
    last_m_ps = [0, 0]
    last_m_wb = [0, 0, 0]
    last_m_srt = [0, 0, 0, 0]
    last_m_ld = [0, 0, 0, 0]
    for f in range(NFILL):
        p2, p3, p4 = f % 2, f % 3, f % 4
        wb = g.fmB[:, 9216 * p3:9216 * p3 + 9216]
        g.P.emit("gp", (lambda o, ix: lambda e: e.indirect_dma_start(
            out=o, out_offset=None, in_=g.d_wt,
            in_offset=IndirectOffsetOnAxis(ap=ix, axis=0)))(wb, g.widx32[:, f:f + 1]),
            waits=((("ve", g.v_widx), ("sy", g.w_wt)) + tuple(g.r_fmB.for_read() + g.r_fmB.for_write())
                   if f < 3 else (("mm", last_m_wb[p3]),)),
            inc=("gp", 16))
        w_w = g.P.val("gp")
        sslot = g.ctxall[:, 1024 * p4:1024 * p4 + 1024]
        w_s = dma(g, "sy", sslot, g.d_srt[128 * f:128 * f + 128, :],
                  waits=([("gp", w_scat)] + g.r_ctxall.for_read() + g.r_ctxall.for_write())
                  if f < 4 else [("mm", last_m_ld[p4])])
        srtT = g.vs65[:, 1024 * p4:1024 * p4 + 1024]
        for kc in range(8):
            m = tr_mm(g, g.p_m1[:, 0:128], sslot[:, 128 * kc:128 * kc + 128],
                      g.identbf,
                      waits=([("sy", w_s)] + g.r_pm1.for_write()
                             + (g.r_vs65.for_write() if f < 4 else [("mm", last_m_srt[p4])]))
                      if kc == 0 else g.r_pm1.for_write())
            g.r_pm1.wrote("mm", m)
            if kc == 7:
                last_m_ld[p4] = m
            v = ve(g, "tensor_copy", out=srtT[:, 128 * kc:128 * kc + 128], in_=g.p_m1[:, 0:128],
                   waits=g.r_pm1.for_read())
            g.r_pm1.read("ve", v)
        v_t = v
        psl = g.p_sc[:, 1024 * p2:1024 * p2 + 1024]
        for nk in range(2):
            for kc in range(8):
                m = mm(g, psl[:, 512 * nk:512 * nk + 512], srtT[:, 128 * kc:128 * kc + 128],
                       wb[:, 1024 * kc + 512 * nk:1024 * kc + 512 * nk + 512], kc == 0, False,
                       waits=([("ve", v_t), ("gp", w_w), ("ve", last_v_evac[p2])]
                              + (g.r_psc.for_write() if f < 2 else [])) if (nk == 0 and kc == 0) else ())
            m = mm(g, psl[:, 512 * nk:512 * nk + 512], g.onesrowbf[:],
                   wb[0:1, 8192 + 512 * nk:8192 + 512 * nk + 512], False, True)
        last_m_ps[p2] = m
        last_m_wb[p3] = m
        last_m_srt[p4] = m
        v = ve(g, "tensor_copy", out=g.stage_f[:, 1024 * p2:1024 * p2 + 1024], in_=psl,
               waits=[("mm", m)] + (g.r_stagef.for_write() if f < 2
                                    else [("scq", last_st[p2])]))
        last_v_evac[p2] = v
        last_st[p2] = dma(g, "scq", g.d_srto[128 * f:128 * f + 128, :],
                          g.stage_f[:, 1024 * p2:1024 * p2 + 1024], waits=(("ve", v),))
    g.w_eo = P.val("scq")
    g.r_psc.wrote("mm", max(last_m_ps))
    g.r_stagef.wrote("ve", max(last_v_evac))
    g.r_fmB.wrote("gp", g.P.val("gp"))
    g.r_fmB.read("mm", max(last_m_ps))
    g.r_fmB.read("ve", max(last_v_evac))
    g.r_ctxall.wrote("sy", g.P.val("sy"))
    g.r_ctxall.read("mm", max(last_m_ps))
    g.r_vs65.wrote("ve", g.P.val("ve"))
    g.r_vs65.read("mm", max(last_m_ps))



def emit_tail(g):
    P = g.P
    emit_kept(g)
    slots = [g.attb[:, 0:2048].bitcast(F32), g.attb[:, 2048:4096].bitcast(F32),
             g.stage_f[:, 0:1024], g.stage_f[:, 1024:2048]]
    slot_res = [Res(), Res(), Res(), Res()]
    ybufs = [g.vs65[:, 4096 + 2048 * p:4096 + 2048 * p + 2048].bitcast(F32) for p in range(2)]
    ybuf_res = [Res(), Res()]
    w_gs = [0] * NT
    w_x1s = [0] * NT
    for ti in range(NT):
        sp = ti % 4
        g.P.emit("gp", (lambda o, ix: lambda e: e.indirect_dma_start(
            out=o, out_offset=None, in_=g.d_srto,
            in_offset=IndirectOffsetOnAxis(ap=ix, axis=0)))(
            slots[sp], g.dest32[:, ti:ti + 1]),
            waits=((("scq", g.w_eo), ("ve", g.v_kept)) + tuple(g.r_stagef.for_write() + g.r_attb0.for_write()
                   + g.r_attb1.for_write() + g.r_attb2.for_write() + g.r_attb3.for_write())
                   if ti == 0 else tuple(slot_res[sp].for_write())),
            inc=("gp", 16))
        w_gs[ti] = g.P.val("gp")
        slot_res[sp].wrote("gp", w_gs[ti])
        w_x1s[ti] = dma(g, "sy", g.x_tok[:, 1024 * ti:1024 * ti + 1024],
                        g.d_x1f[128 * ti:128 * ti + 128, :],
                        waits=g.r_xtok.for_write() if ti == 0 else ())
        if ti < 3:
            continue
        emit_tail_tile(g, ti - 3, slots, slot_res, ybufs, ybuf_res, w_gs, w_x1s)
    for ti in range(NT - 3, NT):
        emit_tail_tile(g, ti, slots, slot_res, ybufs, ybuf_res, w_gs, w_x1s)
    w_out = g.P.val("scq")
    g.P.emit("ve", (lambda: lambda e: e.memset(g.stat[:, 15:16], 0.0))(),
             waits=(("scq", w_out),), inc=("ve", 1))


def emit_tail_tile(g, ti, slots, slot_res, ybufs, ybuf_res, w_gs, w_x1s):
    sp = ti % 4
    yp = ti % 2
    so = 16 * yp
    eo = slots[sp]
    yb = ybufs[yp]
    x2 = yb[:, 0:1024]
    x3 = yb[:, 0:1024]
    v1 = ve(g, "tensor_mul", out=g.stat[:, 32 + 2 * yp:33 + 2 * yp], in0=g.rp[:, ti:ti + 1],
            in1=g.kept[:, ti:ti + 1])
    v2 = ve(g, "tensor_scalar", out=g.stat[:, 33 + 2 * yp:34 + 2 * yp],
            in0=g.stat[:, 32 + 2 * yp:33 + 2 * yp], scalar1=-1.0, scalar2=None,
            op0=ALU.mult)
    v2 = ve(g, "tensor_add", out=g.stat[:, 33 + 2 * yp:34 + 2 * yp],
            in0=g.stat[:, 33 + 2 * yp:34 + 2 * yp], in1=g.rp[:, ti:ti + 1])
    v3 = ve(g, "tensor_scalar", out=x2, in0=eo, scalar1=g.stat[:, 32 + 2 * yp:33 + 2 * yp],
            scalar2=None, op0=ALU.mult,
            waits=[("gp", w_gs[ti]), ("ve", v1)] + ybuf_res[yp].for_write())
    slot_res[sp].read("ve", v3)
    g.P.emit("ve", (lambda o, i0, s1, i1: lambda e: e.scalar_tensor_tensor(
        out=o, in0=i0, scalar=s1, in1=i1, op0=ALU.mult, op1=ALU.add))(
        x2, g.fmA[:, 1024 * ti:1024 * ti + 1024], g.stat[:, 33 + 2 * yp:34 + 2 * yp],
        x2), waits=(("ve", v3), ("ve", v2)), inc=("ve", 1))
    if g.stage == "tl0" and ti == 0:
        wd = dma(g, "sy", g.dbg6[0:128, :], eo, waits=(("gp", w_gs[0]),))
        wd = dma(g, "sy", g.dbg6[128:256, :], g.x_tok[:, 0:1024], waits=(("sy", w_x1s[0]),))
        wd = dma(g, "sy", g.dbg6[256:384, :], x2, waits=(("ve", g.P.val("ve")),))
    v4 = ve(g, "tensor_add", out=x2, in0=x2,
            in1=g.x_tok[:, 1024 * ti:1024 * ti + 1024], waits=(("sy", w_x1s[ti]),))
    if g.stage == "tl0" and ti == 0:
        wd = dma(g, "sy", g.dbg6[384:512, :], x2, waits=(("ve", v4),))
    emit_ln_once(g, x2, eo, None, affine=True, waits0=(("ve", v4),), so=so)
    if g.stage == "tl0" and ti == 0:
        wd = dma(g, "sy", g.dbg6[512:640, :], eo, waits=(("ve", g.P.val("ve")),))
    v5 = ve(g, "tensor_add", out=x3, in0=eo,
            in1=g.x_tok[:, 1024 * ti:1024 * ti + 1024])
    if g.stage == "tl0" and ti == 0:
        wd = dma(g, "sy", g.dbg6[640:768, :], x3, waits=(("ve", v5),))
        wd = dma(g, "sy", g.dbg6[768:769, 0:40], g.stat[0:1, 0:40], waits=(("ve", v5),))
        wd = dma(g, "sy", g.dbg6[769:897, 0:16], g.kept[:, 0:16], waits=(("ve", v5),))
        wd = dma(g, "sy", g.dbg6[897:1025, 0:16], g.rp[:, 0:16], waits=(("ve", v5),))
    slot_res[sp].read("ve", v5)
    last = emit_ln_once(g, x3, g.x_tok[:, 1024 * ti:1024 * ti + 1024], None, affine=False,
                        waits0=(("ve", v5),), so=so)
    vz = g.P.val("ve")
    ybuf_res[yp].wrote("ve", vz)
    ybuf_res[yp].read("ve", vz)
    w = dma(g, "scq", g.out[128 * ti:128 * ti + 128, :], g.x_tok[:, 1024 * ti:1024 * ti + 1024],
            waits=[last])
    g.r_xtok.read("scq", w)



def shard_inputs(inputs, core, consts):
    z = inputs
    m = {
        "x": np.ascontiguousarray(z["L0_attn_x"][core]),
        "wdq": z["L0_attn_W_dq"], "wuq": z["L0_attn_W_uq"], "wdkv": z["L0_attn_W_dkv"],
        "wuk": z["L0_attn_W_uk"], "wuv": z["L0_attn_W_uv"], "wo": z["L0_attn_W_o"],
        "wq": np.ascontiguousarray(z["L0_ffn_Wq"].reshape(D, D)),
        "wk": np.ascontiguousarray(z["L0_ffn_Wk"].reshape(D, D)),
        "wv": np.ascontiguousarray(z["L0_ffn_Wv"].reshape(D, D)),
        "bq": np.ascontiguousarray(z["L0_ffn_bq"].reshape(1, D)),
        "bk": np.ascontiguousarray(z["L0_ffn_bk"].reshape(1, D)),
        "bv": np.ascontiguousarray(z["L0_ffn_bv"].reshape(1, D)),
        "we": np.ascontiguousarray(z["L0_ffn_We"].reshape(E * D, D)),
        "be": np.ascontiguousarray(z["L0_ffn_be"]),
        "wsw": z["L0_ffn_Wsw"],
        "bsw": np.ascontiguousarray(z["L0_ffn_bsw"].reshape(1, E)),
        "gamma": np.ascontiguousarray(z["L0_ffn_gamma"].reshape(1, D)),
        "beta": np.ascontiguousarray(z["L0_ffn_beta"].reshape(1, D)),
        "rankmask": (np.arange(8) < core).astype(np.float32).reshape(8, 1),
    }
    m.update(consts)
    return m


_NC_CACHE = {}


def run(inputs, stage="full"):
    if stage not in _NC_CACHE:
        _NC_CACHE[stage] = build(stage)
    nc = _NC_CACHE[stage]
    consts = host_consts()
    in_maps = [shard_inputs(inputs, c, consts) for c in range(8)]
    return run_bass_kernel_spmd(nc, in_maps, core_ids=list(range(8)))


def kernel(**inputs):
    res = run(inputs, "full")
    return np.stack([res.results[c]["out"] for c in range(8)], axis=0)


# version 21
# speedup vs baseline: 1.0158x; 1.0008x over previous
import numpy as np
import ml_dtypes
import concourse.bass as bass
import concourse.mybir as mybir
from concourse.bass import IndirectOffsetOnAxis
from concourse.bass_utils import run_bass_kernel_spmd

F32 = mybir.dt.float32
BF16 = mybir.dt.bfloat16
U32 = mybir.dt.uint32
AF = mybir.ActivationFunctionType
ALU = mybir.AluOpType
AX = mybir.AxisListType
BF = ml_dtypes.bfloat16

T, D, NH, HS, L, QL, E = 2048, 1024, 16, 64, 128, 128, 8
CAP = 2457
NT = 16
NCK = 4
NFILL = 23
EPS = 1e-5



class Prog:
    def __init__(self):
        self.ops = []
        self.cnt = {}

    SELF_SEM = {"ve": "ve", "sc": "sc"}

    def emit(self, eng, fn, waits=(), inc=None):
        if inc is not None and inc[0] in ("ve", "sc", "gpc"):
            waits = tuple(waits) + ((inc[0], self.cnt.get(inc[0], 0)),)
        import sys
        fr = sys._getframe(1)
        tag = []
        for _ in range(4):
            if fr is None:
                break
            tag.append(fr.f_lineno)
            fr = fr.f_back
        self._tag = tuple(tag)
        ws = {}
        for s, v in waits:
            if v and v > ws.get(s, 0):
                ws[s] = v
        if inc is not None:
            sem, n = inc
            self.cnt[sem] = self.cnt.get(sem, 0) + n
        self.ops.append((eng, fn, tuple(ws.items()), inc, self._tag))

    def val(self, sem):
        return self.cnt.get(sem, 0)

    def replay(self, eng_name, engine, sems):
        for eng, fn, waits, inc, tag in self.ops:
            if eng != eng_name:
                continue
            for sem, v in waits:
                engine.wait_ge(sems[sem], v)
            try:
                inst = fn(engine)
            except Exception as ex:
                raise RuntimeError(f"op failed, emitted at lines {tag}: {ex}") from ex
            if inc is not None:
                sem, n = inc
                inst.then_inc(sems[sem], n)


class Res:

    def __init__(self):
        self.wsem = None
        self.readers = []

    def for_write(self):
        ws = list(self.readers)
        if self.wsem:
            ws.append(self.wsem)
        return ws

    def wrote(self, sem, val):
        self.wsem = (sem, val)
        self.readers = []

    def for_read(self):
        return [self.wsem] if self.wsem else []

    def read(self, sem, val):
        self.readers.append((sem, val))


class Ctx:
    pass



def host_consts():
    c = {}
    c["identbf"] = np.eye(128, dtype=BF)
    c["identf"] = np.eye(128, dtype=np.float32)
    c["triu"] = np.triu(np.ones((128, 128), np.float32))
    c["sl8"] = np.triu(np.ones((8, 8), np.float32), 1)
    c["sl16"] = np.triu(np.ones((16, 16), np.float32), 1)
    c["ones128"] = np.ones((1, 128), np.float32)
    c["onesrowbf"] = np.ones((1, 128), BF)
    c["onescolbf"] = np.ones((128, 1), BF)
    c["ones16"] = np.ones((1, 16), np.float32)
    c["iota8"] = np.tile(np.arange(8, dtype=np.float32), (128, 1))
    c["iota8m"] = (c["iota8"] - 999.0).astype(np.float32)
    c["iotap"] = np.arange(128, dtype=np.float32).reshape(128, 1)
    c["iota23"] = (128.0 * np.arange(NFILL, dtype=np.float32)).reshape(1, NFILL)
    c["thr16"] = (128.0 * np.arange(16, dtype=np.float32)).reshape(16, 1) * np.ones((1, 8), np.float32)
    p = np.arange(128).reshape(128, 1)
    f = np.arange(512).reshape(1, 512)
    m = np.zeros((128, 2048), np.float32)
    for j in range(4):
        m[:, 512 * j:512 * j + 512] = (f >= (128 * j + p)).astype(np.float32)
    c["maskc"] = m.astype(BF)
    c["ones2048bf"] = np.ones((128, 2048), BF)
    sel = np.zeros((16, 4, 512), np.float32)
    for tc in range(4):
        sel[4 * (tc + 1):, tc, :] = 1.0
    c["selbf"] = sel.reshape(16, 2048).astype(BF)
    oh16 = np.zeros((128, 256), np.float32)
    ohr = np.zeros((128, 256), np.float32)
    for h in range(16):
        oh16[:, 16 * h + h] = 1.0
        ohr[64, 16 * h + h] = 1.0
    c["oh16bf"] = oh16.astype(BF)
    c["ohrow16"] = ohr
    return c



def build(stage="full"):
    nc = bass.Bass("TRN2", target_bir_lowering=False)
    g = Ctx()
    g.nc = nc
    g.stage = stage
    g.P = Prog()

    di = lambda n, sh, dt=F32: nc.dram_tensor(n, sh, dt, kind="ExternalInput").ap()
    do = lambda n, sh, dt=F32: nc.dram_tensor(n, sh, dt, kind="ExternalOutput").ap()
    dint = lambda n, sh, dt=F32: nc.dram_tensor(n, sh, dt, kind="Internal").ap()

    g.x = di("x", [T, D])
    g.wdq = di("wdq", [QL, D]); g.wuq = di("wuq", [D, QL])
    g.wdkv = di("wdkv", [L, D]); g.wuk = di("wuk", [D, L]); g.wuv = di("wuv", [D, L])
    g.wo = di("wo", [D, D])
    g.wq = di("wq", [D, D]); g.wk = di("wk", [D, D]); g.wv = di("wv", [D, D])
    g.bq = di("bq", [1, D]); g.bk = di("bk", [1, D]); g.bv = di("bv", [1, D])
    g.we = di("we", [E * D, D]); g.be = di("be", [E, D])
    g.wsw = di("wsw", [E, D]); g.bsw = di("bsw", [1, E])
    g.gamma = di("gamma", [1, D]); g.beta = di("beta", [1, D])
    g.rankmask = di("rankmask", [8, 1])
    g.hc = host_consts()
    for k, v in g.hc.items():
        setattr(g, "c_" + k, di(k, list(v.shape), BF16 if v.dtype == BF else F32))
    g.out = do("out", [T, D])
    if stage == "x1":
        g.dbg = do("dbg_x1", [T, D], F32)
    elif stage == "xy":
        g.dbg = do("dbg_xy", [T, D], F32)
    elif stage == "ckv":
        g.dbg = do("dbg_ckv", [128, 2048], BF16)
        g.dbg5 = do("dbg_xT", [128, 16384], BF16)
        g.dbg2 = do("dbg_qt", [128, 16384], BF16)
        g.dbg3 = do("dbg_keff", [128, 1024], BF16)
        g.dbg4 = do("dbg_veff", [128, 1024], BF16)
    elif stage == "hid":
        g.dbg = do("dbg_hid", [T, D], BF16)
    elif stage == "route":
        g.dbg = do("dbg_route", [128, 16 * 8], F32)
    elif stage == "full2":
        g.dbg = do("dbg_route", [128, 16 * 8], F32)
        g.dbg6 = do("dbg_eo", [T, D], F32)
        g.dbg7 = do("dbg_tail", [128, 5 * D], F32)
    elif stage == "tl0":
        g.dbg6 = do("dbg_eo", [T, D], F32)

    g.d_xbf = dint("d_xbf", [T, D], BF16)
    g.d_x1bf = dint("d_x1bf", [T, D], BF16)
    g.d_hidbf = dint("d_hidbf", [T, D], BF16)
    g.d_x1f = dint("d_x1f", [T, D], F32)
    g.d_webf = dint("d_webf", [E * D, D], BF16)
    g.d_wt = dint("d_wt", [E * 128, 9 * 1024], BF16)
    g.d_wqkvbf = dint("d_wqkvbf", [3 * D, D], BF16)
    g.d_wobf = dint("d_wobf", [D, D], BF16)
    g.d_smallbf = dint("d_smallbf", [3 * 128, D], BF16)
    g.d_srt = dint("d_srt", [NFILL * 128, D], BF16)
    g.d_srto = dint("d_srto", [NFILL * 128, D], F32)
    g.d_ccin = dint("d_ccin", [1, 8])
    g.d_ccout = dint("d_ccout", [8, 8])

    from contextlib import ExitStack
    with ExitStack() as ctx:
        S = lambda n, sh, dt=F32: ctx.enter_context(nc.sbuf_tensor(n, sh, dt))
        Ps = lambda n, sh: ctx.enter_context(nc.psum_tensor(n, sh, F32))

        g.fmA = S("fmA", [128, 16384], BF16)
        g.fmB = S("fmB", [128, 32768], BF16)
        g.x_tok = g.fmB[:].bitcast(F32)
        g.vs65 = S("vs65", [128, NH * 16 * 65], BF16)
        g.ctxall = S("ctxall", [128, 16 * 512], BF16)
        g.attb = S("attb", [128, 2 * 2048], BF16)
        g.wqkvT = g.ctxall
        g.qlb = S("qlb", [128, 1024], BF16)
        g.mlakv = S("mlakv", [128, 4096], BF16)
        g.ckvT = g.mlakv[:, 0:2048]
        g.ckvS = g.mlakv[:, 2048:4096]
        g.q2T = g.attb[:, 0:2048]
        g.gbc = g.mlakv[:].bitcast(F32)
        g.wsmall = S("wsmall", [128, 3 * 1024], BF16)
        g.prepw = S("prepw", [128, 2048], BF16)
        g.wuk_s = g.prepw[:, 0:1024]
        g.wuv_s = g.prepw[:, 1024:2048]
        g.keff = S("keff", [128, 1024], BF16)
        g.veff = S("veff", [128, 1024], BF16)
        g.wtmps = S("wtmps", [128, 2048], BF16)
        g.wtmp = g.wtmps[:, 0:1024]
        g.wtmp2 = g.wtmps[:, 1024:2048]
        g.wswT = S("wswT", [128, 64], BF16)
        g.bqkv = S("bqkv", [128, 24])
        g.u65 = S("u65", [128, 512], BF16)
        g.den_s = S("den_s", [128, 512])
        g.r16 = S("r16", [128, 512])
        g.rtok = S("rtok", [128, 64])
        g.stat = S("stat", [128, 48])
        g.lg = S("lg", [128, 128])
        g.route = S("route", [128, 16])
        g.rp = S("rp", [128, 16])
        g.onehot = S("onehot", [128, 128])
        g.kept = S("kept", [128, 16])
        g.posl = S("posl", [128, 16])
        g.dest32 = S("dest32", [128, NT], U32)
        g.widx32 = S("widx32", [128, NFILL], U32)
        g.cntrow = S("cntrow", [1, 8 * 17])
        g.adjrow = S("adjrow", [1, 128])
        g.segrow = S("segrow", [1, 8])
        g.corow = S("corow", [1, 8])
        g.gab = S("gab", [8, 8])
        g.small = S("small", [128, 64])
        g.svT = S("svT", [128, NH * 65], BF16)
        g.sv_s = S("sv_s", [128, 16])
        g.bcast8 = S("bcast8", [128, 32])
        g.stage_f = S("stage_f", [128, 2048])
        g.stage_b = S("stage_b", [128, 2048], BF16)
        g.stage_w = None
        g.bswr = S("bswr", [1, 8])
        g.rmask = S("rmask", [8, 1])
        g.onescol_f = S("onescol_f", [128, 1])
        _deferred = ("maskc", "ones2048bf", "selbf")
        for n in g.hc:
            if n in _deferred:
                continue
            setattr(g, n, S("s_" + n, list(g.hc[n].shape), BF16 if g.hc[n].dtype == BF else F32))
        g.maskc = g.wsmall[:, 0:2048]
        g.ones2048bf = g.prepw[:, 0:2048]
        g.selbf = g.wtmps[0:16, 0:2048]

        g.p_sc = Ps("p_sc", [128, 2048])
        g.p_ctx = Ps("p_ctx", [128, 512])
        g.p_den = Ps("p_den", [128, 512])
        g.p_m1 = Ps("p_m1", [128, 512])
        g.p_m2 = Ps("p_m2", [128, 512])

        for r in ["r_psc", "r_pscB", "r_pctx", "r_pden", "r_pm1", "r_pm2", "r_attb0", "r_attb1",
                  "r_attb2", "r_attb3",
                  "r_qlb", "r_qlb2", "r_u65", "r_dens", "r_r16", "r_rtok", "r_ctxall", "r_stagef",
                  "r_stageb", "r_fmA", "r_fmB", "r_vs65", "r_wtmp",
                  "r_wtmp2", "r_small", "r_svs", "r_sv", "r_mlakv"]:
            setattr(g, r, Res())
        g.r_xtok = g.r_fmB
        g.r_wqkvT = g.r_ctxall
        g.abres = [g.r_attb0, g.r_attb1, g.r_attb2, g.r_attb3]
        g.r_pscs = [g.r_psc, g.r_pscB]
        g.stage_w = g.stage_b[:, 1024:2048]
        g.r_stagew = g.r_stageb

        sems = {}
        for n in ["sy", "gp", "scq", "mm", "ve", "sc", "gpc"]:
            sems[n] = ctx.enter_context(nc.semaphore(n))
        g.sems = sems

        emit_program(g)

        block = ctx.enter_context(nc.Block())

        @block.sync
        def _(e):
            g.P.replay("sy", e, sems)

        @block.scalar
        def _(e):
            g.P.replay("sc", e, sems)

        @block.tensor
        def _(e):
            g.P.replay("te", e, sems)

        @block.vector
        def _(e):
            g.P.replay("ve", e, sems)

        @block.gpsimd
        def _(e):
            g.P.replay("gp", e, sems)

    return nc



def dma(g, q, out, in_, waits=(), transpose=False):
    eng = {"sy": "sy", "gp": "gp", "scq": "sc"}[q]
    sem = {"sy": "sy", "gp": "gp", "scq": "scq"}[q]
    g.P.emit(eng, (lambda o, i, t: lambda e: e.dma_start(out=o, in_=i, transpose=t))(out, in_, transpose),
             waits=waits, inc=(sem, 16))
    return g.P.val(sem)


def mm(g, out, lhsT, rhs, start, stop, waits=()):
    g.P.emit("te", (lambda o, l, r, s0, s1: lambda e: e.matmul(o, l, r, start=s0, stop=s1))(
        out, lhsT, rhs, start, stop), waits=waits, inc=("mm", 1))
    return g.P.val("mm")


def act(g, out, in_, func=AF.Copy, bias=0.0, scale=1.0, accum_out=None, waits=()):
    g.P.emit("sc", (lambda o, i, f, b, s, a: lambda e: e.activation(o, i, f, bias=b, scale=s, accum_out=a))(
        out, in_, func, bias, scale, accum_out), waits=waits, inc=("sc", 1))
    return g.P.val("sc")


def ve(g, name, *args, waits=(), **kw):
    g.P.emit("ve", (lambda n, a, k: lambda e: getattr(e, n)(*a, **k))(name, args, kw),
             waits=waits, inc=("ve", 1))
    return g.P.val("ve")


def gp_op(g, name, *args, waits=(), **kw):
    g.P.emit("gp", (lambda n, a, k: lambda e: getattr(e, n)(*a, **k))(name, args, kw),
             waits=waits, inc=("gpc", 1))
    return g.P.val("gpc")


def tr_mm(g, out_psum, in_sbuf, ident, waits=()):
    k = in_sbuf.shape[0]
    b = in_sbuf.base_partition()
    return mm(g, out_psum, in_sbuf, ident[b:b + k, b:b + k], True, True, waits=waits)



def emit_program(g):
    emit_consts(g)
    emit_x_path(g)
    emit_small_weights(g)
    emit_weprep(g)
    emit_mla(g)
    if g.stage == "ckv":
        return
    if g.stage == "xy":
        emit_dbg_tiles(g, g.dbg, lambda i: g.x_tok[:, 1024 * i:1024 * i + 1024], g.r_xtok)
        return
    emit_postln1(g)
    if g.stage == "x1":
        emit_dbg_tiles(g, g.dbg, lambda i: g.x_tok[:, 1024 * i:1024 * i + 1024], g.r_xtok)
        return
    emit_ffn_attn(g)
    if g.stage == "hid":
        emit_dbg_tiles(g, g.dbg, lambda i: g.fmA[:, 1024 * i:1024 * i + 1024], g.r_fmA)
        return
    emit_router(g)
    if g.stage == "route":
        emit_kept(g)
        emit_dbg_route(g)
        return
    emit_experts(g)
    if g.stage == "full2":
        for i in range(NT):
            dma(g, "sy", g.dbg6[128 * i:128 * i + 128, :], g.d_srto[128 * i:128 * i + 128, :],
                waits=(("scq", g.w_eo),) if i == 0 else ())
    emit_tail(g)
    if g.stage == "full2":
        emit_dbg_route(g)


def emit_dbg_tiles(g, dst, src_fn, res):
    w = 0
    for i in range(NT):
        w = dma(g, "sy", dst[128 * i:128 * i + 128, :], src_fn(i),
                waits=res.for_read() if i == 0 else ())
    g.P.emit("ve", (lambda: lambda e: e.memset(g.stat[:, 15:16], 0.0))(),
             waits=(("sy", w),), inc=("ve", 1))


def emit_dbg_route(g):
    v = ve(g, "tensor_copy", out=g.lg[:, 0:16], in_=g.route[:, 0:16])
    v = ve(g, "tensor_copy", out=g.lg[:, 16:32], in_=g.rp[:, 0:16])
    v = ve(g, "tensor_copy", out=g.lg[:, 32:48], in_=g.kept[:, 0:16])
    v = ve(g, "tensor_copy", out=g.lg[:, 48:64], in_=g.posl[:, 0:16])
    v = ve(g, "tensor_copy", out=g.lg[:, 64:80], in_=g.dest32[:, 0:16])
    w = dma(g, "sy", g.dbg, g.lg[:, 0:128], waits=(("ve", v),))
    g.P.emit("ve", (lambda: lambda e: e.memset(g.stat[:, 15:16], 0.0))(),
             waits=(("sy", w),), inc=("ve", 1))



def emit_consts(g):
    for n in g.hc:
        if n in ("maskc", "ones2048bf", "selbf"):
            continue
        dma(g, "sy", getattr(g, n)[:], getattr(g, "c_" + n))
    dma(g, "sy", g.rmask[:], g.rankmask)
    dma(g, "sy", g.bswr[:], g.bsw)
    g.w_consts = g.P.val("sy")
    ve(g, "memset", g.onescol_f[:], 1.0)
    g.v_ones = g.P.val("ve")



def emit_x_path(g):
    g.w_x = []
    for i in range(NT):
        g.w_x.append(dma(g, "scq", g.x_tok[:, 1024 * i:1024 * i + 1024],
                         g.x[128 * i:128 * i + 128, :]))
    g.r_xtok.wrote("scq", g.w_x[-1])
    wc = dma(g, "gp", g.d_xbf, g.x)
    for c in range(8):
        w = dma(g, "sy", g.fmA[:, 2048 * c:2048 * c + 2048], g.d_xbf[:, 128 * c:128 * c + 128],
                transpose=True, waits=(("gp", wc),) if c == 0 else ())
    g.w_xT = g.P.val("sy")
    g.r_fmA.wrote("sy", g.w_xT)



def emit_small_weights(g):
    P = g.P
    dma(g, "gp", g.d_smallbf[0:128, :], g.wdq)
    dma(g, "gp", g.d_smallbf[128:256, :], g.wdkv)
    wuq_flat = g.d_smallbf[256:384, :].rearrange("p f -> (p f)").rearrange("(a b) -> a b", a=1024)
    dma(g, "gp", wuq_flat, g.wuq)
    g.w_cast_small = g.P.val("gp")
    dma(g, "gp", g.wuk_s.rearrange("p (a b) -> p a b", a=8),
        g.wuk.rearrange("(a p) b -> p a b", p=128))
    dma(g, "gp", g.wuv_s.rearrange("p (a b) -> p a b", a=8),
        g.wuv.rearrange("(a p) b -> p a b", p=128))
    g.w_uvk = P.val("gp")
    dma(g, "gp", g.d_wobf, g.wo)
    g.w_cast_wo = P.val("gp")
    dma(g, "gp", g.d_wqkvbf[0:D, :], g.wq)
    dma(g, "gp", g.d_wqkvbf[D:2 * D, :], g.wk)
    dma(g, "gp", g.d_wqkvbf[2 * D:3 * D, :], g.wv)
    dma(g, "gp", g.d_webf, g.we)
    g.w_cast = P.val("gp")
    for c in range(8):
        dma(g, "sy", g.wsmall[:, 128 * c:128 * c + 128],
            g.d_smallbf[0:128, 128 * c:128 * c + 128], transpose=True,
            waits=(("gp", g.w_cast_small),) if c == 0 else ())
    for c in range(8):
        dma(g, "sy", g.wsmall[:, 1024 + 128 * c:1024 + 128 * c + 128],
            g.d_smallbf[128:256, 128 * c:128 * c + 128], transpose=True)
    for c in range(8):
        dma(g, "sy", g.wsmall[:, 2048 + 128 * c:2048 + 128 * c + 128],
            wuq_flat[128 * c:128 * c + 128, 0:128], transpose=True)
    g.w_small = P.val("sy")
    w1 = dma(g, "sy", g.wtmp.rearrange("p (a b) -> p a b", a=8),
             wuq_flat.rearrange("(a p) b -> p a b", p=128))
    w2 = dma(g, "sy", g.wtmp2[:, 0:1024], g.d_smallbf[0:128, :], waits=(("gp", g.w_cast_small),))
    g.r_wtmp.wrote("sy", w1)
    g.r_wtmp2.wrote("sy", w2)

    for c in range(8):
        m = mm(g, g.p_m1[:, 0:128], g.wtmp[:, 128 * c:128 * c + 128],
               g.wuk_s[:, 128 * c:128 * c + 128], c == 0, c == 7,
               waits=g.r_wtmp.for_read() + [("gp", g.w_uvk)] + g.r_pm1.for_write() if c == 0 else ())
    g.r_pm1.wrote("mm", m)
    g.r_wtmp.read("mm", m)
    v = ve(g, "tensor_copy", out=g.u65[:, 0:128], in_=g.p_m1[:, 0:128],
           waits=g.r_pm1.for_read() + g.r_u65.for_write())
    g.r_pm1.read("ve", v)
    g.r_u65.wrote("ve", v)
    for c in range(8):
        m = mm(g, g.p_m2[:, 0:128], g.wtmp2[:, 128 * c:128 * c + 128], g.u65[:, 0:128],
               True, True,
               waits=(g.r_wtmp2.for_read() + g.r_u65.for_read() + g.r_pm2.for_write()) if c == 0 else g.r_pm2.for_write())
        g.r_pm2.wrote("mm", m)
        v = ve(g, "tensor_copy", out=g.keff[:, 128 * c:128 * c + 128], in_=g.p_m2[:, 0:128],
               waits=g.r_pm2.for_read())
        g.r_pm2.read("ve", v)
    g.r_wtmp2.read("mm", m)
    g.r_u65.read("mm", m)
    g.v_keff = v

    for c in range(8):
        w = dma(g, "sy", g.wtmp[:, 0:1024], g.d_wobf[0:1024, 128 * c:128 * c + 128],
                transpose=True, waits=g.r_wtmp.for_write() + ([("gp", g.w_cast_wo)] if c == 0 else []))
        g.r_wtmp.wrote("sy", w)
        for nk in range(2):
            pdst = g.p_m1 if nk == 0 else g.p_m2
            rres = g.r_pm1 if nk == 0 else g.r_pm2
            m = mm(g, pdst[:, 0:512], g.wuv_s[:, 128 * c:128 * c + 128],
                   g.wtmp[:, 512 * nk:512 * nk + 512], c == 0, c == 7,
                   waits=([("sy", w)] + rres.for_write()) if c == 0 else [("sy", w)])
            if c == 0:
                rres.wrote("mm", m)
        g.r_wtmp.read("mm", m)
    g.r_pm1.wrote("mm", m)
    g.r_pm2.wrote("mm", m)
    for nk in range(2):
        pdst = g.p_m1 if nk == 0 else g.p_m2
        rres = g.r_pm1 if nk == 0 else g.r_pm2
        v = ve(g, "tensor_copy", out=g.veff[:, 512 * nk:512 * nk + 512], in_=pdst[:, 0:512],
               waits=rres.for_read())
        rres.read("ve", v)
    g.v_veff = v

    w = dma(g, "gp", g.stage_b[0:8, 0:1024], g.wsw, waits=g.r_stageb.for_write())
    g.r_stageb.wrote("gp", w)
    for c in range(8):
        m = tr_mm(g, g.p_m2[:, 0:8], g.stage_b[0:8, 128 * c:128 * c + 128], g.identbf,
                  waits=g.r_stageb.for_read() + g.r_pm2.for_write())
        g.r_pm2.wrote("mm", m)
        v = ve(g, "tensor_copy", out=g.wswT[:, 8 * c:8 * c + 8], in_=g.p_m2[:, 0:8],
               waits=g.r_pm2.for_read())
        g.r_pm2.read("ve", v)
    g.r_stageb.read("mm", m)
    g.v_wsw = v

    for pi, b in enumerate([g.bq, g.bk, g.bv]):
        for mt in range(8):
            dma(g, "sy", g.bqkv[:, 8 * pi + mt:8 * pi + mt + 1],
                b[0:1, 128 * mt:128 * mt + 128].rearrange("a p -> (a p)").unsqueeze(1))
    g.w_bqkv = P.val("sy")
    m = mm(g, g.p_m2[:, 0:8], g.ones128[:], g.bswr[:], True, True,
           waits=[("sy", g.w_consts)] + g.r_pm2.for_write())
    g.r_pm2.wrote("mm", m)
    v = ve(g, "tensor_copy", out=g.bcast8[:, 0:8], in_=g.p_m2[:, 0:8], waits=g.r_pm2.for_read())
    g.r_pm2.read("ve", v)
    g.v_bsw = v



def emit_weprep(g):
    P = g.P
    v0 = ve(g, "memset", g.stage_w[:, 0:1024], 0.0, waits=g.r_stagew.for_write())
    g.r_stagew.wrote("ve", v0)
    for e in range(E):
        w = dma(g, "sy", g.d_wt[128 * e:128 * e + 128, 8192:9216], g.stage_w[:, 0:1024],
                waits=g.r_stagew.for_read() if e == 0 else ())
        g.r_stagew.read("sy", w)
    for e in range(E):
        dma(g, "gp", g.d_wt[128 * e:128 * e + 1, 8192:9216], g.be[e:e + 1, :],
            waits=(("sy", P.val("sy")),) if e == 0 else ())
    for e in range(E):
        for j in range(8):
            w1 = dma(g, "sy", g.stage_w[:, 0:1024],
                     g.d_webf[1024 * e:1024 * e + 1024, 128 * j:128 * j + 128], transpose=True,
                     waits=g.r_stagew.for_write() + ([("gp", g.w_cast)] if (e == 0 and j == 0) else []))
            g.r_stagew.wrote("sy", w1)
            w2 = dma(g, "sy", g.d_wt[128 * e:128 * e + 128, 1024 * j:1024 * j + 1024],
                     g.stage_w[:, 0:1024])
            g.r_stagew.read("sy", w2)
    g.w_wt = P.val("sy")



def emit_mla(g):
    P = g.P
    xT = lambda c: g.fmA[:, 2048 * c:2048 * c + 2048]
    dump_ckv = (g.stage == "ckv")
    if dump_ckv:
        dma(g, "sy", g.dbg5, g.fmA[:, 0:16384], waits=(("sy", g.w_xT),))

    for nk in range(4):
        for c in range(8):
            m = mm(g, g.p_sc[:, 512 * nk:512 * nk + 512],
                   g.wsmall[:, 1024 + 128 * c:1024 + 128 * c + 128],
                   xT(c)[:, 512 * nk:512 * nk + 512], c == 0, c == 7,
                   waits=([("sy", g.w_xT), ("sy", g.w_small)] + g.r_psc.for_write()) if (nk == 0 and c == 0) else ())
        s = act(g, g.ckvT[:, 512 * nk:512 * nk + 512], g.p_sc[:, 512 * nk:512 * nk + 512],
                AF.Copy, waits=(("mm", m),))
        g.r_psc.read("sc", s)
    g.r_psc.wrote("mm", m)
    s_ckv = s
    for nk in range(4):
        for c in range(8):
            m = mm(g, g.p_ctx[:, 0:512], g.wsmall[:, 128 * c:128 * c + 128],
                   xT(c)[:, 512 * nk:512 * nk + 512], c == 0, c == 7,
                   waits=g.r_pctx.for_write() if c == 0 else ())
        g.r_pctx.wrote("mm", m)
        s = act(g, g.q2T[:, 512 * nk:512 * nk + 512], g.p_ctx[:, 0:512], AF.Copy,
                waits=g.r_pctx.for_read())
        g.r_pctx.read("sc", s)
    s_q2 = s
    g.r_fmA.read("mm", m)
    for si in range(16):
        m = tr_mm(g, g.p_m1[:, 0:128], g.ckvT[:, 128 * si:128 * si + 128], g.identbf,
                  waits=([("sc", s_ckv)] if si == 0 else []) + g.r_pm1.for_write())
        g.r_pm1.wrote("mm", m)
        s = act(g, g.ckvS[:, 128 * si:128 * si + 128], g.p_m1[:, 0:128], AF.Copy,
                waits=g.r_pm1.for_read())
        g.r_pm1.read("sc", s)
    s_ckvs = s
    for mt in range(8):
        for nk in range(4):
            m = mm(g, g.p_m2[:, 0:512], g.wsmall[:, 2048 + 128 * mt:2048 + 128 * mt + 128],
                   g.q2T[:, 512 * nk:512 * nk + 512], True, True,
                   waits=([("sc", s_q2)] if (mt == 0 and nk == 0) else []) + g.r_pm2.for_write()
                   + (g.r_fmA.for_write() if (mt == 0 and nk == 0) else []))
            g.r_pm2.wrote("mm", m)
            s = act(g, g.fmA[:, 2048 * mt + 512 * nk:2048 * mt + 512 * nk + 512],
                    g.p_m2[:, 0:512], AF.Copy, waits=g.r_pm2.for_read())
            g.r_pm2.read("sc", s)
    g.r_fmA.wrote("sc", s)
    s_qT = s
    if dump_ckv:
        w = dma(g, "sy", g.dbg, g.ckvT, waits=(("sc", s_ckv), ("ve", g.v_keff), ("ve", g.v_veff)))
        w = dma(g, "sy", g.dbg2, g.fmA[:, 0:16384], waits=(("sc", s_qT),))
        w = dma(g, "sy", g.dbg3, g.keff[:, 0:1024])
        w = dma(g, "sy", g.dbg4, g.veff[:, 0:1024])
        g.P.emit("ve", (lambda: lambda e: e.memset(g.stat[:, 15:16], 0.0))(),
                 waits=(("sy", w),), inc=("ve", 1))
        return
    g.w_maskc_q = dma(g, "scq", g.maskc, g.c_maskc, waits=(("mm", P.val("mm")),))

    gctr_box = [0]

    def ql_res(h):
        return g.r_qlb if h % 2 == 0 else g.r_qlb2

    def ql_ap(h):
        return g.qlb[:, 512 * (h % 2):512 * (h % 2) + 512]

    def emit_ql(tc, h, first=False):
        hb = 64 * (h % 2)
        lhsT_ql = g.keff[hb:hb + 64, 128 * (h // 2):128 * (h // 2) + 128]
        rhs_q = g.fmA[hb:hb + 64, 2048 * (h // 2) + 512 * tc:2048 * (h // 2) + 512 * tc + 512]
        m = mm(g, g.p_m2[:, 0:512], lhsT_ql, rhs_q, True, True,
               waits=([("ve", g.v_keff), ("sc", s_qT)] if first else [])
               + g.r_pm2.for_write() + (g.r_fmA.for_read() if first else []))
        g.r_pm2.wrote("mm", m)
        r = ql_res(h)
        v = ve(g, "tensor_copy", out=ql_ap(h), in_=g.p_m2[:, 0:512],
               waits=g.r_pm2.for_read() + r.for_write())
        g.r_pm2.read("ve", v)
        r.wrote("ve", v)

    def emit_scores(tc, h, gi):
        reg = gctr_box[0] % 2
        psc = g.p_sc[:, 1024 * reg:1024 * reg + 1024]
        rsc = g.r_pscs[reg]
        abres = g.abres[gctr_box[0] % 4]
        ab = g.attb[:, 1024 * (gctr_box[0] % 4):1024 * (gctr_box[0] % 4) + 1024]
        gctr_box[0] += 1
        r = ql_res(h)
        for j in range(2):
            si = 2 * gi + j
            m = mm(g, psc[:, 512 * j:512 * j + 512],
                   g.ckvT[:, 128 * si:128 * si + 128], ql_ap(h), True, True,
                   waits=(r.for_read() + rsc.for_write()) if j == 0 else ())
        rsc.wrote("mm", m)
        r.read("mm", m)
        s = act(g, ab, psc, AF.Exp, scale=0.125,
                waits=rsc.for_read() + abres.for_write())
        rsc.read("sc", s)
        abres.wrote("sc", s)
        if gi >= 2 * tc:
            v = ve(g, "tensor_mul", out=ab, in0=ab,
                   in1=g.maskc[:, 1024 * (gi - 2 * tc):1024 * (gi - 2 * tc) + 1024],
                   waits=abres.for_read() + [("scq", g.w_maskc_q)])
            abres.wrote("ve", v)
        return (tc, h, gi, ab, abres)

    def emit_ctxden(p):
        tc, h, gi, ab, abres = p
        ngi = 2 * (tc + 1)
        for j in range(2):
            si = 2 * gi + j
            wts = []
            wts2 = []
            if j == 0:
                wts += abres.for_read()
                if gi == 0:
                    wts += [("sc", s_ckvs)]
                    wts += g.r_pctx.for_write()
                if h == 0 and gi == 0:
                    wts2 += g.r_pden.for_write()
            m1 = mm(g, g.p_ctx[:, 0:512], g.ckvS[:, 128 * si:128 * si + 128],
                    ab[:, 512 * j:512 * j + 512], (gi == 0 and j == 0),
                    (gi == ngi - 1 and j == 1), waits=wts)
            m2 = mm(g, g.p_den[0:16, 0:512], g.oh16bf[:, 16 * h:16 * h + 16],
                    ab[:, 512 * j:512 * j + 512], (h == 0 and gi == 0 and j == 0),
                    (h == NH - 1 and gi == ngi - 1 and j == 1), waits=wts2)
        abres.read("mm", m2)
        if gi == ngi - 1:
            g.r_pctx.wrote("mm", m1)
            v = ve(g, "tensor_copy", out=g.ctxall[:, 512 * h:512 * h + 512],
                   in_=g.p_ctx[:, 0:512],
                   waits=g.r_pctx.for_read() + (g.r_ctxall.for_write() if h == 0 else []))
            g.r_pctx.read("ve", v)
        if h == NH - 1 and gi == ngi - 1:
            g.r_pden.wrote("mm", m2)

    def emit_epilogue(tc):
        v = ve(g, "reciprocal", out=g.r16[0:16, 0:512], in_=g.p_den[0:16, 0:512],
               waits=g.r_pden.for_read() + g.r_r16.for_write())
        g.r_pden.read("ve", v)
        g.r_r16.wrote("ve", v)
        for sub in range(4):
            trp, rt = (g.p_m1, g.r_pm1) if sub % 2 == 0 else (g.p_den, g.r_pden)
            m = tr_mm(g, trp[:, 0:16], g.r16[0:16, 128 * sub:128 * sub + 128], g.identf,
                      waits=g.r_r16.for_read() + rt.for_write())
            rt.wrote("mm", m)
            v = ve(g, "tensor_copy", out=g.rtok[:, 16 * sub:16 * sub + 16], in_=trp[:, 0:16],
                   waits=rt.for_read() + (g.r_rtok.for_write() if sub == 0 else []))
            rt.read("ve", v)
        g.r_r16.read("mm", m)
        g.r_rtok.wrote("ve", v)
        vv = v
        for pr in range(8):
            for hh in range(2):
                h = 2 * pr + hh
                m = mm(g, g.p_m2[64 * hh:64 * hh + 64, 0:512], g.veff[:, 64 * h:64 * h + 64],
                       g.ctxall[:, 512 * h:512 * h + 512], True, True,
                       waits=(g.r_ctxall.for_read() + [("ve", g.v_veff)] + g.r_pm2.for_write()) if hh == 0 else ())
            g.r_pm2.wrote("mm", m)
            v = ve(g, "tensor_copy", out=g.u65[:, 0:512], in_=g.p_m2[:, 0:512],
                   waits=g.r_pm2.for_read() + g.r_u65.for_write())
            g.r_pm2.read("ve", v)
            g.r_u65.wrote("ve", v)
            for sub in range(4):
                trp, rt = (g.p_m1, g.r_pm1) if sub % 2 == 0 else (g.p_den, g.r_pden)
                m = tr_mm(g, trp[:, 0:128], g.u65[:, 128 * sub:128 * sub + 128], g.identbf,
                          waits=g.r_u65.for_read() + rt.for_write())
                rt.wrote("mm", m)
                ti = 4 * tc + sub
                for hh in range(2):
                    h = 2 * pr + hh
                    xs = g.x_tok[:, 1024 * ti + 64 * h:1024 * ti + 64 * h + 64]
                    g.P.emit("ve", (lambda o, i0, s1, i1, pm: lambda e: e.scalar_tensor_tensor(
                        out=o, in0=i0, scalar=s1, in1=i1, op0=ALU.mult, op1=ALU.add))(
                        xs, trp[0:128, 64 * hh:64 * hh + 64],
                        g.rtok[:, 16 * sub + h:16 * sub + h + 1], xs, trp),
                        waits=tuple(rt.for_read() + (g.r_rtok.for_read() if hh == 0 else [])
                                    + ([("scq", g.w_x[ti])] if (tc == 0 and pr == 0) else [])),
                        inc=("ve", 1))
                vv = g.P.val("ve")
                rt.read("ve", vv)
                g.r_u65.read("mm", m)
        g.r_rtok.read("ve", vv)
        g.r_ctxall.read("mm", g.P.val("mm"))
        if g.stage not in ("xy", "ckv"):
            for sub in range(4):
                ti = 4 * tc + sub
                xs_t = g.x_tok[:, 1024 * ti:1024 * ti + 1024]
                last = emit_ln_once(g, xs_t, xs_t, None, affine=False,
                                    waits0=((("ve", vv),) if sub == 0 else ()))
                w = dma(g, "sy", g.d_x1f[128 * ti:128 * ti + 128, :], xs_t, waits=[last])
                g.r_xtok.read("sy", w)
                wb = dma(g, "gp", g.d_x1bf[128 * ti:128 * ti + 128, :], xs_t, waits=[last])
                g.r_xtok.read("gp", wb)
            g.w_x1spill = w
            g.w_x1bf = wb
        return g.P.val("ve")

    all_items = [(tc, h, gi) for tc in range(NCK) for h in range(NH)
                 for gi in range(2 * (tc + 1))]
    emit_ql(0, 0, first=True)
    pq = []
    due_epi = None

    def flush_one():
        nonlocal due_epi
        if due_epi is not None:
            emit_epilogue(due_epi)
            due_epi = None
        p = pq.pop(0)
        emit_ctxden(p)
        ptc, ph, pgi = p[0], p[1], p[2]
        if ph == NH - 1 and pgi == 2 * (ptc + 1) - 1:
            due_epi = ptc

    for tc, h, gi in all_items:
        pq.append(emit_scores(tc, h, gi))
        if gi == 0:
            if h + 1 < NH:
                emit_ql(tc, h + 1)
            elif tc + 1 < NCK:
                emit_ql(tc + 1, 0)
        if len(pq) > 2:
            flush_one()
    while pq:
        flush_one()
    vv_last = emit_epilogue(NCK - 1)
    g.r_fmA.read("mm", g.P.val("mm"))
    g.r_xtok.wrote("ve", vv_last)
    g.v_mla = vv_last



def emit_ln_once(g, src, dst, sq_scratch, affine, waits0=(), so=0):
    st = g.stat[:, so:so + 16]
    ve(g, "bn_stats", out=st[:, 0:6], in_=src[:, 0:512], waits=waits0)
    ve(g, "bn_stats", out=st[:, 6:12], in_=src[:, 512:1024], waits=waits0)
    ve(g, "bn_aggr", out=st[:, 12:14], in_=st[:, 0:12])
    v5 = ve(g, "tensor_scalar", out=st[:, 14:15], in0=st[:, 13:14], scalar1=EPS,
            scalar2=None, op0=ALU.add)
    s2 = act(g, st[:, 15:16], st[:, 14:15], AF.Sqrt, waits=(("ve", v5),))
    v6 = ve(g, "reciprocal", out=st[:, 0:1], in_=st[:, 15:16], waits=(("sc", s2),))
    v7 = ve(g, "tensor_mul", out=st[:, 1:2], in0=st[:, 12:13], in1=st[:, 0:1])
    v8 = ve(g, "tensor_scalar", out=st[:, 2:3], in0=st[:, 1:2], scalar1=-1.0,
            scalar2=None, op0=ALU.mult)
    s3 = ve(g, "tensor_scalar", out=dst, in0=src, scalar1=st[:, 0:1],
            scalar2=st[:, 2:3], op0=ALU.mult, op1=ALU.add)
    if affine:
        v9 = ve(g, "tensor_mul", out=dst, in0=dst, in1=g.gbc[:, 0:1024],
                waits=(("ve", g.v_gbc),))
        v9 = ve(g, "tensor_add", out=dst, in0=dst, in1=g.gbc[:, 1024:2048])
        return ("ve", v9)
    return ("ve", s3)


def emit_postln1(g):
    P = g.P
    w2 = 0
    for c in range(8):
        w2 = dma(g, "sy", g.fmA[:, 2048 * c:2048 * c + 2048], g.d_x1bf[:, 128 * c:128 * c + 128],
                 transpose=True, waits=([("gp", g.w_x1bf)] + g.r_fmA.for_write() + g.r_fmA.for_read()) if c == 0 else ())
    g.r_fmA.wrote("sy", w2)
    g.w_x1T = w2



def emit_ffn_attn(g):
    P = g.P
    g.w_ffnc_q = dma(g, "scq", g.ones2048bf, g.c_ones2048bf, waits=(("mm", P.val("mm")),))
    g.w_ffnc_q = dma(g, "scq", g.selbf, g.c_selbf)
    x1T = lambda c: g.fmA[:, 2048 * c:2048 * c + 2048]
    v_ones = ve(g, "memset",
                g.vs65[:, :].rearrange("p (a b) -> p a b", b=65)[:, :, 64:65], 1.0,
                waits=g.r_vs65.for_write())
    g.r_vs65.wrote("ve", v_ones)

    for proj in range(3):
        wsrc = g.d_wqkvbf[1024 * proj:1024 * proj + 1024, :]
        for kc in range(8):
            w = dma(g, "sy", g.wqkvT[:, 1024 * kc:1024 * kc + 1024],
                    wsrc[0:1024, 128 * kc:128 * kc + 128], transpose=True,
                    waits=(g.r_wqkvT.for_write() + [("gp", g.w_cast)]) if kc == 0 else ())
        g.r_wqkvT.wrote("sy", w)
        for mt in range(8):
            for nk in range(4):
                pp, rr = (g.p_m2, g.r_pm2) if ((mt * 4 + nk) % 2 == 0) else (g.p_den, g.r_pden)
                m = mm(g, pp[:, 0:512], g.wqkvT[:, 128 * mt:128 * mt + 128],
                       x1T(0)[:, 512 * nk:512 * nk + 512], True, False,
                       waits=(g.r_wqkvT.for_read() + [("sy", g.w_x1T), ("sy", g.w_bqkv)]
                              + rr.for_write()) if (mt == 0 and nk == 0) else rr.for_write())
                for kc in range(1, 8):
                    m = mm(g, pp[:, 0:512], g.wqkvT[:, 1024 * kc + 128 * mt:1024 * kc + 128 * mt + 128],
                           x1T(kc)[:, 512 * nk:512 * nk + 512], False, kc == 7)
                rr.wrote("mm", m)
                if proj < 2:
                    dst = g.fmB[:, 16384 * proj + 2048 * mt + 512 * nk:
                                16384 * proj + 2048 * mt + 512 * nk + 512]
                    s = ve(g, "tensor_scalar", out=dst, in0=pp[:, 0:512],
                           scalar1=g.bqkv[:, 8 * proj + mt:8 * proj + mt + 1],
                           scalar2=None, op0=ALU.add,
                           waits=rr.for_read() + (g.r_fmB.for_write() if (proj == 0 and mt == 0 and nk == 0) else []))
                    rr.read("ve", s)
                else:
                    abslot = (mt * 4 + nk) % 4
                    ares = g.abres[abslot]
                    abv = g.attb[:, 1024 * abslot:1024 * abslot + 512]
                    s = ve(g, "tensor_scalar", out=abv, in0=pp[:, 0:512],
                           scalar1=g.bqkv[:, 16 + mt:16 + mt + 1],
                           scalar2=None, op0=ALU.add,
                           waits=rr.for_read() + ares.for_write())
                    rr.read("ve", s)
                    ares.wrote("ve", s)
                    for sub in range(4):
                        si = 4 * nk + sub
                        for hh in range(2):
                            h = 2 * mt + hh
                            pt, rt = (g.p_m1, g.r_pm1) if ((sub * 2 + hh) % 2 == 0) else (g.p_ctx, g.r_pctx)
                            m2 = tr_mm(g, pt[:, 0:64],
                                       abv[64 * hh:64 * hh + 64, 128 * sub:128 * sub + 128],
                                       g.identbf,
                                       waits=(ares.for_read() if (sub == 0 and hh == 0) else []) + rt.for_write())
                            rt.wrote("mm", m2)
                            ares.read("mm", m2)
                            v = ve(g, "tensor_copy",
                                   out=g.vs65[:, 65 * (16 * h + si):65 * (16 * h + si) + 64],
                                   in_=pt[:, 0:64], waits=rt.for_read())
                            rt.read("ve", v)
        g.r_wqkvT.read("mm", g.P.val("mm"))
    g.r_fmB.wrote("sc", g.P.val("sc"))
    g.r_vs65.wrote("ve", g.P.val("ve"))
    s_qk = g.P.val("sc")

    for h in range(NH):
        for si in range(16):
            m = mm(g, g.p_m2[0:65, si:si + 1], g.vs65[:, 65 * (16 * h + si):65 * (16 * h + si) + 65],
                   g.onescolbf[:], True, True,
                   waits=(g.r_vs65.for_read() + [("ve", g.v_ones), ("sy", g.w_consts)]
                          + g.r_pm2.for_write()) if si == 0 else ())
        g.r_pm2.wrote("mm", m)
        v = ve(g, "tensor_copy", out=g.sv_s[0:65, 0:16],
               in_=g.p_m2[0:65, 0:16], waits=g.r_pm2.for_read() + g.r_svs.for_write())
        g.r_pm2.read("ve", v)
        g.r_svs.wrote("ve", v)
        s = act(g, g.stage_b[0:65, 0:16], g.sv_s[0:65, 0:16], AF.Copy,
                waits=g.r_svs.for_read() + g.r_stageb.for_write())
        g.r_svs.read("sc", s)
        g.r_stageb.wrote("sc", s)
        m = tr_mm(g, g.p_m1[0:16, 0:65], g.stage_b[0:65, 0:16], g.identbf,
                  waits=g.r_stageb.for_read() + g.r_pm1.for_write())
        g.r_stageb.read("mm", m)
        g.r_pm1.wrote("mm", m)
        v = ve(g, "tensor_copy", out=g.svT[0:16, 65 * h:65 * h + 65], in_=g.p_m1[0:16, 0:65],
               waits=g.r_pm1.for_read() + (g.r_sv.for_write() if h == 0 else []))
        g.r_pm1.read("ve", v)
    g.r_sv.wrote("ve", v)
    v_sv = v

    qTs = lambda h, t0: g.fmB[64 * (h % 2):64 * (h % 2) + 64,
                              2048 * (h // 2) + t0:2048 * (h // 2) + t0 + 512]
    kTs = lambda h, si: g.fmB[64 * (h % 2):64 * (h % 2) + 64,
                              16384 + 2048 * (h // 2) + 128 * si:16384 + 2048 * (h // 2) + 128 * si + 128]
    gctr_box = [0]

    def emit_scores(tc, h, gi):
        t0 = 512 * tc
        reg = gctr_box[0] % 2
        psc = g.p_sc[:, 1024 * reg:1024 * reg + 1024]
        rsc = g.r_pscs[reg]
        abres = g.abres[gctr_box[0] % 4]
        ab = g.attb[:, 1024 * (gctr_box[0] % 4):1024 * (gctr_box[0] % 4) + 1024]
        gctr_box[0] += 1
        for j in range(2):
            si = 2 * gi + j
            m = mm(g, psc[:, 512 * j:512 * j + 512], kTs(h, si), qTs(h, t0),
                   True, True,
                   waits=(rsc.for_write() + (g.r_fmB.for_read() if (tc == 0 and h == 0 and gi == 0) else [])
                          + ([("ve", v_sv)] if (tc == 0 and h == 0 and gi == 0) else [])) if j == 0 else ())
        rsc.wrote("mm", m)
        s = act(g, ab, psc, AF.Exp, scale=1.0 / 32.0,
                waits=rsc.for_read() + abres.for_write())
        rsc.read("sc", s)
        abres.wrote("sc", s)
        if gi >= 2 * tc:
            mk = g.maskc[:, 1024 * (gi - 2 * tc):1024 * (gi - 2 * tc) + 1024]
            v = ve(g, "select", out=ab, mask=mk.bitcast(mybir.dt.uint16), on_true=ab,
                   on_false=g.ones2048bf[:, 0:1024],
                   waits=abres.for_read() + [("scq", g.w_ffnc_q)])
            abres.wrote("ve", v)
        return (tc, h, gi, ab, abres)

    def emit_ctx(p):
        tc, h, gi, ab, abres = p
        ngi = 2 * (tc + 1)
        pc, rc = (g.p_ctx, g.r_pctx) if h % 2 == 0 else (g.p_m2, g.r_pm2)
        for j in range(2):
            si = 2 * gi + j
            wts = list(abres.for_read()) if j == 0 else []
            if gi == 0 and j == 0:
                wts += rc.for_write()
            m1 = mm(g, pc[0:65, 0:512],
                    g.vs65[:, 65 * (16 * h + si):65 * (16 * h + si) + 65],
                    ab[:, 512 * j:512 * j + 512], (gi == 0 and j == 0), False, waits=wts)
        abres.read("mm", m1)
        if gi != ngi - 1:
            return
        m1 = mm(g, pc[0:65, 0:512], g.svT[0:16, 65 * h:65 * h + 65],
                g.selbf[:, 512 * tc:512 * tc + 512], False, True,
                waits=g.r_sv.for_read() + [("scq", g.w_ffnc_q)])
        rc.wrote("mm", m1)
        g.r_sv.read("mm", m1)
        stg, rs = (g.den_s, g.r_dens) if h % 2 == 0 else (g.r16, g.r_r16)
        v_u = ve(g, "tensor_copy", out=stg[0:65, 0:512], in_=pc[0:65, 0:512],
                 waits=rc.for_read() + rs.for_write())
        rs.wrote("ve", v_u)
        rc.read("ve", v_u)
        return (tc, h, stg, rs)

    def emit_ctx_tail(p):
        tc, h, stg, rs = p
        for sub in range(4):
            trp, rt = (g.p_m1, g.r_pm1) if sub % 2 == 0 else (g.p_den, g.r_pden)
            m2 = tr_mm(g, trp[:, 0:65], stg[0:65, 128 * sub:128 * sub + 128],
                       g.identf, waits=rs.for_read() + rt.for_write())
            rt.wrote("mm", m2)
            rs.read("mm", m2)
            v = ve(g, "tensor_copy",
                   out=g.ctxall[:, 512 * h + 64 * sub:512 * h + 64 * sub + 64],
                   in_=trp[:, 0:64],
                   waits=rt.for_read() + (g.r_ctxall.for_write() if (h == 0 and sub == 0) else []))
            v2 = ve(g, "tensor_copy", out=g.rtok[:, 16 * sub + h:16 * sub + h + 1],
                    in_=trp[:, 64:65],
                    waits=(g.r_rtok.for_write() if (h == 0 and sub == 0) else ()))
            rt.read("ve", v2)
        if h == NH - 1:
            g.r_rtok.wrote("ve", v2)

    def emit_fepi(tc):
        v = ve(g, "reciprocal", out=g.rtok[:, 0:64], in_=g.rtok[:, 0:64])
        g.r_rtok.wrote("ve", v)
        for h in range(NH):
            for sub in range(4):
                ti = 4 * tc + sub
                g.P.emit("ve", (lambda o, i0, s1: lambda e: e.tensor_scalar(
                    out=o, in0=i0, scalar1=s1, scalar2=None, op0=ALU.mult))(
                    g.fmA[:, 1024 * ti + 64 * h:1024 * ti + 64 * h + 64],
                    g.ctxall[:, 512 * h + 64 * sub:512 * h + 64 * sub + 64],
                    g.rtok[:, 16 * sub + h:16 * sub + h + 1]),
                    waits=tuple(g.r_rtok.for_read() + g.r_ctxall.for_read()
                                + (g.r_fmA.for_write() if (tc == 0 and h == 0 and sub == 0) else [])),
                    inc=("ve", 1))
        vv = g.P.val("ve")
        g.r_ctxall.read("ve", vv)
        g.r_rtok.read("ve", vv)
        for sub in range(4):
            ti = 4 * tc + sub
            wh = dma(g, "sy", g.d_hidbf[128 * ti:128 * ti + 128, :],
                     g.fmA[:, 1024 * ti:1024 * ti + 1024], waits=(("ve", vv),) if sub == 0 else ())
            g.r_fmA.read("sy", wh)
        g.w_hidst = wh
        return vv

    all_items = [(tc, h, gi) for tc in range(NCK) for h in range(NH)
                 for gi in range(2 * (tc + 1))]
    pq = []
    due_epi = None
    due_tail = None

    def flush_one():
        nonlocal due_tail, due_epi
        if due_tail is not None:
            emit_ctx_tail(due_tail)
            due_tail = None
        if due_epi is not None:
            emit_fepi(due_epi)
            due_epi = None
        p = pq.pop(0)
        t = emit_ctx(p)
        if t is not None:
            due_tail = t
        ptc, ph, pgi = p[0], p[1], p[2]
        if ph == NH - 1 and pgi == 2 * (ptc + 1) - 1:
            due_epi = ptc

    for tc, h, gi in all_items:
        pq.append(emit_scores(tc, h, gi))
        if len(pq) > 2:
            flush_one()
    while pq:
        flush_one()
    if due_tail is not None:
        emit_ctx_tail(due_tail)
    vv = emit_fepi(NCK - 1)
    g.r_fmB.read("mm", g.P.val("mm"))
    g.r_vs65.read("mm", g.P.val("mm"))
    g.r_fmA.wrote("ve", vv)
    g.v_hid = vv



def emit_gbc(g):
    w = dma(g, "sy", g.stage_f[0:1, 0:1024], g.gamma, waits=g.r_stagef.for_write())
    w = dma(g, "sy", g.stage_f[0:1, 1024:2048], g.beta)
    g.r_stagef.wrote("sy", w)
    m = 0
    for r in range(2):
        for nk in range(2):
            src = g.stage_f[0:1, 1024 * r + 512 * nk:1024 * r + 512 * nk + 512]
            m = mm(g, g.p_m1[:, 0:512], g.ones128[:], src,
                   True, True, waits=g.r_stagef.for_read() + g.r_pm1.for_write()
                   + (g.r_mlakv.for_write() if (r == 0 and nk == 0) else []))
            g.r_pm1.wrote("mm", m)
            v = ve(g, "tensor_copy", out=g.gbc[:, 1024 * r + 512 * nk:1024 * r + 512 * nk + 512],
                   in_=g.p_m1[:, 0:512], waits=g.r_pm1.for_read())
            g.r_pm1.read("ve", v)
    g.r_stagef.read("mm", m)
    g.r_mlakv.wrote("ve", v)
    g.v_gbc = v


def emit_router(g):
    P = g.P
    emit_gbc(g)
    for c in range(8):
        w2 = dma(g, "sy", g.fmB[:, 2048 * c:2048 * c + 2048], g.d_hidbf[:, 128 * c:128 * c + 128],
                 transpose=True, waits=([("sy", g.w_hidst)] + g.r_fmB.for_write()) if c == 0 else ())
    g.r_fmB.wrote("sy", w2)
    for tc in range(NCK):
        for c in range(8):
            m = mm(g, g.p_m2[0:8, 0:512], g.wswT[:, 8 * c:8 * c + 8],
                   g.fmB[:, 2048 * c + 512 * tc:2048 * c + 512 * tc + 512], c == 0, c == 7,
                   waits=(g.r_fmB.for_read() + [("ve", g.v_wsw)] + g.r_pm2.for_write()) if c == 0 else ())
        g.r_pm2.wrote("mm", m)
        if tc == NCK - 1:
            g.r_fmB.read("mm", m)
        v = ve(g, "tensor_copy", out=g.den_s[0:8, 0:512], in_=g.p_m2[0:8, 0:512],
               waits=g.r_pm2.for_read() + g.r_dens.for_write())
        g.r_pm2.read("ve", v)
        g.r_dens.wrote("ve", v)
        for sub in range(4):
            ti = 4 * tc + sub
            m = tr_mm(g, g.p_m1[:, 0:8], g.den_s[0:8, 128 * sub:128 * sub + 128], g.identf,
                      waits=g.r_dens.for_read() + g.r_pm1.for_write())
            g.r_pm1.wrote("mm", m)
            v = ve(g, "tensor_copy", out=g.lg[:, 8 * ti:8 * ti + 8], in_=g.p_m1[:, 0:8],
                   waits=g.r_pm1.for_read())
            g.r_pm1.read("ve", v)
        g.r_dens.read("mm", m)
    for ti in range(NT):
        lg = g.lg[:, 8 * ti:8 * ti + 8]
        ve(g, "tensor_add", out=lg, in0=lg, in1=g.bcast8[:, 0:8],
           waits=(("ve", g.v_bsw),) if ti == 0 else ())
        v2 = ve(g, "reduce_max", out=g.stat[:, 0:1], in_=lg, axis=AX.X)
        vneg = ve(g, "tensor_scalar", out=g.stat[:, 1:2], in0=g.stat[:, 0:1], scalar1=-1.0,
                  scalar2=None, op0=ALU.mult)
        s = act(g, g.stage_f[:, 0:8], lg, AF.Exp, bias=g.stat[:, 1:2],
                waits=[("ve", vneg)] + g.r_stagef.for_write())
        g.r_stagef.wrote("sc", s)
        v3 = ve(g, "reduce_sum", out=g.stat[:, 2:3], in_=g.stage_f[:, 0:8], axis=AX.X,
                waits=g.r_stagef.for_read())
        g.r_stagef.read("ve", v3)
        ve(g, "reciprocal", out=g.rp[:, ti:ti + 1], in_=g.stat[:, 2:3])
        v4 = ve(g, "tensor_scalar", out=g.stage_f[:, 8:16], in0=lg, scalar1=g.stat[:, 0:1],
                scalar2=None, op0=ALU.is_ge)
        v5 = ve(g, "tensor_mul", out=g.stage_f[:, 16:24], in0=g.stage_f[:, 8:16], in1=g.iota8m[:])
        v6 = ve(g, "tensor_scalar", out=g.stage_f[:, 24:32], in0=g.stage_f[:, 16:24],
                scalar1=999.0, scalar2=None, op0=ALU.add)
        g.P.emit("ve", (lambda o, i: lambda e: e.tensor_reduce(
            out=o, in_=i, axis=AX.X, op=ALU.min))(g.route[:, ti:ti + 1], g.stage_f[:, 24:32]),
            waits=(("ve", v6),), inc=("ve", 1))
        ve(g, "tensor_scalar", out=g.onehot[:, 8 * ti:8 * ti + 8], in0=g.iota8[:],
           scalar1=g.route[:, ti:ti + 1], scalar2=None, op0=ALU.is_equal)
    v_route = g.P.val("ve")

    for ti in range(NT):
        m = mm(g, g.p_m2[0:8, ti:ti + 1], g.onehot[:, 8 * ti:8 * ti + 8], g.onescol_f[:],
               True, True,
               waits=([("ve", v_route), ("ve", g.v_ones)] + g.r_pm2.for_write()) if ti == 0 else ())
    g.r_pm2.wrote("mm", m)
    v = ve(g, "tensor_copy", out=g.small[0:8, 0:16], in_=g.p_m2[0:8, 0:16],
           waits=g.r_pm2.for_read() + g.r_small.for_write())
    g.r_pm2.read("ve", v)
    v_tot = ve(g, "reduce_sum", out=g.small[0:8, 48:49], in_=g.small[0:8, 0:16], axis=AX.X)
    g.r_small.wrote("ve", v_tot)
    m = tr_mm(g, g.p_m1[0:16, 0:8], g.small[0:8, 0:16], g.identf,
              waits=g.r_small.for_read() + g.r_pm1.for_write())
    g.r_pm1.wrote("mm", m)
    g.r_small.read("mm", m)
    v = ve(g, "tensor_copy", out=g.small[0:16, 16:24], in_=g.p_m1[0:16, 0:8],
           waits=g.r_pm1.for_read())
    g.r_pm1.read("ve", v)
    m = mm(g, g.p_m1[0:16, 0:8], g.sl16[:], g.small[0:16, 16:24], True, True,
           waits=[("ve", v), ("sy", g.w_consts)] + g.r_pm1.for_write())
    g.r_pm1.wrote("mm", m)
    v = ve(g, "tensor_copy", out=g.small[0:16, 24:32], in_=g.p_m1[0:16, 0:8],
           waits=g.r_pm1.for_read())
    g.r_pm1.read("ve", v)
    m = tr_mm(g, g.p_m1[0:8, 0:16], g.small[0:16, 24:32], g.identf,
              waits=[("ve", v)] + g.r_pm1.for_write())
    g.r_pm1.wrote("mm", m)
    v_excl = ve(g, "tensor_copy", out=g.small[0:8, 32:48], in_=g.p_m1[0:8, 0:16],
                waits=g.r_pm1.for_read())
    g.r_pm1.read("ve", v_excl)
    m = tr_mm(g, g.p_m1[0:1, 0:8], g.small[0:8, 48:49], g.identf,
              waits=[("ve", v_tot)] + g.r_pm1.for_write())
    g.r_pm1.wrote("mm", m)
    v = ve(g, "tensor_copy", out=g.cntrow[:, 0:8], in_=g.p_m1[0:1, 0:8], waits=g.r_pm1.for_read())
    g.r_pm1.read("ve", v)
    w1 = dma(g, "sy", g.d_ccin, g.cntrow[:, 0:8], waits=(("ve", v),))
    g.P.emit("gp", (lambda: lambda e: e.collective_compute(
        "AllGather", ALU.bypass, replica_groups=[[0, 1, 2, 3, 4, 5, 6, 7]],
        ins=[g.d_ccin.opt()], outs=[g.d_ccout.opt()]))(),
        waits=(("sy", w1),), inc=("gpc", 1))
    g.c_ccdone = g.P.val("gpc")

    m = mm(g, g.p_m1[0:16, 0:8], g.ones16[:], g.cntrow[:, 0:8], True, True,
           waits=[("ve", v)] + g.r_pm1.for_write())
    g.r_pm1.wrote("mm", m)
    v = ve(g, "tensor_copy", out=g.small[0:16, 50:58], in_=g.p_m1[0:16, 0:8],
           waits=g.r_pm1.for_read())
    g.r_pm1.read("ve", v)
    v = ve(g, "tensor_tensor", out=g.small[0:16, 50:58], in0=g.small[0:16, 50:58],
           in1=g.thr16[:], op=ALU.is_gt)
    m = mm(g, g.p_m1[0:8, 0:1], g.small[0:16, 50:58], g.onescol_f[0:16, 0:1], True, True,
           waits=[("ve", v), ("ve", g.v_ones)] + g.r_pm1.for_write())
    g.r_pm1.wrote("mm", m)
    v = ve(g, "tensor_scalar", out=g.small[0:8, 58:59], in0=g.p_m1[0:8, 0:1], scalar1=128.0,
           scalar2=None, op0=ALU.mult, waits=g.r_pm1.for_read())
    g.r_pm1.read("ve", v)
    m = mm(g, g.p_m2[0:8, 0:1], g.sl8[:], g.small[0:8, 58:59], True, True,
           waits=[("ve", v)] + g.r_pm2.for_write())
    g.r_pm2.wrote("mm", m)
    v_seg = ve(g, "tensor_copy", out=g.small[0:8, 59:60], in_=g.p_m2[0:8, 0:1],
               waits=g.r_pm2.for_read())
    g.r_pm2.read("ve", v_seg)
    m = tr_mm(g, g.p_m1[0:1, 0:8], g.small[0:8, 59:60], g.identf,
              waits=[("ve", v_seg)] + g.r_pm1.for_write())
    g.r_pm1.wrote("mm", m)
    v_segr = ve(g, "tensor_copy", out=g.segrow[:], in_=g.p_m1[0:1, 0:8], waits=g.r_pm1.for_read())
    g.r_pm1.read("ve", v_segr)
    m = mm(g, g.p_m1[:, 0:8], g.ones128[:], g.segrow[:], True, True,
           waits=[("ve", v_segr)] + g.r_pm1.for_write())
    g.r_pm1.wrote("mm", m)
    v_segb = ve(g, "tensor_copy", out=g.bcast8[:, 16:24], in_=g.p_m1[:, 0:8],
                waits=g.r_pm1.for_read())
    g.r_pm1.read("ve", v_segb)

    v_rpl = ve(g, "tensor_scalar", out=g.small[0:8, 32:48], in0=g.small[0:8, 32:48],
               scalar1=-1.0, scalar2=None, op0=ALU.add, waits=(("ve", v_excl),))
    pools = [(g.p_m2, g.r_pm2), (g.p_den, g.r_pden)]
    tpools = [(g.p_m1, g.r_pm1), (g.p_ctx, g.r_pctx)]
    for ti in range(NT):
        psu, rsu = pools[ti % 2]
        ptr_, rtr = tpools[ti % 2]
        m = mm(g, psu[:, 0:8], g.triu[:], g.onehot[:, 8 * ti:8 * ti + 8], True, True,
               waits=([("sy", g.w_consts)] if ti == 0 else []) + rsu.for_write())
        rsu.wrote("mm", m)
        m2 = tr_mm(g, ptr_[0:8, 0:128], g.onehot[:, 8 * ti:8 * ti + 8], g.identf,
                   waits=rtr.for_write())
        rtr.wrote("mm", m2)
        v = ve(g, "tensor_copy", out=g.den_s[0:8, 128 * (ti % 4):128 * (ti % 4) + 128],
               in_=ptr_[0:8, 0:128], waits=rtr.for_read() + (g.r_dens.for_write() if ti < 4 else []))
        rtr.read("ve", v)
        v = ve(g, "tensor_mul", out=g.stage_f[:, 0:8], in0=psu[:, 0:8],
               in1=g.onehot[:, 8 * ti:8 * ti + 8], waits=rsu.for_read() + g.r_stagef.for_write())
        rsu.read("ve", v)
        g.r_stagef.wrote("ve", v)
        v_par = ve(g, "reduce_sum", out=g.stat[:, 0:1], in_=g.stage_f[:, 0:8], axis=AX.X)
        g.r_stagef.read("ve", v_par)
        m3 = mm(g, ptr_[0:128, 8:9], g.den_s[0:8, 128 * (ti % 4):128 * (ti % 4) + 128],
                g.small[0:8, 32 + ti:32 + ti + 1], True, True,
                waits=[("ve", v_rpl), ("ve", g.P.val("ve"))] + rtr.for_write())
        rtr.wrote("mm", m3)
        v = ve(g, "tensor_add", out=g.posl[:, ti:ti + 1], in0=g.stat[:, 0:1],
               in1=ptr_[0:128, 8:9], waits=rtr.for_read())
        rtr.read("ve", v)
        v = ve(g, "tensor_mul", out=g.stage_f[:, 8:16], in0=g.onehot[:, 8 * ti:8 * ti + 8],
               in1=g.bcast8[:, 16:24], waits=(("ve", v_segb),) if ti == 0 else ())
        v = ve(g, "reduce_sum", out=g.stat[:, 1:2], in_=g.stage_f[:, 8:16], axis=AX.X)
        v = ve(g, "tensor_add", out=g.stat[:, 2:3], in0=g.stat[:, 1:2], in1=g.posl[:, ti:ti + 1])
        ve(g, "tensor_copy", out=g.dest32[:, ti:ti + 1], in_=g.stat[:, 2:3])
    g.v_dest = g.P.val("ve")

    v = ve(g, "memset", g.adjrow[:, 0:23], 0.0)
    for e in range(8):
        v1 = ve(g, "tensor_scalar", out=g.adjrow[:, 32:55], in0=g.iota23[:],
                scalar1=g.segrow[0:1, e:e + 1], scalar2=None, op0=ALU.is_ge,
                waits=(("ve", v_segr),) if e == 0 else ())
        v = ve(g, "tensor_add", out=g.adjrow[:, 0:23], in0=g.adjrow[:, 0:23],
               in1=g.adjrow[:, 32:55])
    v = ve(g, "tensor_scalar", out=g.adjrow[:, 0:23], in0=g.adjrow[:, 0:23], scalar1=128.0,
           scalar2=-128.0, op0=ALU.mult, op1=ALU.add)
    m = mm(g, g.p_m1[:, 0:23], g.ones128[:], g.adjrow[:, 0:23], True, True,
           waits=[("ve", v)] + g.r_pm1.for_write())
    g.r_pm1.wrote("mm", m)
    g.P.emit("ve", (lambda o, a, b: lambda e: e.scalar_tensor_tensor(
        out=o, in0=a, scalar=1.0, in1=b, op0=ALU.mult, op1=ALU.add))(
        g.small[:, 40:63], g.p_m1[:, 0:23], g.iotap[:].broadcast_to([128, 23])),
        waits=tuple(g.r_pm1.for_read() + g.r_small.for_write() + [("sy", g.w_consts),
                    ("ve", g.P.val("ve"))]),
        inc=("ve", 1))
    v = g.P.val("ve")
    g.r_pm1.read("ve", v)
    g.r_small.wrote("ve", v)
    g.v_widx = ve(g, "tensor_copy", out=g.widx32[:, 0:23], in_=g.small[:, 40:63],
                  waits=g.r_small.for_read())
    g.r_small.read("ve", g.v_widx)


def emit_kept(g):
    w2 = dma(g, "sy", g.gab[:], g.d_ccout, waits=(("gpc", g.c_ccdone),))
    m = mm(g, g.p_m1[0:1, 0:8], g.rmask[:], g.gab[:], True, True,
           waits=[("sy", w2), ("sy", g.w_consts)] + g.r_pm1.for_write())
    g.r_pm1.wrote("mm", m)
    v_co = ve(g, "tensor_copy", out=g.corow[:], in_=g.p_m1[0:1, 0:8], waits=g.r_pm1.for_read())
    g.r_pm1.read("ve", v_co)
    m = mm(g, g.p_m1[:, 0:8], g.ones128[:], g.corow[:], True, True,
           waits=[("ve", v_co)] + g.r_pm1.for_write())
    g.r_pm1.wrote("mm", m)
    v_cob = ve(g, "tensor_copy", out=g.bcast8[:, 8:16], in_=g.p_m1[:, 0:8],
               waits=g.r_pm1.for_read())
    g.r_pm1.read("ve", v_cob)
    for ti in range(NT):
        v = ve(g, "tensor_mul", out=g.stage_f[:, 0:8], in0=g.onehot[:, 8 * ti:8 * ti + 8],
               in1=g.bcast8[:, 8:16],
               waits=([("ve", v_cob)] + g.r_stagef.for_write()) if ti == 0 else ())
        v = ve(g, "reduce_sum", out=g.stat[:, 0:1], in_=g.stage_f[:, 0:8], axis=AX.X)
        v = ve(g, "tensor_add", out=g.stat[:, 1:2], in0=g.stat[:, 0:1], in1=g.posl[:, ti:ti + 1])
        ve(g, "tensor_scalar", out=g.kept[:, ti:ti + 1], in0=g.stat[:, 1:2],
           scalar1=float(CAP), scalar2=None, op0=ALU.is_lt)
    g.r_stagef.wrote("ve", g.P.val("ve"))
    g.v_kept = g.P.val("ve")



def emit_experts(g):
    P = g.P
    w_scat = 0
    for ti in range(NT):
        g.P.emit("gp", (lambda o, i, ix: lambda e: e.indirect_dma_start(
            out=o, out_offset=IndirectOffsetOnAxis(ap=ix, axis=0), in_=i, in_offset=None))(
            g.d_srt, g.fmA[:, 1024 * ti:1024 * ti + 1024], g.dest32[:, ti:ti + 1]),
            waits=((("ve", g.v_dest),) + tuple(g.r_fmA.for_read())) if ti == 0 else (),
            inc=("gp", 16))
        w_scat = g.P.val("gp")
    g.r_fmA.read("gp", w_scat)

    last_v_evac = [0, 0]
    last_st = [0, 0]
    last_m_ps = [0, 0]
    last_m_wb = [0, 0, 0]
    last_m_srt = [0, 0, 0, 0]
    last_m_ld = [0, 0, 0, 0]
    for f in range(NFILL):
        p2, p3, p4 = f % 2, f % 3, f % 4
        wb = g.fmB[:, 9216 * p3:9216 * p3 + 9216]
        g.P.emit("gp", (lambda o, ix: lambda e: e.indirect_dma_start(
            out=o, out_offset=None, in_=g.d_wt,
            in_offset=IndirectOffsetOnAxis(ap=ix, axis=0)))(wb, g.widx32[:, f:f + 1]),
            waits=((("ve", g.v_widx), ("sy", g.w_wt)) + tuple(g.r_fmB.for_read() + g.r_fmB.for_write())
                   if f < 3 else (("mm", last_m_wb[p3]),)),
            inc=("gp", 16))
        w_w = g.P.val("gp")
        sslot = g.ctxall[:, 1024 * p4:1024 * p4 + 1024]
        w_s = dma(g, "sy", sslot, g.d_srt[128 * f:128 * f + 128, :],
                  waits=([("gp", w_scat)] + g.r_ctxall.for_read() + g.r_ctxall.for_write())
                  if f < 4 else [("mm", last_m_ld[p4])])
        srtT = g.vs65[:, 1024 * p4:1024 * p4 + 1024]
        for kc in range(8):
            m = tr_mm(g, g.p_m1[:, 0:128], sslot[:, 128 * kc:128 * kc + 128],
                      g.identbf,
                      waits=([("sy", w_s)] + g.r_pm1.for_write()
                             + (g.r_vs65.for_write() if f < 4 else [("mm", last_m_srt[p4])]))
                      if kc == 0 else g.r_pm1.for_write())
            g.r_pm1.wrote("mm", m)
            if kc == 7:
                last_m_ld[p4] = m
            v = ve(g, "tensor_copy", out=srtT[:, 128 * kc:128 * kc + 128], in_=g.p_m1[:, 0:128],
                   waits=g.r_pm1.for_read())
            g.r_pm1.read("ve", v)
        v_t = v
        psl = g.p_sc[:, 1024 * p2:1024 * p2 + 1024]
        for nk in range(2):
            for kc in range(8):
                m = mm(g, psl[:, 512 * nk:512 * nk + 512], srtT[:, 128 * kc:128 * kc + 128],
                       wb[:, 1024 * kc + 512 * nk:1024 * kc + 512 * nk + 512], kc == 0, False,
                       waits=([("ve", v_t), ("gp", w_w), ("ve", last_v_evac[p2])]
                              + (g.r_psc.for_write() if f < 2 else [])) if (nk == 0 and kc == 0) else ())
            m = mm(g, psl[:, 512 * nk:512 * nk + 512], g.onesrowbf[:],
                   wb[0:1, 8192 + 512 * nk:8192 + 512 * nk + 512], False, True)
        last_m_ps[p2] = m
        last_m_wb[p3] = m
        last_m_srt[p4] = m
        v = ve(g, "tensor_copy", out=g.stage_f[:, 1024 * p2:1024 * p2 + 1024], in_=psl,
               waits=[("mm", m)] + (g.r_stagef.for_write() if f < 2
                                    else [("scq", last_st[p2])]))
        last_v_evac[p2] = v
        last_st[p2] = dma(g, "scq", g.d_srto[128 * f:128 * f + 128, :],
                          g.stage_f[:, 1024 * p2:1024 * p2 + 1024], waits=(("ve", v),))
    g.w_eo = P.val("scq")
    g.r_psc.wrote("mm", max(last_m_ps))
    g.r_stagef.wrote("ve", max(last_v_evac))
    g.r_fmB.wrote("gp", g.P.val("gp"))
    g.r_fmB.read("mm", max(last_m_ps))
    g.r_fmB.read("ve", max(last_v_evac))
    g.r_ctxall.wrote("sy", g.P.val("sy"))
    g.r_ctxall.read("mm", max(last_m_ps))
    g.r_vs65.wrote("ve", g.P.val("ve"))
    g.r_vs65.read("mm", max(last_m_ps))



def emit_tail(g):
    P = g.P
    emit_kept(g)
    slots = [g.attb[:, 0:2048].bitcast(F32), g.attb[:, 2048:4096].bitcast(F32),
             g.stage_f[:, 0:1024], g.stage_f[:, 1024:2048]]
    slot_res = [Res(), Res(), Res(), Res()]
    ybufs = [g.vs65[:, 4096 + 2048 * p:4096 + 2048 * p + 2048].bitcast(F32) for p in range(2)]
    ybuf_res = [Res(), Res()]
    w_gs = [0] * NT
    w_x1s = [0] * NT
    for ti in range(NT):
        sp = ti % 4
        g.P.emit("gp", (lambda o, ix: lambda e: e.indirect_dma_start(
            out=o, out_offset=None, in_=g.d_srto,
            in_offset=IndirectOffsetOnAxis(ap=ix, axis=0)))(
            slots[sp], g.dest32[:, ti:ti + 1]),
            waits=((("scq", g.w_eo), ("ve", g.v_kept)) + tuple(g.r_stagef.for_write() + g.r_attb0.for_write()
                   + g.r_attb1.for_write() + g.r_attb2.for_write() + g.r_attb3.for_write())
                   if ti == 0 else tuple(slot_res[sp].for_write())),
            inc=("gp", 16))
        w_gs[ti] = g.P.val("gp")
        slot_res[sp].wrote("gp", w_gs[ti])
        w_x1s[ti] = dma(g, "sy", g.x_tok[:, 1024 * ti:1024 * ti + 1024],
                        g.d_x1f[128 * ti:128 * ti + 128, :],
                        waits=g.r_xtok.for_write() if ti == 0 else ())
        if ti < 3:
            continue
        emit_tail_tile(g, ti - 3, slots, slot_res, ybufs, ybuf_res, w_gs, w_x1s)
    for ti in range(NT - 3, NT):
        emit_tail_tile(g, ti, slots, slot_res, ybufs, ybuf_res, w_gs, w_x1s)
    w_out = g.P.val("scq")
    g.P.emit("ve", (lambda: lambda e: e.memset(g.stat[:, 15:16], 0.0))(),
             waits=(("scq", w_out),), inc=("ve", 1))


def emit_tail_tile(g, ti, slots, slot_res, ybufs, ybuf_res, w_gs, w_x1s):
    sp = ti % 4
    yp = ti % 2
    so = 16 * yp
    eo = slots[sp]
    yb = ybufs[yp]
    x2 = yb[:, 0:1024]
    x3 = yb[:, 0:1024]
    v1 = ve(g, "tensor_mul", out=g.stat[:, 32 + 2 * yp:33 + 2 * yp], in0=g.rp[:, ti:ti + 1],
            in1=g.kept[:, ti:ti + 1])
    v2 = ve(g, "tensor_scalar", out=g.stat[:, 33 + 2 * yp:34 + 2 * yp],
            in0=g.stat[:, 32 + 2 * yp:33 + 2 * yp], scalar1=-1.0, scalar2=None,
            op0=ALU.mult)
    v2 = ve(g, "tensor_add", out=g.stat[:, 33 + 2 * yp:34 + 2 * yp],
            in0=g.stat[:, 33 + 2 * yp:34 + 2 * yp], in1=g.rp[:, ti:ti + 1])
    v3 = ve(g, "tensor_scalar", out=x2, in0=eo, scalar1=g.stat[:, 32 + 2 * yp:33 + 2 * yp],
            scalar2=None, op0=ALU.mult,
            waits=[("gp", w_gs[ti]), ("ve", v1)] + ybuf_res[yp].for_write())
    slot_res[sp].read("ve", v3)
    g.P.emit("ve", (lambda o, i0, s1, i1: lambda e: e.scalar_tensor_tensor(
        out=o, in0=i0, scalar=s1, in1=i1, op0=ALU.mult, op1=ALU.add))(
        x2, g.fmA[:, 1024 * ti:1024 * ti + 1024], g.stat[:, 33 + 2 * yp:34 + 2 * yp],
        x2), waits=(("ve", v3), ("ve", v2)), inc=("ve", 1))
    if g.stage == "tl0" and ti == 0:
        wd = dma(g, "sy", g.dbg6[0:128, :], eo, waits=(("gp", w_gs[0]),))
        wd = dma(g, "sy", g.dbg6[128:256, :], g.x_tok[:, 0:1024], waits=(("sy", w_x1s[0]),))
        wd = dma(g, "sy", g.dbg6[256:384, :], x2, waits=(("ve", g.P.val("ve")),))
    v4 = ve(g, "tensor_add", out=x2, in0=x2,
            in1=g.x_tok[:, 1024 * ti:1024 * ti + 1024], waits=(("sy", w_x1s[ti]),))
    if g.stage == "tl0" and ti == 0:
        wd = dma(g, "sy", g.dbg6[384:512, :], x2, waits=(("ve", v4),))
    emit_ln_once(g, x2, eo, None, affine=True, waits0=(("ve", v4),), so=so)
    if g.stage == "tl0" and ti == 0:
        wd = dma(g, "sy", g.dbg6[512:640, :], eo, waits=(("ve", g.P.val("ve")),))
    v5 = ve(g, "tensor_add", out=x3, in0=eo,
            in1=g.x_tok[:, 1024 * ti:1024 * ti + 1024])
    if g.stage == "tl0" and ti == 0:
        wd = dma(g, "sy", g.dbg6[640:768, :], x3, waits=(("ve", v5),))
        wd = dma(g, "sy", g.dbg6[768:769, 0:40], g.stat[0:1, 0:40], waits=(("ve", v5),))
        wd = dma(g, "sy", g.dbg6[769:897, 0:16], g.kept[:, 0:16], waits=(("ve", v5),))
        wd = dma(g, "sy", g.dbg6[897:1025, 0:16], g.rp[:, 0:16], waits=(("ve", v5),))
    slot_res[sp].read("ve", v5)
    last = emit_ln_once(g, x3, g.x_tok[:, 1024 * ti:1024 * ti + 1024], None, affine=False,
                        waits0=(("ve", v5),), so=so)
    vz = g.P.val("ve")
    ybuf_res[yp].wrote("ve", vz)
    ybuf_res[yp].read("ve", vz)
    w = dma(g, "scq", g.out[128 * ti:128 * ti + 128, :], g.x_tok[:, 1024 * ti:1024 * ti + 1024],
            waits=[last])
    g.r_xtok.read("scq", w)



def shard_inputs(inputs, core, consts):
    z = inputs
    m = {
        "x": np.ascontiguousarray(z["L0_attn_x"][core]),
        "wdq": z["L0_attn_W_dq"], "wuq": z["L0_attn_W_uq"], "wdkv": z["L0_attn_W_dkv"],
        "wuk": z["L0_attn_W_uk"], "wuv": z["L0_attn_W_uv"], "wo": z["L0_attn_W_o"],
        "wq": np.ascontiguousarray(z["L0_ffn_Wq"].reshape(D, D)),
        "wk": np.ascontiguousarray(z["L0_ffn_Wk"].reshape(D, D)),
        "wv": np.ascontiguousarray(z["L0_ffn_Wv"].reshape(D, D)),
        "bq": np.ascontiguousarray(z["L0_ffn_bq"].reshape(1, D)),
        "bk": np.ascontiguousarray(z["L0_ffn_bk"].reshape(1, D)),
        "bv": np.ascontiguousarray(z["L0_ffn_bv"].reshape(1, D)),
        "we": np.ascontiguousarray(z["L0_ffn_We"].reshape(E * D, D)),
        "be": np.ascontiguousarray(z["L0_ffn_be"]),
        "wsw": z["L0_ffn_Wsw"],
        "bsw": np.ascontiguousarray(z["L0_ffn_bsw"].reshape(1, E)),
        "gamma": np.ascontiguousarray(z["L0_ffn_gamma"].reshape(1, D)),
        "beta": np.ascontiguousarray(z["L0_ffn_beta"].reshape(1, D)),
        "rankmask": (np.arange(8) < core).astype(np.float32).reshape(8, 1),
    }
    m.update(consts)
    return m


_NC_CACHE = {}


def run(inputs, stage="full"):
    if stage not in _NC_CACHE:
        _NC_CACHE[stage] = build(stage)
    nc = _NC_CACHE[stage]
    consts = host_consts()
    in_maps = [shard_inputs(inputs, c, consts) for c in range(8)]
    return run_bass_kernel_spmd(nc, in_maps, core_ids=list(range(8)))


def kernel(**inputs):
    res = run(inputs, "full")
    return np.stack([res.results[c]["out"] for c in range(8)], axis=0)


# version 22 (final)
# speedup vs baseline: 1.0197x; 1.0038x over previous
import numpy as np
import ml_dtypes
import concourse.bass as bass
import concourse.mybir as mybir
from concourse.bass import IndirectOffsetOnAxis
from concourse.bass_utils import run_bass_kernel_spmd

F32 = mybir.dt.float32
BF16 = mybir.dt.bfloat16
U32 = mybir.dt.uint32
AF = mybir.ActivationFunctionType
ALU = mybir.AluOpType
AX = mybir.AxisListType
BF = ml_dtypes.bfloat16

T, D, NH, HS, L, QL, E = 2048, 1024, 16, 64, 128, 128, 8
CAP = 2457
NT = 16
NCK = 4
NFILL = 23
EPS = 1e-5



class Prog:
    def __init__(self):
        self.ops = []
        self.cnt = {}

    SELF_SEM = {"ve": "ve", "sc": "sc"}

    def emit(self, eng, fn, waits=(), inc=None):
        if inc is not None and inc[0] in ("ve", "sc", "gpc"):
            waits = tuple(waits) + ((inc[0], self.cnt.get(inc[0], 0)),)
        import sys
        fr = sys._getframe(1)
        tag = []
        for _ in range(4):
            if fr is None:
                break
            tag.append(fr.f_lineno)
            fr = fr.f_back
        self._tag = tuple(tag)
        ws = {}
        for s, v in waits:
            if v and v > ws.get(s, 0):
                ws[s] = v
        if inc is not None:
            sem, n = inc
            self.cnt[sem] = self.cnt.get(sem, 0) + n
        self.ops.append((eng, fn, tuple(ws.items()), inc, self._tag))

    def val(self, sem):
        return self.cnt.get(sem, 0)

    def replay(self, eng_name, engine, sems):
        for eng, fn, waits, inc, tag in self.ops:
            if eng != eng_name:
                continue
            for sem, v in waits:
                engine.wait_ge(sems[sem], v)
            try:
                inst = fn(engine)
            except Exception as ex:
                raise RuntimeError(f"op failed, emitted at lines {tag}: {ex}") from ex
            if inc is not None:
                sem, n = inc
                inst.then_inc(sems[sem], n)


class Res:

    def __init__(self):
        self.wsem = None
        self.readers = []

    def for_write(self):
        ws = list(self.readers)
        if self.wsem:
            ws.append(self.wsem)
        return ws

    def wrote(self, sem, val):
        self.wsem = (sem, val)
        self.readers = []

    def for_read(self):
        return [self.wsem] if self.wsem else []

    def read(self, sem, val):
        self.readers.append((sem, val))


class Ctx:
    pass



def host_consts():
    c = {}
    c["identbf"] = np.eye(128, dtype=BF)
    c["identf"] = np.eye(128, dtype=np.float32)
    c["triu"] = np.triu(np.ones((128, 128), np.float32))
    c["sl8"] = np.triu(np.ones((8, 8), np.float32), 1)
    c["sl16"] = np.triu(np.ones((16, 16), np.float32), 1)
    c["ones128"] = np.ones((1, 128), np.float32)
    c["onesrowbf"] = np.ones((1, 128), BF)
    c["onescolbf"] = np.ones((128, 1), BF)
    c["ones16"] = np.ones((1, 16), np.float32)
    c["iota8"] = np.tile(np.arange(8, dtype=np.float32), (128, 1))
    c["iota8m"] = (c["iota8"] - 999.0).astype(np.float32)
    c["iotap"] = np.arange(128, dtype=np.float32).reshape(128, 1)
    c["iota23"] = (128.0 * np.arange(NFILL, dtype=np.float32)).reshape(1, NFILL)
    c["thr16"] = (128.0 * np.arange(16, dtype=np.float32)).reshape(16, 1) * np.ones((1, 8), np.float32)
    p = np.arange(128).reshape(128, 1)
    f = np.arange(512).reshape(1, 512)
    m = np.zeros((128, 2048), np.float32)
    for j in range(4):
        m[:, 512 * j:512 * j + 512] = (f >= (128 * j + p)).astype(np.float32)
    c["maskc"] = m.astype(BF)
    c["ones2048bf"] = np.ones((128, 2048), BF)
    sel = np.zeros((16, 4, 512), np.float32)
    for tc in range(4):
        sel[4 * (tc + 1):, tc, :] = 1.0
    c["selbf"] = sel.reshape(16, 2048).astype(BF)
    oh16 = np.zeros((128, 256), np.float32)
    ohr = np.zeros((128, 256), np.float32)
    for h in range(16):
        oh16[:, 16 * h + h] = 1.0
        ohr[64, 16 * h + h] = 1.0
    c["oh16bf"] = oh16.astype(BF)
    c["ohrow16"] = ohr
    return c



def build(stage="full"):
    nc = bass.Bass("TRN2", target_bir_lowering=False)
    g = Ctx()
    g.nc = nc
    g.stage = stage
    g.P = Prog()

    di = lambda n, sh, dt=F32: nc.dram_tensor(n, sh, dt, kind="ExternalInput").ap()
    do = lambda n, sh, dt=F32: nc.dram_tensor(n, sh, dt, kind="ExternalOutput").ap()
    dint = lambda n, sh, dt=F32: nc.dram_tensor(n, sh, dt, kind="Internal").ap()

    g.x = di("x", [T, D])
    g.wdq = di("wdq", [QL, D]); g.wuq = di("wuq", [D, QL])
    g.wdkv = di("wdkv", [L, D]); g.wuk = di("wuk", [D, L]); g.wuv = di("wuv", [D, L])
    g.wo = di("wo", [D, D])
    g.wq = di("wq", [D, D]); g.wk = di("wk", [D, D]); g.wv = di("wv", [D, D])
    g.bq = di("bq", [1, D]); g.bk = di("bk", [1, D]); g.bv = di("bv", [1, D])
    g.we = di("we", [E * D, D]); g.be = di("be", [E, D])
    g.wsw = di("wsw", [E, D]); g.bsw = di("bsw", [1, E])
    g.gamma = di("gamma", [1, D]); g.beta = di("beta", [1, D])
    g.rankmask = di("rankmask", [8, 1])
    g.hc = host_consts()
    for k, v in g.hc.items():
        setattr(g, "c_" + k, di(k, list(v.shape), BF16 if v.dtype == BF else F32))
    g.out = do("out", [T, D])
    if stage == "x1":
        g.dbg = do("dbg_x1", [T, D], F32)
    elif stage == "xy":
        g.dbg = do("dbg_xy", [T, D], F32)
    elif stage == "ckv":
        g.dbg = do("dbg_ckv", [128, 2048], BF16)
        g.dbg5 = do("dbg_xT", [128, 16384], BF16)
        g.dbg2 = do("dbg_qt", [128, 16384], BF16)
        g.dbg3 = do("dbg_keff", [128, 1024], BF16)
        g.dbg4 = do("dbg_veff", [128, 1024], BF16)
    elif stage == "hid":
        g.dbg = do("dbg_hid", [T, D], BF16)
    elif stage == "route":
        g.dbg = do("dbg_route", [128, 16 * 8], F32)
    elif stage == "full2":
        g.dbg = do("dbg_route", [128, 16 * 8], F32)
        g.dbg6 = do("dbg_eo", [T, D], F32)
        g.dbg7 = do("dbg_tail", [128, 5 * D], F32)
    elif stage == "tl0":
        g.dbg6 = do("dbg_eo", [T, D], F32)

    g.d_xbf = dint("d_xbf", [T, D], BF16)
    g.d_x1bf = dint("d_x1bf", [T, D], BF16)
    g.d_hidbf = dint("d_hidbf", [T, D], BF16)
    g.d_x1f = dint("d_x1f", [T, D], F32)
    g.d_webf = dint("d_webf", [E * D, D], BF16)
    g.d_wt = dint("d_wt", [E * 128, 9 * 1024], BF16)
    g.d_wqkvbf = dint("d_wqkvbf", [3 * D, D], BF16)
    g.d_wobf = dint("d_wobf", [D, D], BF16)
    g.d_smallbf = dint("d_smallbf", [3 * 128, D], BF16)
    g.d_srt = dint("d_srt", [NFILL * 128, D], BF16)
    g.d_srto = dint("d_srto", [NFILL * 128, D], F32)
    g.d_ccin = dint("d_ccin", [1, 8])
    g.d_ccout = dint("d_ccout", [8, 8])

    from contextlib import ExitStack
    with ExitStack() as ctx:
        S = lambda n, sh, dt=F32: ctx.enter_context(nc.sbuf_tensor(n, sh, dt))
        Ps = lambda n, sh: ctx.enter_context(nc.psum_tensor(n, sh, F32))

        g.fmA = S("fmA", [128, 16384], BF16)
        g.fmB = S("fmB", [128, 32768], BF16)
        g.x_tok = g.fmB[:].bitcast(F32)
        g.vs65 = S("vs65", [128, NH * 16 * 65], BF16)
        g.ctxall = S("ctxall", [128, 16 * 512], BF16)
        g.attb = S("attb", [128, 2 * 2048], BF16)
        g.wqkvT = g.ctxall
        g.qlb = S("qlb", [128, 1024], BF16)
        g.mlakv = S("mlakv", [128, 4096], BF16)
        g.ckvT = g.mlakv[:, 0:2048]
        g.ckvS = g.mlakv[:, 2048:4096]
        g.q2T = g.attb[:, 0:2048]
        g.gbc = g.mlakv[:].bitcast(F32)
        g.wsmall = S("wsmall", [128, 3 * 1024], BF16)
        g.prepw = S("prepw", [128, 2048], BF16)
        g.wuk_s = g.prepw[:, 0:1024]
        g.wuv_s = g.prepw[:, 1024:2048]
        g.keff = S("keff", [128, 1024], BF16)
        g.veff = S("veff", [128, 1024], BF16)
        g.wtmps = S("wtmps", [128, 2048], BF16)
        g.wtmp = g.wtmps[:, 0:1024]
        g.wtmp2 = g.wtmps[:, 1024:2048]
        g.wswT = S("wswT", [128, 64], BF16)
        g.bqkv = S("bqkv", [128, 24])
        g.u65 = S("u65", [128, 512], BF16)
        g.den_s = S("den_s", [128, 512])
        g.r16 = S("r16", [128, 512])
        g.rtok = S("rtok", [128, 64])
        g.stat = S("stat", [128, 48])
        g.lg = S("lg", [128, 128])
        g.route = S("route", [128, 16])
        g.rp = S("rp", [128, 16])
        g.onehot = S("onehot", [128, 128])
        g.kept = S("kept", [128, 16])
        g.posl = S("posl", [128, 16])
        g.dest32 = S("dest32", [128, NT], U32)
        g.widx32 = S("widx32", [128, NFILL], U32)
        g.cntrow = S("cntrow", [1, 8 * 17])
        g.adjrow = S("adjrow", [1, 128])
        g.segrow = S("segrow", [1, 8])
        g.corow = S("corow", [1, 8])
        g.gab = S("gab", [8, 8])
        g.small = S("small", [128, 64])
        g.svT = S("svT", [128, NH * 65], BF16)
        g.sv_s = S("sv_s", [128, 16])
        g.bcast8 = S("bcast8", [128, 32])
        g.stage_f = S("stage_f", [128, 2048])
        g.stage_b = S("stage_b", [128, 2048], BF16)
        g.stage_w = None
        g.bswr = S("bswr", [1, 8])
        g.rmask = S("rmask", [8, 1])
        g.onescol_f = S("onescol_f", [128, 1])
        _deferred = ("maskc", "ones2048bf", "selbf")
        for n in g.hc:
            if n in _deferred:
                continue
            setattr(g, n, S("s_" + n, list(g.hc[n].shape), BF16 if g.hc[n].dtype == BF else F32))
        g.maskc = g.wsmall[:, 0:2048]
        g.ones2048bf = g.prepw[:, 0:2048]
        g.selbf = g.wtmps[0:16, 0:2048]

        g.p_sc = Ps("p_sc", [128, 2048])
        g.p_ctx = Ps("p_ctx", [128, 512])
        g.p_den = Ps("p_den", [128, 512])
        g.p_m1 = Ps("p_m1", [128, 512])
        g.p_m2 = Ps("p_m2", [128, 512])

        for r in ["r_psc", "r_pscB", "r_pctx", "r_pden", "r_pm1", "r_pm2", "r_attb0", "r_attb1",
                  "r_attb2", "r_attb3",
                  "r_qlb", "r_qlb2", "r_u65", "r_dens", "r_r16", "r_rtok", "r_ctxall", "r_stagef",
                  "r_stageb", "r_fmA", "r_fmB", "r_vs65", "r_wtmp",
                  "r_wtmp2", "r_small", "r_svs", "r_sv", "r_mlakv"]:
            setattr(g, r, Res())
        g.r_xtok = g.r_fmB
        g.r_wqkvT = g.r_ctxall
        g.abres = [g.r_attb0, g.r_attb1, g.r_attb2, g.r_attb3]
        g.r_pscs = [g.r_psc, g.r_pscB]
        g.stage_w = g.stage_b[:, 1024:2048]
        g.r_stagew = g.r_stageb

        sems = {}
        for n in ["sy", "gp", "scq", "mm", "ve", "sc", "gpc"]:
            sems[n] = ctx.enter_context(nc.semaphore(n))
        g.sems = sems

        emit_program(g)

        block = ctx.enter_context(nc.Block())

        @block.sync
        def _(e):
            g.P.replay("sy", e, sems)

        @block.scalar
        def _(e):
            g.P.replay("sc", e, sems)

        @block.tensor
        def _(e):
            g.P.replay("te", e, sems)

        @block.vector
        def _(e):
            g.P.replay("ve", e, sems)

        @block.gpsimd
        def _(e):
            g.P.replay("gp", e, sems)

    return nc



def dma(g, q, out, in_, waits=(), transpose=False):
    eng = {"sy": "sy", "gp": "gp", "scq": "sc"}[q]
    sem = {"sy": "sy", "gp": "gp", "scq": "scq"}[q]
    g.P.emit(eng, (lambda o, i, t: lambda e: e.dma_start(out=o, in_=i, transpose=t))(out, in_, transpose),
             waits=waits, inc=(sem, 16))
    return g.P.val(sem)


def mm(g, out, lhsT, rhs, start, stop, waits=()):
    g.P.emit("te", (lambda o, l, r, s0, s1: lambda e: e.matmul(o, l, r, start=s0, stop=s1))(
        out, lhsT, rhs, start, stop), waits=waits, inc=("mm", 1))
    return g.P.val("mm")


def act(g, out, in_, func=AF.Copy, bias=0.0, scale=1.0, accum_out=None, waits=()):
    g.P.emit("sc", (lambda o, i, f, b, s, a: lambda e: e.activation(o, i, f, bias=b, scale=s, accum_out=a))(
        out, in_, func, bias, scale, accum_out), waits=waits, inc=("sc", 1))
    return g.P.val("sc")


def ve(g, name, *args, waits=(), **kw):
    g.P.emit("ve", (lambda n, a, k: lambda e: getattr(e, n)(*a, **k))(name, args, kw),
             waits=waits, inc=("ve", 1))
    return g.P.val("ve")


def gp_op(g, name, *args, waits=(), **kw):
    g.P.emit("gp", (lambda n, a, k: lambda e: getattr(e, n)(*a, **k))(name, args, kw),
             waits=waits, inc=("gpc", 1))
    return g.P.val("gpc")


def tr_mm(g, out_psum, in_sbuf, ident, waits=()):
    k = in_sbuf.shape[0]
    b = in_sbuf.base_partition()
    return mm(g, out_psum, in_sbuf, ident[b:b + k, b:b + k], True, True, waits=waits)



def emit_program(g):
    emit_consts(g)
    emit_x_path(g)
    emit_small_weights(g)
    emit_weprep(g)
    emit_mla(g)
    if g.stage == "ckv":
        return
    if g.stage == "xy":
        emit_dbg_tiles(g, g.dbg, lambda i: g.x_tok[:, 1024 * i:1024 * i + 1024], g.r_xtok)
        return
    emit_postln1(g)
    if g.stage == "x1":
        emit_dbg_tiles(g, g.dbg, lambda i: g.x_tok[:, 1024 * i:1024 * i + 1024], g.r_xtok)
        return
    emit_ffn_attn(g)
    if g.stage == "hid":
        emit_dbg_tiles(g, g.dbg, lambda i: g.fmA[:, 1024 * i:1024 * i + 1024], g.r_fmA)
        return
    emit_router(g)
    if g.stage == "route":
        emit_kept(g)
        emit_dbg_route(g)
        return
    emit_experts(g)
    if g.stage == "full2":
        for i in range(NT):
            dma(g, "sy", g.dbg6[128 * i:128 * i + 128, :], g.d_srto[128 * i:128 * i + 128, :],
                waits=(("scq", g.w_eo),) if i == 0 else ())
    emit_tail(g)
    if g.stage == "full2":
        emit_dbg_route(g)


def emit_dbg_tiles(g, dst, src_fn, res):
    w = 0
    for i in range(NT):
        w = dma(g, "sy", dst[128 * i:128 * i + 128, :], src_fn(i),
                waits=res.for_read() if i == 0 else ())
    g.P.emit("ve", (lambda: lambda e: e.memset(g.stat[:, 15:16], 0.0))(),
             waits=(("sy", w),), inc=("ve", 1))


def emit_dbg_route(g):
    v = ve(g, "tensor_copy", out=g.lg[:, 0:16], in_=g.route[:, 0:16])
    v = ve(g, "tensor_copy", out=g.lg[:, 16:32], in_=g.rp[:, 0:16])
    v = ve(g, "tensor_copy", out=g.lg[:, 32:48], in_=g.kept[:, 0:16])
    v = ve(g, "tensor_copy", out=g.lg[:, 48:64], in_=g.posl[:, 0:16])
    v = ve(g, "tensor_copy", out=g.lg[:, 64:80], in_=g.dest32[:, 0:16])
    w = dma(g, "sy", g.dbg, g.lg[:, 0:128], waits=(("ve", v),))
    g.P.emit("ve", (lambda: lambda e: e.memset(g.stat[:, 15:16], 0.0))(),
             waits=(("sy", w),), inc=("ve", 1))



def emit_consts(g):
    for n in g.hc:
        if n in ("maskc", "ones2048bf", "selbf"):
            continue
        dma(g, "sy", getattr(g, n)[:], getattr(g, "c_" + n))
    dma(g, "sy", g.rmask[:], g.rankmask)
    dma(g, "sy", g.bswr[:], g.bsw)
    g.w_consts = g.P.val("sy")
    ve(g, "memset", g.onescol_f[:], 1.0)
    g.v_ones = g.P.val("ve")



def emit_x_path(g):
    g.w_x = []
    for i in range(NT):
        g.w_x.append(dma(g, "scq", g.x_tok[:, 1024 * i:1024 * i + 1024],
                         g.x[128 * i:128 * i + 128, :]))
    g.r_xtok.wrote("scq", g.w_x[-1])
    wc = dma(g, "gp", g.d_xbf, g.x)
    for c in range(8):
        w = dma(g, "sy", g.fmA[:, 2048 * c:2048 * c + 2048], g.d_xbf[:, 128 * c:128 * c + 128],
                transpose=True, waits=(("gp", wc),) if c == 0 else ())
    g.w_xT = g.P.val("sy")
    g.r_fmA.wrote("sy", g.w_xT)



def emit_small_weights(g):
    P = g.P
    dma(g, "gp", g.d_smallbf[0:128, :], g.wdq)
    dma(g, "gp", g.d_smallbf[128:256, :], g.wdkv)
    wuq_flat = g.d_smallbf[256:384, :].rearrange("p f -> (p f)").rearrange("(a b) -> a b", a=1024)
    dma(g, "gp", wuq_flat, g.wuq)
    g.w_cast_small = g.P.val("gp")
    dma(g, "gp", g.wuk_s.rearrange("p (a b) -> p a b", a=8),
        g.wuk.rearrange("(a p) b -> p a b", p=128))
    dma(g, "gp", g.wuv_s.rearrange("p (a b) -> p a b", a=8),
        g.wuv.rearrange("(a p) b -> p a b", p=128))
    g.w_uvk = P.val("gp")
    dma(g, "gp", g.d_wobf, g.wo)
    g.w_cast_wo = P.val("gp")
    dma(g, "gp", g.d_wqkvbf[0:D, :], g.wq)
    dma(g, "gp", g.d_wqkvbf[D:2 * D, :], g.wk)
    dma(g, "gp", g.d_wqkvbf[2 * D:3 * D, :], g.wv)
    dma(g, "gp", g.d_webf, g.we)
    g.w_cast = P.val("gp")
    for c in range(8):
        dma(g, "sy", g.wsmall[:, 128 * c:128 * c + 128],
            g.d_smallbf[0:128, 128 * c:128 * c + 128], transpose=True,
            waits=(("gp", g.w_cast_small),) if c == 0 else ())
    for c in range(8):
        dma(g, "sy", g.wsmall[:, 1024 + 128 * c:1024 + 128 * c + 128],
            g.d_smallbf[128:256, 128 * c:128 * c + 128], transpose=True)
    for c in range(8):
        dma(g, "sy", g.wsmall[:, 2048 + 128 * c:2048 + 128 * c + 128],
            wuq_flat[128 * c:128 * c + 128, 0:128], transpose=True)
    g.w_small = P.val("sy")
    w1 = dma(g, "sy", g.wtmp.rearrange("p (a b) -> p a b", a=8),
             wuq_flat.rearrange("(a p) b -> p a b", p=128))
    w2 = dma(g, "sy", g.wtmp2[:, 0:1024], g.d_smallbf[0:128, :], waits=(("gp", g.w_cast_small),))
    g.r_wtmp.wrote("sy", w1)
    g.r_wtmp2.wrote("sy", w2)

    for c in range(8):
        m = mm(g, g.p_m1[:, 0:128], g.wtmp[:, 128 * c:128 * c + 128],
               g.wuk_s[:, 128 * c:128 * c + 128], c == 0, c == 7,
               waits=g.r_wtmp.for_read() + [("gp", g.w_uvk)] + g.r_pm1.for_write() if c == 0 else ())
    g.r_pm1.wrote("mm", m)
    g.r_wtmp.read("mm", m)
    v = ve(g, "tensor_copy", out=g.u65[:, 0:128], in_=g.p_m1[:, 0:128],
           waits=g.r_pm1.for_read() + g.r_u65.for_write())
    g.r_pm1.read("ve", v)
    g.r_u65.wrote("ve", v)
    for c in range(8):
        m = mm(g, g.p_m2[:, 0:128], g.wtmp2[:, 128 * c:128 * c + 128], g.u65[:, 0:128],
               True, True,
               waits=(g.r_wtmp2.for_read() + g.r_u65.for_read() + g.r_pm2.for_write()) if c == 0 else g.r_pm2.for_write())
        g.r_pm2.wrote("mm", m)
        v = ve(g, "tensor_copy", out=g.keff[:, 128 * c:128 * c + 128], in_=g.p_m2[:, 0:128],
               waits=g.r_pm2.for_read())
        g.r_pm2.read("ve", v)
    g.r_wtmp2.read("mm", m)
    g.r_u65.read("mm", m)
    g.v_keff = v

    for c in range(8):
        w = dma(g, "sy", g.wtmp[:, 0:1024], g.d_wobf[0:1024, 128 * c:128 * c + 128],
                transpose=True, waits=g.r_wtmp.for_write() + ([("gp", g.w_cast_wo)] if c == 0 else []))
        g.r_wtmp.wrote("sy", w)
        for nk in range(2):
            pdst = g.p_m1 if nk == 0 else g.p_m2
            rres = g.r_pm1 if nk == 0 else g.r_pm2
            m = mm(g, pdst[:, 0:512], g.wuv_s[:, 128 * c:128 * c + 128],
                   g.wtmp[:, 512 * nk:512 * nk + 512], c == 0, c == 7,
                   waits=([("sy", w)] + rres.for_write()) if c == 0 else [("sy", w)])
            if c == 0:
                rres.wrote("mm", m)
        g.r_wtmp.read("mm", m)
    g.r_pm1.wrote("mm", m)
    g.r_pm2.wrote("mm", m)
    for nk in range(2):
        pdst = g.p_m1 if nk == 0 else g.p_m2
        rres = g.r_pm1 if nk == 0 else g.r_pm2
        v = ve(g, "tensor_copy", out=g.veff[:, 512 * nk:512 * nk + 512], in_=pdst[:, 0:512],
               waits=rres.for_read())
        rres.read("ve", v)
    g.v_veff = v

    w = dma(g, "gp", g.stage_b[0:8, 0:1024], g.wsw, waits=g.r_stageb.for_write())
    g.r_stageb.wrote("gp", w)
    for c in range(8):
        m = tr_mm(g, g.p_m2[:, 0:8], g.stage_b[0:8, 128 * c:128 * c + 128], g.identbf,
                  waits=g.r_stageb.for_read() + g.r_pm2.for_write())
        g.r_pm2.wrote("mm", m)
        v = ve(g, "tensor_copy", out=g.wswT[:, 8 * c:8 * c + 8], in_=g.p_m2[:, 0:8],
               waits=g.r_pm2.for_read())
        g.r_pm2.read("ve", v)
    g.r_stageb.read("mm", m)
    g.v_wsw = v

    for pi, b in enumerate([g.bq, g.bk, g.bv]):
        for mt in range(8):
            dma(g, "sy", g.bqkv[:, 8 * pi + mt:8 * pi + mt + 1],
                b[0:1, 128 * mt:128 * mt + 128].rearrange("a p -> (a p)").unsqueeze(1))
    g.w_bqkv = P.val("sy")
    m = mm(g, g.p_m2[:, 0:8], g.ones128[:], g.bswr[:], True, True,
           waits=[("sy", g.w_consts)] + g.r_pm2.for_write())
    g.r_pm2.wrote("mm", m)
    v = ve(g, "tensor_copy", out=g.bcast8[:, 0:8], in_=g.p_m2[:, 0:8], waits=g.r_pm2.for_read())
    g.r_pm2.read("ve", v)
    g.v_bsw = v



def emit_weprep(g):
    P = g.P
    v0 = ve(g, "memset", g.stage_w[:, 0:1024], 0.0, waits=g.r_stagew.for_write())
    g.r_stagew.wrote("ve", v0)
    for e in range(E):
        w = dma(g, "sy", g.d_wt[128 * e:128 * e + 128, 8192:9216], g.stage_w[:, 0:1024],
                waits=g.r_stagew.for_read() if e == 0 else ())
        g.r_stagew.read("sy", w)
    for e in range(E):
        dma(g, "gp", g.d_wt[128 * e:128 * e + 1, 8192:9216], g.be[e:e + 1, :],
            waits=(("sy", P.val("sy")),) if e == 0 else ())
    for e in range(E):
        for j in range(8):
            w1 = dma(g, "sy", g.stage_w[:, 0:1024],
                     g.d_webf[1024 * e:1024 * e + 1024, 128 * j:128 * j + 128], transpose=True,
                     waits=g.r_stagew.for_write() + ([("gp", g.w_cast)] if (e == 0 and j == 0) else []))
            g.r_stagew.wrote("sy", w1)
            w2 = dma(g, "sy", g.d_wt[128 * e:128 * e + 128, 1024 * j:1024 * j + 1024],
                     g.stage_w[:, 0:1024])
            g.r_stagew.read("sy", w2)
    g.w_wt = P.val("sy")



def emit_mla(g):
    P = g.P
    xT = lambda c: g.fmA[:, 2048 * c:2048 * c + 2048]
    dump_ckv = (g.stage == "ckv")
    if dump_ckv:
        dma(g, "sy", g.dbg5, g.fmA[:, 0:16384], waits=(("sy", g.w_xT),))

    for nk in range(4):
        for c in range(8):
            m = mm(g, g.p_sc[:, 512 * nk:512 * nk + 512],
                   g.wsmall[:, 1024 + 128 * c:1024 + 128 * c + 128],
                   xT(c)[:, 512 * nk:512 * nk + 512], c == 0, c == 7,
                   waits=([("sy", g.w_xT), ("sy", g.w_small)] + g.r_psc.for_write()) if (nk == 0 and c == 0) else ())
        s = act(g, g.ckvT[:, 512 * nk:512 * nk + 512], g.p_sc[:, 512 * nk:512 * nk + 512],
                AF.Copy, waits=(("mm", m),))
        g.r_psc.read("sc", s)
    g.r_psc.wrote("mm", m)
    s_ckv = s
    for nk in range(4):
        for c in range(8):
            m = mm(g, g.p_ctx[:, 0:512], g.wsmall[:, 128 * c:128 * c + 128],
                   xT(c)[:, 512 * nk:512 * nk + 512], c == 0, c == 7,
                   waits=g.r_pctx.for_write() if c == 0 else ())
        g.r_pctx.wrote("mm", m)
        s = act(g, g.q2T[:, 512 * nk:512 * nk + 512], g.p_ctx[:, 0:512], AF.Copy,
                waits=g.r_pctx.for_read())
        g.r_pctx.read("sc", s)
    s_q2 = s
    g.r_fmA.read("mm", m)
    for si in range(16):
        m = tr_mm(g, g.p_m1[:, 0:128], g.ckvT[:, 128 * si:128 * si + 128], g.identbf,
                  waits=([("sc", s_ckv)] if si == 0 else []) + g.r_pm1.for_write())
        g.r_pm1.wrote("mm", m)
        s = act(g, g.ckvS[:, 128 * si:128 * si + 128], g.p_m1[:, 0:128], AF.Copy,
                waits=g.r_pm1.for_read())
        g.r_pm1.read("sc", s)
    s_ckvs = s
    for mt in range(8):
        for nk in range(4):
            m = mm(g, g.p_m2[:, 0:512], g.wsmall[:, 2048 + 128 * mt:2048 + 128 * mt + 128],
                   g.q2T[:, 512 * nk:512 * nk + 512], True, True,
                   waits=([("sc", s_q2)] if (mt == 0 and nk == 0) else []) + g.r_pm2.for_write()
                   + (g.r_fmA.for_write() if (mt == 0 and nk == 0) else []))
            g.r_pm2.wrote("mm", m)
            s = act(g, g.fmA[:, 2048 * mt + 512 * nk:2048 * mt + 512 * nk + 512],
                    g.p_m2[:, 0:512], AF.Copy, waits=g.r_pm2.for_read())
            g.r_pm2.read("sc", s)
    g.r_fmA.wrote("sc", s)
    s_qT = s
    if dump_ckv:
        w = dma(g, "sy", g.dbg, g.ckvT, waits=(("sc", s_ckv), ("ve", g.v_keff), ("ve", g.v_veff)))
        w = dma(g, "sy", g.dbg2, g.fmA[:, 0:16384], waits=(("sc", s_qT),))
        w = dma(g, "sy", g.dbg3, g.keff[:, 0:1024])
        w = dma(g, "sy", g.dbg4, g.veff[:, 0:1024])
        g.P.emit("ve", (lambda: lambda e: e.memset(g.stat[:, 15:16], 0.0))(),
                 waits=(("sy", w),), inc=("ve", 1))
        return
    g.w_maskc_q = dma(g, "scq", g.maskc, g.c_maskc, waits=(("mm", P.val("mm")),))

    gctr_box = [0]

    def ql_res(h):
        return g.r_qlb if h % 2 == 0 else g.r_qlb2

    def ql_ap(h):
        return g.qlb[:, 512 * (h % 2):512 * (h % 2) + 512]

    def emit_ql(tc, h, first=False):
        hb = 64 * (h % 2)
        lhsT_ql = g.keff[hb:hb + 64, 128 * (h // 2):128 * (h // 2) + 128]
        rhs_q = g.fmA[hb:hb + 64, 2048 * (h // 2) + 512 * tc:2048 * (h // 2) + 512 * tc + 512]
        m = mm(g, g.p_m2[:, 0:512], lhsT_ql, rhs_q, True, True,
               waits=([("ve", g.v_keff), ("sc", s_qT)] if first else [])
               + g.r_pm2.for_write() + (g.r_fmA.for_read() if first else []))
        g.r_pm2.wrote("mm", m)
        r = ql_res(h)
        v = ve(g, "tensor_copy", out=ql_ap(h), in_=g.p_m2[:, 0:512],
               waits=g.r_pm2.for_read() + r.for_write())
        g.r_pm2.read("ve", v)
        r.wrote("ve", v)

    def emit_scores(tc, h, gi):
        reg = gctr_box[0] % 2
        psc = g.p_sc[:, 1024 * reg:1024 * reg + 1024]
        rsc = g.r_pscs[reg]
        abres = g.abres[gctr_box[0] % 4]
        ab = g.attb[:, 1024 * (gctr_box[0] % 4):1024 * (gctr_box[0] % 4) + 1024]
        gctr_box[0] += 1
        r = ql_res(h)
        for j in range(2):
            si = 2 * gi + j
            m = mm(g, psc[:, 512 * j:512 * j + 512],
                   g.ckvT[:, 128 * si:128 * si + 128], ql_ap(h), True, True,
                   waits=(r.for_read() + rsc.for_write()) if j == 0 else ())
        rsc.wrote("mm", m)
        r.read("mm", m)
        s = act(g, ab, psc, AF.Exp, scale=0.125,
                waits=rsc.for_read() + abres.for_write())
        rsc.read("sc", s)
        abres.wrote("sc", s)
        if gi >= 2 * tc:
            v = ve(g, "tensor_mul", out=ab, in0=ab,
                   in1=g.maskc[:, 1024 * (gi - 2 * tc):1024 * (gi - 2 * tc) + 1024],
                   waits=abres.for_read() + [("scq", g.w_maskc_q)])
            abres.wrote("ve", v)
        return (tc, h, gi, ab, abres)

    def emit_ctxden(p):
        tc, h, gi, ab, abres = p
        ngi = 2 * (tc + 1)
        for j in range(2):
            si = 2 * gi + j
            wts = []
            wts2 = []
            if j == 0:
                wts += abres.for_read()
                if gi == 0:
                    wts += [("sc", s_ckvs)]
                    wts += g.r_pctx.for_write()
                if h == 0 and gi == 0:
                    wts2 += g.r_pden.for_write()
            m1 = mm(g, g.p_ctx[:, 0:512], g.ckvS[:, 128 * si:128 * si + 128],
                    ab[:, 512 * j:512 * j + 512], (gi == 0 and j == 0),
                    (gi == ngi - 1 and j == 1), waits=wts)
            m2 = mm(g, g.p_den[0:16, 0:512], g.oh16bf[:, 16 * h:16 * h + 16],
                    ab[:, 512 * j:512 * j + 512], (h == 0 and gi == 0 and j == 0),
                    (h == NH - 1 and gi == ngi - 1 and j == 1), waits=wts2)
        abres.read("mm", m2)
        if gi == ngi - 1:
            g.r_pctx.wrote("mm", m1)
            v = ve(g, "tensor_copy", out=g.ctxall[:, 512 * h:512 * h + 512],
                   in_=g.p_ctx[:, 0:512],
                   waits=g.r_pctx.for_read() + (g.r_ctxall.for_write() if h == 0 else []))
            g.r_pctx.read("ve", v)
        if h == NH - 1 and gi == ngi - 1:
            g.r_pden.wrote("mm", m2)

    def emit_epilogue(tc):
        v = ve(g, "reciprocal", out=g.r16[0:16, 0:512], in_=g.p_den[0:16, 0:512],
               waits=g.r_pden.for_read() + g.r_r16.for_write())
        g.r_pden.read("ve", v)
        g.r_r16.wrote("ve", v)
        for sub in range(4):
            m = tr_mm(g, g.p_m1[:, 0:16], g.r16[0:16, 128 * sub:128 * sub + 128], g.identf,
                      waits=g.r_r16.for_read() + g.r_pm1.for_write())
            g.r_pm1.wrote("mm", m)
            v = ve(g, "tensor_copy", out=g.rtok[:, 16 * sub:16 * sub + 16], in_=g.p_m1[:, 0:16],
                   waits=g.r_pm1.for_read() + (g.r_rtok.for_write() if sub == 0 else []))
            g.r_pm1.read("ve", v)
        g.r_r16.read("mm", m)
        g.r_rtok.wrote("ve", v)
        vv = v
        for pr in range(8):
            for hh in range(2):
                h = 2 * pr + hh
                m = mm(g, g.p_m2[64 * hh:64 * hh + 64, 0:512], g.veff[:, 64 * h:64 * h + 64],
                       g.ctxall[:, 512 * h:512 * h + 512], True, True,
                       waits=(g.r_ctxall.for_read() + [("ve", g.v_veff)] + g.r_pm2.for_write()) if hh == 0 else ())
            g.r_pm2.wrote("mm", m)
            v = ve(g, "tensor_copy", out=g.u65[:, 0:512], in_=g.p_m2[:, 0:512],
                   waits=g.r_pm2.for_read() + g.r_u65.for_write())
            g.r_pm2.read("ve", v)
            g.r_u65.wrote("ve", v)
            for sub in range(4):
                trp, rt = (g.p_m1, g.r_pm1) if sub % 2 == 0 else (g.p_den, g.r_pden)
                m = tr_mm(g, trp[:, 0:128], g.u65[:, 128 * sub:128 * sub + 128], g.identbf,
                          waits=g.r_u65.for_read() + rt.for_write())
                rt.wrote("mm", m)
                ti = 4 * tc + sub
                for hh in range(2):
                    h = 2 * pr + hh
                    xs = g.x_tok[:, 1024 * ti + 64 * h:1024 * ti + 64 * h + 64]
                    g.P.emit("ve", (lambda o, i0, s1, i1, pm: lambda e: e.scalar_tensor_tensor(
                        out=o, in0=i0, scalar=s1, in1=i1, op0=ALU.mult, op1=ALU.add))(
                        xs, trp[0:128, 64 * hh:64 * hh + 64],
                        g.rtok[:, 16 * sub + h:16 * sub + h + 1], xs, trp),
                        waits=tuple(rt.for_read() + (g.r_rtok.for_read() if hh == 0 else [])
                                    + ([("scq", g.w_x[ti])] if (tc == 0 and pr == 0) else [])),
                        inc=("ve", 1))
                vv = g.P.val("ve")
                rt.read("ve", vv)
                g.r_u65.read("mm", m)
        g.r_rtok.read("ve", vv)
        g.r_ctxall.read("mm", g.P.val("mm"))
        if g.stage not in ("xy", "ckv"):
            for sub in range(4):
                ti = 4 * tc + sub
                xs_t = g.x_tok[:, 1024 * ti:1024 * ti + 1024]
                last = emit_ln_once(g, xs_t, xs_t, None, affine=False,
                                    waits0=((("ve", vv),) if sub == 0 else ()))
                w = dma(g, "sy", g.d_x1f[128 * ti:128 * ti + 128, :], xs_t, waits=[last])
                g.r_xtok.read("sy", w)
                wb = dma(g, "gp", g.d_x1bf[128 * ti:128 * ti + 128, :], xs_t, waits=[last])
                g.r_xtok.read("gp", wb)
            g.w_x1spill = w
            g.w_x1bf = wb
        return g.P.val("ve")

    all_items = [(tc, h, gi) for tc in range(NCK) for h in range(NH)
                 for gi in range(2 * (tc + 1))]
    emit_ql(0, 0, first=True)
    pq = []
    due_epi = None

    def flush_one():
        nonlocal due_epi
        if due_epi is not None:
            emit_epilogue(due_epi)
            due_epi = None
        p = pq.pop(0)
        emit_ctxden(p)
        ptc, ph, pgi = p[0], p[1], p[2]
        if ph == NH - 1 and pgi == 2 * (ptc + 1) - 1:
            due_epi = ptc

    for tc, h, gi in all_items:
        pq.append(emit_scores(tc, h, gi))
        if gi == 0:
            if h + 1 < NH:
                emit_ql(tc, h + 1)
            elif tc + 1 < NCK:
                emit_ql(tc + 1, 0)
        if len(pq) > 2:
            flush_one()
    while pq:
        flush_one()
    vv_last = emit_epilogue(NCK - 1)
    g.r_fmA.read("mm", g.P.val("mm"))
    g.r_xtok.wrote("ve", vv_last)
    g.v_mla = vv_last



def emit_ln_once(g, src, dst, sq_scratch, affine, waits0=(), so=0):
    st = g.stat[:, so:so + 16]
    ve(g, "bn_stats", out=st[:, 0:6], in_=src[:, 0:512], waits=waits0)
    ve(g, "bn_stats", out=st[:, 6:12], in_=src[:, 512:1024], waits=waits0)
    ve(g, "bn_aggr", out=st[:, 12:14], in_=st[:, 0:12])
    v5 = ve(g, "tensor_scalar", out=st[:, 14:15], in0=st[:, 13:14], scalar1=EPS,
            scalar2=None, op0=ALU.add)
    s2 = act(g, st[:, 15:16], st[:, 14:15], AF.Sqrt, waits=(("ve", v5),))
    v6 = ve(g, "reciprocal", out=st[:, 0:1], in_=st[:, 15:16], waits=(("sc", s2),))
    v7 = ve(g, "tensor_mul", out=st[:, 1:2], in0=st[:, 12:13], in1=st[:, 0:1])
    v8 = ve(g, "tensor_scalar", out=st[:, 2:3], in0=st[:, 1:2], scalar1=-1.0,
            scalar2=None, op0=ALU.mult)
    s3 = ve(g, "tensor_scalar", out=dst, in0=src, scalar1=st[:, 0:1],
            scalar2=st[:, 2:3], op0=ALU.mult, op1=ALU.add)
    if affine:
        v9 = ve(g, "tensor_mul", out=dst, in0=dst, in1=g.gbc[:, 0:1024],
                waits=(("ve", g.v_gbc),))
        v9 = ve(g, "tensor_add", out=dst, in0=dst, in1=g.gbc[:, 1024:2048])
        return ("ve", v9)
    return ("ve", s3)


def emit_postln1(g):
    P = g.P
    w2 = 0
    for c in range(8):
        w2 = dma(g, "sy", g.fmA[:, 2048 * c:2048 * c + 2048], g.d_x1bf[:, 128 * c:128 * c + 128],
                 transpose=True, waits=([("gp", g.w_x1bf)] + g.r_fmA.for_write() + g.r_fmA.for_read()) if c == 0 else ())
    g.r_fmA.wrote("sy", w2)
    g.w_x1T = w2



def emit_ffn_attn(g):
    P = g.P
    g.w_ffnc_q = dma(g, "scq", g.ones2048bf, g.c_ones2048bf, waits=(("mm", P.val("mm")),))
    g.w_ffnc_q = dma(g, "scq", g.selbf, g.c_selbf)
    x1T = lambda c: g.fmA[:, 2048 * c:2048 * c + 2048]
    v_ones = ve(g, "memset",
                g.vs65[:, :].rearrange("p (a b) -> p a b", b=65)[:, :, 64:65], 1.0,
                waits=g.r_vs65.for_write())
    g.r_vs65.wrote("ve", v_ones)

    for proj in range(3):
        wsrc = g.d_wqkvbf[1024 * proj:1024 * proj + 1024, :]
        for kc in range(8):
            w = dma(g, "sy", g.wqkvT[:, 1024 * kc:1024 * kc + 1024],
                    wsrc[0:1024, 128 * kc:128 * kc + 128], transpose=True,
                    waits=(g.r_wqkvT.for_write() + [("gp", g.w_cast)]) if kc == 0 else ())
        g.r_wqkvT.wrote("sy", w)
        for mt in range(8):
            for nk in range(4):
                pp, rr = (g.p_m2, g.r_pm2) if ((mt * 4 + nk) % 2 == 0) else (g.p_den, g.r_pden)
                m = mm(g, pp[:, 0:512], g.wqkvT[:, 128 * mt:128 * mt + 128],
                       x1T(0)[:, 512 * nk:512 * nk + 512], True, False,
                       waits=(g.r_wqkvT.for_read() + [("sy", g.w_x1T), ("sy", g.w_bqkv)]
                              + rr.for_write()) if (mt == 0 and nk == 0) else rr.for_write())
                for kc in range(1, 8):
                    m = mm(g, pp[:, 0:512], g.wqkvT[:, 1024 * kc + 128 * mt:1024 * kc + 128 * mt + 128],
                           x1T(kc)[:, 512 * nk:512 * nk + 512], False, kc == 7)
                rr.wrote("mm", m)
                if proj < 2:
                    dst = g.fmB[:, 16384 * proj + 2048 * mt + 512 * nk:
                                16384 * proj + 2048 * mt + 512 * nk + 512]
                    s = ve(g, "tensor_scalar", out=dst, in0=pp[:, 0:512],
                           scalar1=g.bqkv[:, 8 * proj + mt:8 * proj + mt + 1],
                           scalar2=None, op0=ALU.add,
                           waits=rr.for_read() + (g.r_fmB.for_write() if (proj == 0 and mt == 0 and nk == 0) else []))
                    rr.read("ve", s)
                else:
                    abslot = (mt * 4 + nk) % 4
                    ares = g.abres[abslot]
                    abv = g.attb[:, 1024 * abslot:1024 * abslot + 512]
                    s = ve(g, "tensor_scalar", out=abv, in0=pp[:, 0:512],
                           scalar1=g.bqkv[:, 16 + mt:16 + mt + 1],
                           scalar2=None, op0=ALU.add,
                           waits=rr.for_read() + ares.for_write())
                    rr.read("ve", s)
                    ares.wrote("ve", s)
                    for sub in range(4):
                        si = 4 * nk + sub
                        for hh in range(2):
                            h = 2 * mt + hh
                            pt, rt = (g.p_m1, g.r_pm1) if ((sub * 2 + hh) % 2 == 0) else (g.p_ctx, g.r_pctx)
                            m2 = tr_mm(g, pt[:, 0:64],
                                       abv[64 * hh:64 * hh + 64, 128 * sub:128 * sub + 128],
                                       g.identbf,
                                       waits=(ares.for_read() if (sub == 0 and hh == 0) else []) + rt.for_write())
                            rt.wrote("mm", m2)
                            ares.read("mm", m2)
                            v = ve(g, "tensor_copy",
                                   out=g.vs65[:, 65 * (16 * h + si):65 * (16 * h + si) + 64],
                                   in_=pt[:, 0:64], waits=rt.for_read())
                            rt.read("ve", v)
        g.r_wqkvT.read("mm", g.P.val("mm"))
    g.r_fmB.wrote("sc", g.P.val("sc"))
    g.r_vs65.wrote("ve", g.P.val("ve"))
    s_qk = g.P.val("sc")

    for h in range(NH):
        pp, rr = (g.p_m2, g.r_pm2) if h % 2 == 0 else (g.p_den, g.r_pden)
        trp, rt = (g.p_m1, g.r_pm1) if h % 2 == 0 else (g.p_ctx, g.r_pctx)
        for si in range(16):
            m = mm(g, pp[0:65, si:si + 1], g.vs65[:, 65 * (16 * h + si):65 * (16 * h + si) + 65],
                   g.onescolbf[:], True, True,
                   waits=((g.r_vs65.for_read() + [("ve", g.v_ones), ("sy", g.w_consts)]
                           if h < 2 else []) + rr.for_write()) if si == 0 else ())
        rr.wrote("mm", m)
        v = ve(g, "tensor_copy", out=g.sv_s[0:65, 0:16],
               in_=pp[0:65, 0:16], waits=rr.for_read() + g.r_svs.for_write())
        rr.read("ve", v)
        g.r_svs.wrote("ve", v)
        s = act(g, g.stage_b[0:65, 0:16], g.sv_s[0:65, 0:16], AF.Copy,
                waits=g.r_svs.for_read() + g.r_stageb.for_write())
        g.r_svs.read("sc", s)
        g.r_stageb.wrote("sc", s)
        m = tr_mm(g, trp[0:16, 0:65], g.stage_b[0:65, 0:16], g.identbf,
                  waits=g.r_stageb.for_read() + rt.for_write())
        g.r_stageb.read("mm", m)
        rt.wrote("mm", m)
        v = ve(g, "tensor_copy", out=g.svT[0:16, 65 * h:65 * h + 65], in_=trp[0:16, 0:65],
               waits=rt.for_read() + (g.r_sv.for_write() if h == 0 else []))
        rt.read("ve", v)
    g.r_sv.wrote("ve", v)
    v_sv = v

    qTs = lambda h, t0: g.fmB[64 * (h % 2):64 * (h % 2) + 64,
                              2048 * (h // 2) + t0:2048 * (h // 2) + t0 + 512]
    kTs = lambda h, si: g.fmB[64 * (h % 2):64 * (h % 2) + 64,
                              16384 + 2048 * (h // 2) + 128 * si:16384 + 2048 * (h // 2) + 128 * si + 128]
    gctr_box = [0]

    def emit_scores(tc, h, gi):
        t0 = 512 * tc
        reg = gctr_box[0] % 2
        psc = g.p_sc[:, 1024 * reg:1024 * reg + 1024]
        rsc = g.r_pscs[reg]
        abres = g.abres[gctr_box[0] % 4]
        ab = g.attb[:, 1024 * (gctr_box[0] % 4):1024 * (gctr_box[0] % 4) + 1024]
        gctr_box[0] += 1
        for j in range(2):
            si = 2 * gi + j
            m = mm(g, psc[:, 512 * j:512 * j + 512], kTs(h, si), qTs(h, t0),
                   True, True,
                   waits=(rsc.for_write() + (g.r_fmB.for_read() if (tc == 0 and h == 0 and gi == 0) else [])
                          + ([("ve", v_sv)] if (tc == 0 and h == 0 and gi == 0) else [])) if j == 0 else ())
        rsc.wrote("mm", m)
        s = act(g, ab, psc, AF.Exp, scale=1.0 / 32.0,
                waits=rsc.for_read() + abres.for_write())
        rsc.read("sc", s)
        abres.wrote("sc", s)
        if gi >= 2 * tc:
            mk = g.maskc[:, 1024 * (gi - 2 * tc):1024 * (gi - 2 * tc) + 1024]
            v = ve(g, "select", out=ab, mask=mk.bitcast(mybir.dt.uint16), on_true=ab,
                   on_false=g.ones2048bf[:, 0:1024],
                   waits=abres.for_read() + [("scq", g.w_ffnc_q)])
            abres.wrote("ve", v)
        return (tc, h, gi, ab, abres)

    def emit_ctx(p):
        tc, h, gi, ab, abres = p
        ngi = 2 * (tc + 1)
        pc, rc = (g.p_ctx, g.r_pctx) if h % 2 == 0 else (g.p_m2, g.r_pm2)
        for j in range(2):
            si = 2 * gi + j
            wts = list(abres.for_read()) if j == 0 else []
            if gi == 0 and j == 0:
                wts += rc.for_write()
            m1 = mm(g, pc[0:65, 0:512],
                    g.vs65[:, 65 * (16 * h + si):65 * (16 * h + si) + 65],
                    ab[:, 512 * j:512 * j + 512], (gi == 0 and j == 0), False, waits=wts)
        abres.read("mm", m1)
        if gi != ngi - 1:
            return
        m1 = mm(g, pc[0:65, 0:512], g.svT[0:16, 65 * h:65 * h + 65],
                g.selbf[:, 512 * tc:512 * tc + 512], False, True,
                waits=g.r_sv.for_read() + [("scq", g.w_ffnc_q)])
        rc.wrote("mm", m1)
        g.r_sv.read("mm", m1)
        stg, rs = (g.den_s, g.r_dens) if h % 2 == 0 else (g.r16, g.r_r16)
        v_u = ve(g, "tensor_copy", out=stg[0:65, 0:512], in_=pc[0:65, 0:512],
                 waits=rc.for_read() + rs.for_write())
        rs.wrote("ve", v_u)
        rc.read("ve", v_u)
        return (tc, h, stg, rs)

    def emit_ctx_tail(p):
        tc, h, stg, rs = p
        for sub in range(4):
            trp, rt = (g.p_m1, g.r_pm1) if sub % 2 == 0 else (g.p_den, g.r_pden)
            m2 = tr_mm(g, trp[:, 0:65], stg[0:65, 128 * sub:128 * sub + 128],
                       g.identf, waits=rs.for_read() + rt.for_write())
            rt.wrote("mm", m2)
            rs.read("mm", m2)
            v = ve(g, "tensor_copy",
                   out=g.ctxall[:, 512 * h + 64 * sub:512 * h + 64 * sub + 64],
                   in_=trp[:, 0:64],
                   waits=rt.for_read() + (g.r_ctxall.for_write() if (h == 0 and sub == 0) else []))
            v2 = ve(g, "tensor_copy", out=g.rtok[:, 16 * sub + h:16 * sub + h + 1],
                    in_=trp[:, 64:65],
                    waits=(g.r_rtok.for_write() if (h == 0 and sub == 0) else ()))
            rt.read("ve", v2)
        if h == NH - 1:
            g.r_rtok.wrote("ve", v2)

    def emit_fepi(tc):
        v = ve(g, "reciprocal", out=g.rtok[:, 0:64], in_=g.rtok[:, 0:64])
        g.r_rtok.wrote("ve", v)
        for h in range(NH):
            for sub in range(4):
                ti = 4 * tc + sub
                g.P.emit("ve", (lambda o, i0, s1: lambda e: e.tensor_scalar(
                    out=o, in0=i0, scalar1=s1, scalar2=None, op0=ALU.mult))(
                    g.fmA[:, 1024 * ti + 64 * h:1024 * ti + 64 * h + 64],
                    g.ctxall[:, 512 * h + 64 * sub:512 * h + 64 * sub + 64],
                    g.rtok[:, 16 * sub + h:16 * sub + h + 1]),
                    waits=tuple(g.r_rtok.for_read() + g.r_ctxall.for_read()
                                + (g.r_fmA.for_write() if (tc == 0 and h == 0 and sub == 0) else [])),
                    inc=("ve", 1))
        vv = g.P.val("ve")
        g.r_ctxall.read("ve", vv)
        g.r_rtok.read("ve", vv)
        for sub in range(4):
            ti = 4 * tc + sub
            wh = dma(g, "sy", g.d_hidbf[128 * ti:128 * ti + 128, :],
                     g.fmA[:, 1024 * ti:1024 * ti + 1024], waits=(("ve", vv),) if sub == 0 else ())
            g.r_fmA.read("sy", wh)
        g.w_hidst = wh
        return vv

    all_items = [(tc, h, gi) for tc in range(NCK) for h in range(NH)
                 for gi in range(2 * (tc + 1))]
    pq = []
    due_epi = None
    due_tail = None

    def flush_one():
        nonlocal due_tail, due_epi
        if due_tail is not None:
            emit_ctx_tail(due_tail)
            due_tail = None
        if due_epi is not None:
            emit_fepi(due_epi)
            due_epi = None
        p = pq.pop(0)
        t = emit_ctx(p)
        if t is not None:
            due_tail = t
        ptc, ph, pgi = p[0], p[1], p[2]
        if ph == NH - 1 and pgi == 2 * (ptc + 1) - 1:
            due_epi = ptc

    for tc, h, gi in all_items:
        pq.append(emit_scores(tc, h, gi))
        if len(pq) > 2:
            flush_one()
    while pq:
        flush_one()
    if due_tail is not None:
        emit_ctx_tail(due_tail)
    vv = emit_fepi(NCK - 1)
    g.r_fmB.read("mm", g.P.val("mm"))
    g.r_vs65.read("mm", g.P.val("mm"))
    g.r_fmA.wrote("ve", vv)
    g.v_hid = vv



def emit_gbc(g):
    w = dma(g, "sy", g.stage_f[0:1, 0:1024], g.gamma, waits=g.r_stagef.for_write())
    w = dma(g, "sy", g.stage_f[0:1, 1024:2048], g.beta)
    g.r_stagef.wrote("sy", w)
    m = 0
    for r in range(2):
        for nk in range(2):
            src = g.stage_f[0:1, 1024 * r + 512 * nk:1024 * r + 512 * nk + 512]
            m = mm(g, g.p_m1[:, 0:512], g.ones128[:], src,
                   True, True, waits=g.r_stagef.for_read() + g.r_pm1.for_write()
                   + (g.r_mlakv.for_write() if (r == 0 and nk == 0) else []))
            g.r_pm1.wrote("mm", m)
            v = ve(g, "tensor_copy", out=g.gbc[:, 1024 * r + 512 * nk:1024 * r + 512 * nk + 512],
                   in_=g.p_m1[:, 0:512], waits=g.r_pm1.for_read())
            g.r_pm1.read("ve", v)
    g.r_stagef.read("mm", m)
    g.r_mlakv.wrote("ve", v)
    g.v_gbc = v


def emit_router(g):
    P = g.P
    emit_gbc(g)
    for c in range(8):
        w2 = dma(g, "sy", g.fmB[:, 2048 * c:2048 * c + 2048], g.d_hidbf[:, 128 * c:128 * c + 128],
                 transpose=True, waits=([("sy", g.w_hidst)] + g.r_fmB.for_write()) if c == 0 else ())
    g.r_fmB.wrote("sy", w2)
    for tc in range(NCK):
        for c in range(8):
            m = mm(g, g.p_m2[0:8, 0:512], g.wswT[:, 8 * c:8 * c + 8],
                   g.fmB[:, 2048 * c + 512 * tc:2048 * c + 512 * tc + 512], c == 0, c == 7,
                   waits=(g.r_fmB.for_read() + [("ve", g.v_wsw)] + g.r_pm2.for_write()) if c == 0 else ())
        g.r_pm2.wrote("mm", m)
        if tc == NCK - 1:
            g.r_fmB.read("mm", m)
        v = ve(g, "tensor_copy", out=g.den_s[0:8, 0:512], in_=g.p_m2[0:8, 0:512],
               waits=g.r_pm2.for_read() + g.r_dens.for_write())
        g.r_pm2.read("ve", v)
        g.r_dens.wrote("ve", v)
        for sub in range(4):
            ti = 4 * tc + sub
            m = tr_mm(g, g.p_m1[:, 0:8], g.den_s[0:8, 128 * sub:128 * sub + 128], g.identf,
                      waits=g.r_dens.for_read() + g.r_pm1.for_write())
            g.r_pm1.wrote("mm", m)
            v = ve(g, "tensor_copy", out=g.lg[:, 8 * ti:8 * ti + 8], in_=g.p_m1[:, 0:8],
                   waits=g.r_pm1.for_read())
            g.r_pm1.read("ve", v)
        g.r_dens.read("mm", m)
    for ti in range(NT):
        lg = g.lg[:, 8 * ti:8 * ti + 8]
        ve(g, "tensor_add", out=lg, in0=lg, in1=g.bcast8[:, 0:8],
           waits=(("ve", g.v_bsw),) if ti == 0 else ())
        v2 = ve(g, "reduce_max", out=g.stat[:, 0:1], in_=lg, axis=AX.X)
        vneg = ve(g, "tensor_scalar", out=g.stat[:, 1:2], in0=g.stat[:, 0:1], scalar1=-1.0,
                  scalar2=None, op0=ALU.mult)
        s = act(g, g.stage_f[:, 0:8], lg, AF.Exp, bias=g.stat[:, 1:2],
                waits=[("ve", vneg)] + g.r_stagef.for_write())
        g.r_stagef.wrote("sc", s)
        v3 = ve(g, "reduce_sum", out=g.stat[:, 2:3], in_=g.stage_f[:, 0:8], axis=AX.X,
                waits=g.r_stagef.for_read())
        g.r_stagef.read("ve", v3)
        ve(g, "reciprocal", out=g.rp[:, ti:ti + 1], in_=g.stat[:, 2:3])
        v4 = ve(g, "tensor_scalar", out=g.stage_f[:, 8:16], in0=lg, scalar1=g.stat[:, 0:1],
                scalar2=None, op0=ALU.is_ge)
        v5 = ve(g, "tensor_mul", out=g.stage_f[:, 16:24], in0=g.stage_f[:, 8:16], in1=g.iota8m[:])
        v6 = ve(g, "tensor_scalar", out=g.stage_f[:, 24:32], in0=g.stage_f[:, 16:24],
                scalar1=999.0, scalar2=None, op0=ALU.add)
        g.P.emit("ve", (lambda o, i: lambda e: e.tensor_reduce(
            out=o, in_=i, axis=AX.X, op=ALU.min))(g.route[:, ti:ti + 1], g.stage_f[:, 24:32]),
            waits=(("ve", v6),), inc=("ve", 1))
        ve(g, "tensor_scalar", out=g.onehot[:, 8 * ti:8 * ti + 8], in0=g.iota8[:],
           scalar1=g.route[:, ti:ti + 1], scalar2=None, op0=ALU.is_equal)
    v_route = g.P.val("ve")

    for ti in range(NT):
        m = mm(g, g.p_m2[0:8, ti:ti + 1], g.onehot[:, 8 * ti:8 * ti + 8], g.onescol_f[:],
               True, True,
               waits=([("ve", v_route), ("ve", g.v_ones)] + g.r_pm2.for_write()) if ti == 0 else ())
    g.r_pm2.wrote("mm", m)
    v = ve(g, "tensor_copy", out=g.small[0:8, 0:16], in_=g.p_m2[0:8, 0:16],
           waits=g.r_pm2.for_read() + g.r_small.for_write())
    g.r_pm2.read("ve", v)
    v_tot = ve(g, "reduce_sum", out=g.small[0:8, 48:49], in_=g.small[0:8, 0:16], axis=AX.X)
    g.r_small.wrote("ve", v_tot)
    m = tr_mm(g, g.p_m1[0:16, 0:8], g.small[0:8, 0:16], g.identf,
              waits=g.r_small.for_read() + g.r_pm1.for_write())
    g.r_pm1.wrote("mm", m)
    g.r_small.read("mm", m)
    v = ve(g, "tensor_copy", out=g.small[0:16, 16:24], in_=g.p_m1[0:16, 0:8],
           waits=g.r_pm1.for_read())
    g.r_pm1.read("ve", v)
    m = mm(g, g.p_m1[0:16, 0:8], g.sl16[:], g.small[0:16, 16:24], True, True,
           waits=[("ve", v), ("sy", g.w_consts)] + g.r_pm1.for_write())
    g.r_pm1.wrote("mm", m)
    v = ve(g, "tensor_copy", out=g.small[0:16, 24:32], in_=g.p_m1[0:16, 0:8],
           waits=g.r_pm1.for_read())
    g.r_pm1.read("ve", v)
    m = tr_mm(g, g.p_m1[0:8, 0:16], g.small[0:16, 24:32], g.identf,
              waits=[("ve", v)] + g.r_pm1.for_write())
    g.r_pm1.wrote("mm", m)
    v_excl = ve(g, "tensor_copy", out=g.small[0:8, 32:48], in_=g.p_m1[0:8, 0:16],
                waits=g.r_pm1.for_read())
    g.r_pm1.read("ve", v_excl)
    m = tr_mm(g, g.p_m1[0:1, 0:8], g.small[0:8, 48:49], g.identf,
              waits=[("ve", v_tot)] + g.r_pm1.for_write())
    g.r_pm1.wrote("mm", m)
    v = ve(g, "tensor_copy", out=g.cntrow[:, 0:8], in_=g.p_m1[0:1, 0:8], waits=g.r_pm1.for_read())
    g.r_pm1.read("ve", v)
    w1 = dma(g, "sy", g.d_ccin, g.cntrow[:, 0:8], waits=(("ve", v),))
    g.P.emit("gp", (lambda: lambda e: e.collective_compute(
        "AllGather", ALU.bypass, replica_groups=[[0, 1, 2, 3, 4, 5, 6, 7]],
        ins=[g.d_ccin.opt()], outs=[g.d_ccout.opt()]))(),
        waits=(("sy", w1),), inc=("gpc", 1))
    g.c_ccdone = g.P.val("gpc")

    m = mm(g, g.p_m1[0:16, 0:8], g.ones16[:], g.cntrow[:, 0:8], True, True,
           waits=[("ve", v)] + g.r_pm1.for_write())
    g.r_pm1.wrote("mm", m)
    v = ve(g, "tensor_copy", out=g.small[0:16, 50:58], in_=g.p_m1[0:16, 0:8],
           waits=g.r_pm1.for_read())
    g.r_pm1.read("ve", v)
    v = ve(g, "tensor_tensor", out=g.small[0:16, 50:58], in0=g.small[0:16, 50:58],
           in1=g.thr16[:], op=ALU.is_gt)
    m = mm(g, g.p_m1[0:8, 0:1], g.small[0:16, 50:58], g.onescol_f[0:16, 0:1], True, True,
           waits=[("ve", v), ("ve", g.v_ones)] + g.r_pm1.for_write())
    g.r_pm1.wrote("mm", m)
    v = ve(g, "tensor_scalar", out=g.small[0:8, 58:59], in0=g.p_m1[0:8, 0:1], scalar1=128.0,
           scalar2=None, op0=ALU.mult, waits=g.r_pm1.for_read())
    g.r_pm1.read("ve", v)
    m = mm(g, g.p_m2[0:8, 0:1], g.sl8[:], g.small[0:8, 58:59], True, True,
           waits=[("ve", v)] + g.r_pm2.for_write())
    g.r_pm2.wrote("mm", m)
    v_seg = ve(g, "tensor_copy", out=g.small[0:8, 59:60], in_=g.p_m2[0:8, 0:1],
               waits=g.r_pm2.for_read())
    g.r_pm2.read("ve", v_seg)
    m = tr_mm(g, g.p_m1[0:1, 0:8], g.small[0:8, 59:60], g.identf,
              waits=[("ve", v_seg)] + g.r_pm1.for_write())
    g.r_pm1.wrote("mm", m)
    v_segr = ve(g, "tensor_copy", out=g.segrow[:], in_=g.p_m1[0:1, 0:8], waits=g.r_pm1.for_read())
    g.r_pm1.read("ve", v_segr)
    m = mm(g, g.p_m1[:, 0:8], g.ones128[:], g.segrow[:], True, True,
           waits=[("ve", v_segr)] + g.r_pm1.for_write())
    g.r_pm1.wrote("mm", m)
    v_segb = ve(g, "tensor_copy", out=g.bcast8[:, 16:24], in_=g.p_m1[:, 0:8],
                waits=g.r_pm1.for_read())
    g.r_pm1.read("ve", v_segb)

    v_rpl = ve(g, "tensor_scalar", out=g.small[0:8, 32:48], in0=g.small[0:8, 32:48],
               scalar1=-1.0, scalar2=None, op0=ALU.add, waits=(("ve", v_excl),))
    pools = [(g.p_m2, g.r_pm2), (g.p_den, g.r_pden)]
    tpools = [(g.p_m1, g.r_pm1), (g.p_ctx, g.r_pctx)]
    for ti in range(NT):
        psu, rsu = pools[ti % 2]
        ptr_, rtr = tpools[ti % 2]
        m = mm(g, psu[:, 0:8], g.triu[:], g.onehot[:, 8 * ti:8 * ti + 8], True, True,
               waits=([("sy", g.w_consts)] if ti == 0 else []) + rsu.for_write())
        rsu.wrote("mm", m)
        m2 = tr_mm(g, ptr_[0:8, 0:128], g.onehot[:, 8 * ti:8 * ti + 8], g.identf,
                   waits=rtr.for_write())
        rtr.wrote("mm", m2)
        v = ve(g, "tensor_copy", out=g.den_s[0:8, 128 * (ti % 4):128 * (ti % 4) + 128],
               in_=ptr_[0:8, 0:128], waits=rtr.for_read() + (g.r_dens.for_write() if ti < 4 else []))
        rtr.read("ve", v)
        v = ve(g, "tensor_mul", out=g.stage_f[:, 0:8], in0=psu[:, 0:8],
               in1=g.onehot[:, 8 * ti:8 * ti + 8], waits=rsu.for_read() + g.r_stagef.for_write())
        rsu.read("ve", v)
        g.r_stagef.wrote("ve", v)
        v_par = ve(g, "reduce_sum", out=g.stat[:, 0:1], in_=g.stage_f[:, 0:8], axis=AX.X)
        g.r_stagef.read("ve", v_par)
        m3 = mm(g, ptr_[0:128, 8:9], g.den_s[0:8, 128 * (ti % 4):128 * (ti % 4) + 128],
                g.small[0:8, 32 + ti:32 + ti + 1], True, True,
                waits=[("ve", v_rpl), ("ve", g.P.val("ve"))] + rtr.for_write())
        rtr.wrote("mm", m3)
        v = ve(g, "tensor_add", out=g.posl[:, ti:ti + 1], in0=g.stat[:, 0:1],
               in1=ptr_[0:128, 8:9], waits=rtr.for_read())
        rtr.read("ve", v)
        v = ve(g, "tensor_mul", out=g.stage_f[:, 8:16], in0=g.onehot[:, 8 * ti:8 * ti + 8],
               in1=g.bcast8[:, 16:24], waits=(("ve", v_segb),) if ti == 0 else ())
        v = ve(g, "reduce_sum", out=g.stat[:, 1:2], in_=g.stage_f[:, 8:16], axis=AX.X)
        v = ve(g, "tensor_add", out=g.stat[:, 2:3], in0=g.stat[:, 1:2], in1=g.posl[:, ti:ti + 1])
        ve(g, "tensor_copy", out=g.dest32[:, ti:ti + 1], in_=g.stat[:, 2:3])
    g.v_dest = g.P.val("ve")

    v = ve(g, "memset", g.adjrow[:, 0:23], 0.0)
    for e in range(8):
        v1 = ve(g, "tensor_scalar", out=g.adjrow[:, 32:55], in0=g.iota23[:],
                scalar1=g.segrow[0:1, e:e + 1], scalar2=None, op0=ALU.is_ge,
                waits=(("ve", v_segr),) if e == 0 else ())
        v = ve(g, "tensor_add", out=g.adjrow[:, 0:23], in0=g.adjrow[:, 0:23],
               in1=g.adjrow[:, 32:55])
    v = ve(g, "tensor_scalar", out=g.adjrow[:, 0:23], in0=g.adjrow[:, 0:23], scalar1=128.0,
           scalar2=-128.0, op0=ALU.mult, op1=ALU.add)
    m = mm(g, g.p_m1[:, 0:23], g.ones128[:], g.adjrow[:, 0:23], True, True,
           waits=[("ve", v)] + g.r_pm1.for_write())
    g.r_pm1.wrote("mm", m)
    g.P.emit("ve", (lambda o, a, b: lambda e: e.scalar_tensor_tensor(
        out=o, in0=a, scalar=1.0, in1=b, op0=ALU.mult, op1=ALU.add))(
        g.small[:, 40:63], g.p_m1[:, 0:23], g.iotap[:].broadcast_to([128, 23])),
        waits=tuple(g.r_pm1.for_read() + g.r_small.for_write() + [("sy", g.w_consts),
                    ("ve", g.P.val("ve"))]),
        inc=("ve", 1))
    v = g.P.val("ve")
    g.r_pm1.read("ve", v)
    g.r_small.wrote("ve", v)
    g.v_widx = ve(g, "tensor_copy", out=g.widx32[:, 0:23], in_=g.small[:, 40:63],
                  waits=g.r_small.for_read())
    g.r_small.read("ve", g.v_widx)


def emit_kept(g):
    w2 = dma(g, "sy", g.gab[:], g.d_ccout, waits=(("gpc", g.c_ccdone),))
    m = mm(g, g.p_m1[0:1, 0:8], g.rmask[:], g.gab[:], True, True,
           waits=[("sy", w2), ("sy", g.w_consts)] + g.r_pm1.for_write())
    g.r_pm1.wrote("mm", m)
    v_co = ve(g, "tensor_copy", out=g.corow[:], in_=g.p_m1[0:1, 0:8], waits=g.r_pm1.for_read())
    g.r_pm1.read("ve", v_co)
    m = mm(g, g.p_m1[:, 0:8], g.ones128[:], g.corow[:], True, True,
           waits=[("ve", v_co)] + g.r_pm1.for_write())
    g.r_pm1.wrote("mm", m)
    v_cob = ve(g, "tensor_copy", out=g.bcast8[:, 8:16], in_=g.p_m1[:, 0:8],
               waits=g.r_pm1.for_read())
    g.r_pm1.read("ve", v_cob)
    for ti in range(NT):
        v = ve(g, "tensor_mul", out=g.stage_f[:, 0:8], in0=g.onehot[:, 8 * ti:8 * ti + 8],
               in1=g.bcast8[:, 8:16],
               waits=([("ve", v_cob)] + g.r_stagef.for_write()) if ti == 0 else ())
        v = ve(g, "reduce_sum", out=g.stat[:, 0:1], in_=g.stage_f[:, 0:8], axis=AX.X)
        v = ve(g, "tensor_add", out=g.stat[:, 1:2], in0=g.stat[:, 0:1], in1=g.posl[:, ti:ti + 1])
        ve(g, "tensor_scalar", out=g.kept[:, ti:ti + 1], in0=g.stat[:, 1:2],
           scalar1=float(CAP), scalar2=None, op0=ALU.is_lt)
    g.r_stagef.wrote("ve", g.P.val("ve"))
    g.v_kept = g.P.val("ve")



def emit_experts(g):
    P = g.P
    w_scat = 0
    for ti in range(NT):
        g.P.emit("gp", (lambda o, i, ix: lambda e: e.indirect_dma_start(
            out=o, out_offset=IndirectOffsetOnAxis(ap=ix, axis=0), in_=i, in_offset=None))(
            g.d_srt, g.fmA[:, 1024 * ti:1024 * ti + 1024], g.dest32[:, ti:ti + 1]),
            waits=((("ve", g.v_dest),) + tuple(g.r_fmA.for_read())) if ti == 0 else (),
            inc=("gp", 16))
        w_scat = g.P.val("gp")
    g.r_fmA.read("gp", w_scat)

    last_v_evac = [0, 0]
    last_st = [0, 0]
    last_m_ps = [0, 0]
    last_m_wb = [0, 0, 0]
    last_m_srt = [0, 0, 0, 0]
    last_m_ld = [0, 0, 0, 0]
    for f in range(NFILL):
        p2, p3, p4 = f % 2, f % 3, f % 4
        wb = g.fmB[:, 9216 * p3:9216 * p3 + 9216]
        g.P.emit("gp", (lambda o, ix: lambda e: e.indirect_dma_start(
            out=o, out_offset=None, in_=g.d_wt,
            in_offset=IndirectOffsetOnAxis(ap=ix, axis=0)))(wb, g.widx32[:, f:f + 1]),
            waits=((("ve", g.v_widx), ("sy", g.w_wt)) + tuple(g.r_fmB.for_read() + g.r_fmB.for_write())
                   if f < 3 else (("mm", last_m_wb[p3]),)),
            inc=("gp", 16))
        w_w = g.P.val("gp")
        sslot = g.ctxall[:, 1024 * p4:1024 * p4 + 1024]
        w_s = dma(g, "sy", sslot, g.d_srt[128 * f:128 * f + 128, :],
                  waits=([("gp", w_scat)] + g.r_ctxall.for_read() + g.r_ctxall.for_write())
                  if f < 4 else [("mm", last_m_ld[p4])])
        srtT = g.vs65[:, 1024 * p4:1024 * p4 + 1024]
        for kc in range(8):
            m = tr_mm(g, g.p_m1[:, 0:128], sslot[:, 128 * kc:128 * kc + 128],
                      g.identbf,
                      waits=([("sy", w_s)] + g.r_pm1.for_write()
                             + (g.r_vs65.for_write() if f < 4 else [("mm", last_m_srt[p4])]))
                      if kc == 0 else g.r_pm1.for_write())
            g.r_pm1.wrote("mm", m)
            if kc == 7:
                last_m_ld[p4] = m
            v = ve(g, "tensor_copy", out=srtT[:, 128 * kc:128 * kc + 128], in_=g.p_m1[:, 0:128],
                   waits=g.r_pm1.for_read())
            g.r_pm1.read("ve", v)
        v_t = v
        psl = g.p_sc[:, 1024 * p2:1024 * p2 + 1024]
        for nk in range(2):
            for kc in range(8):
                m = mm(g, psl[:, 512 * nk:512 * nk + 512], srtT[:, 128 * kc:128 * kc + 128],
                       wb[:, 1024 * kc + 512 * nk:1024 * kc + 512 * nk + 512], kc == 0, False,
                       waits=([("ve", v_t), ("gp", w_w), ("ve", last_v_evac[p2])]
                              + (g.r_psc.for_write() if f < 2 else [])) if (nk == 0 and kc == 0) else ())
            m = mm(g, psl[:, 512 * nk:512 * nk + 512], g.onesrowbf[:],
                   wb[0:1, 8192 + 512 * nk:8192 + 512 * nk + 512], False, True)
        last_m_ps[p2] = m
        last_m_wb[p3] = m
        last_m_srt[p4] = m
        v = ve(g, "tensor_copy", out=g.stage_f[:, 1024 * p2:1024 * p2 + 1024], in_=psl,
               waits=[("mm", m)] + (g.r_stagef.for_write() if f < 2
                                    else [("scq", last_st[p2])]))
        last_v_evac[p2] = v
        last_st[p2] = dma(g, "scq", g.d_srto[128 * f:128 * f + 128, :],
                          g.stage_f[:, 1024 * p2:1024 * p2 + 1024], waits=(("ve", v),))
    g.w_eo = P.val("scq")
    g.r_psc.wrote("mm", max(last_m_ps))
    g.r_stagef.wrote("ve", max(last_v_evac))
    g.r_fmB.wrote("gp", g.P.val("gp"))
    g.r_fmB.read("mm", max(last_m_ps))
    g.r_fmB.read("ve", max(last_v_evac))
    g.r_ctxall.wrote("sy", g.P.val("sy"))
    g.r_ctxall.read("mm", max(last_m_ps))
    g.r_vs65.wrote("ve", g.P.val("ve"))
    g.r_vs65.read("mm", max(last_m_ps))



def emit_tail(g):
    P = g.P
    emit_kept(g)
    slots = [g.attb[:, 0:2048].bitcast(F32), g.attb[:, 2048:4096].bitcast(F32),
             g.stage_f[:, 0:1024], g.stage_f[:, 1024:2048]]
    slot_res = [Res(), Res(), Res(), Res()]
    ybufs = [g.vs65[:, 4096 + 2048 * p:4096 + 2048 * p + 2048].bitcast(F32) for p in range(2)]
    ybuf_res = [Res(), Res()]
    w_gs = [0] * NT
    w_x1s = [0] * NT
    for ti in range(NT):
        sp = ti % 4
        g.P.emit("gp", (lambda o, ix: lambda e: e.indirect_dma_start(
            out=o, out_offset=None, in_=g.d_srto,
            in_offset=IndirectOffsetOnAxis(ap=ix, axis=0)))(
            slots[sp], g.dest32[:, ti:ti + 1]),
            waits=((("scq", g.w_eo), ("ve", g.v_kept)) + tuple(g.r_stagef.for_write() + g.r_attb0.for_write()
                   + g.r_attb1.for_write() + g.r_attb2.for_write() + g.r_attb3.for_write())
                   if ti == 0 else tuple(slot_res[sp].for_write())),
            inc=("gp", 16))
        w_gs[ti] = g.P.val("gp")
        slot_res[sp].wrote("gp", w_gs[ti])
        w_x1s[ti] = dma(g, "sy", g.x_tok[:, 1024 * ti:1024 * ti + 1024],
                        g.d_x1f[128 * ti:128 * ti + 128, :],
                        waits=g.r_xtok.for_write() if ti == 0 else ())
        if ti < 3:
            continue
        emit_tail_tile(g, ti - 3, slots, slot_res, ybufs, ybuf_res, w_gs, w_x1s)
    for ti in range(NT - 3, NT):
        emit_tail_tile(g, ti, slots, slot_res, ybufs, ybuf_res, w_gs, w_x1s)
    w_out = g.P.val("scq")
    g.P.emit("ve", (lambda: lambda e: e.memset(g.stat[:, 15:16], 0.0))(),
             waits=(("scq", w_out),), inc=("ve", 1))


def emit_tail_tile(g, ti, slots, slot_res, ybufs, ybuf_res, w_gs, w_x1s):
    sp = ti % 4
    yp = ti % 2
    so = 16 * yp
    eo = slots[sp]
    yb = ybufs[yp]
    x2 = yb[:, 0:1024]
    x3 = yb[:, 0:1024]
    v1 = ve(g, "tensor_mul", out=g.stat[:, 32 + 2 * yp:33 + 2 * yp], in0=g.rp[:, ti:ti + 1],
            in1=g.kept[:, ti:ti + 1])
    v2 = ve(g, "tensor_scalar", out=g.stat[:, 33 + 2 * yp:34 + 2 * yp],
            in0=g.stat[:, 32 + 2 * yp:33 + 2 * yp], scalar1=-1.0, scalar2=None,
            op0=ALU.mult)
    v2 = ve(g, "tensor_add", out=g.stat[:, 33 + 2 * yp:34 + 2 * yp],
            in0=g.stat[:, 33 + 2 * yp:34 + 2 * yp], in1=g.rp[:, ti:ti + 1])
    v3 = ve(g, "tensor_scalar", out=x2, in0=eo, scalar1=g.stat[:, 32 + 2 * yp:33 + 2 * yp],
            scalar2=None, op0=ALU.mult,
            waits=[("gp", w_gs[ti]), ("ve", v1)] + ybuf_res[yp].for_write())
    slot_res[sp].read("ve", v3)
    g.P.emit("ve", (lambda o, i0, s1, i1: lambda e: e.scalar_tensor_tensor(
        out=o, in0=i0, scalar=s1, in1=i1, op0=ALU.mult, op1=ALU.add))(
        x2, g.fmA[:, 1024 * ti:1024 * ti + 1024], g.stat[:, 33 + 2 * yp:34 + 2 * yp],
        x2), waits=(("ve", v3), ("ve", v2)), inc=("ve", 1))
    if g.stage == "tl0" and ti == 0:
        wd = dma(g, "sy", g.dbg6[0:128, :], eo, waits=(("gp", w_gs[0]),))
        wd = dma(g, "sy", g.dbg6[128:256, :], g.x_tok[:, 0:1024], waits=(("sy", w_x1s[0]),))
        wd = dma(g, "sy", g.dbg6[256:384, :], x2, waits=(("ve", g.P.val("ve")),))
    v4 = ve(g, "tensor_add", out=x2, in0=x2,
            in1=g.x_tok[:, 1024 * ti:1024 * ti + 1024], waits=(("sy", w_x1s[ti]),))
    if g.stage == "tl0" and ti == 0:
        wd = dma(g, "sy", g.dbg6[384:512, :], x2, waits=(("ve", v4),))
    emit_ln_once(g, x2, eo, None, affine=True, waits0=(("ve", v4),), so=so)
    if g.stage == "tl0" and ti == 0:
        wd = dma(g, "sy", g.dbg6[512:640, :], eo, waits=(("ve", g.P.val("ve")),))
    v5 = ve(g, "tensor_add", out=x3, in0=eo,
            in1=g.x_tok[:, 1024 * ti:1024 * ti + 1024])
    if g.stage == "tl0" and ti == 0:
        wd = dma(g, "sy", g.dbg6[640:768, :], x3, waits=(("ve", v5),))
        wd = dma(g, "sy", g.dbg6[768:769, 0:40], g.stat[0:1, 0:40], waits=(("ve", v5),))
        wd = dma(g, "sy", g.dbg6[769:897, 0:16], g.kept[:, 0:16], waits=(("ve", v5),))
        wd = dma(g, "sy", g.dbg6[897:1025, 0:16], g.rp[:, 0:16], waits=(("ve", v5),))
    slot_res[sp].read("ve", v5)
    last = emit_ln_once(g, x3, g.x_tok[:, 1024 * ti:1024 * ti + 1024], None, affine=False,
                        waits0=(("ve", v5),), so=so)
    vz = g.P.val("ve")
    ybuf_res[yp].wrote("ve", vz)
    ybuf_res[yp].read("ve", vz)
    w = dma(g, "scq", g.out[128 * ti:128 * ti + 128, :], g.x_tok[:, 1024 * ti:1024 * ti + 1024],
            waits=[last])
    g.r_xtok.read("scq", w)



def shard_inputs(inputs, core, consts):
    z = inputs
    m = {
        "x": np.ascontiguousarray(z["L0_attn_x"][core]),
        "wdq": z["L0_attn_W_dq"], "wuq": z["L0_attn_W_uq"], "wdkv": z["L0_attn_W_dkv"],
        "wuk": z["L0_attn_W_uk"], "wuv": z["L0_attn_W_uv"], "wo": z["L0_attn_W_o"],
        "wq": np.ascontiguousarray(z["L0_ffn_Wq"].reshape(D, D)),
        "wk": np.ascontiguousarray(z["L0_ffn_Wk"].reshape(D, D)),
        "wv": np.ascontiguousarray(z["L0_ffn_Wv"].reshape(D, D)),
        "bq": np.ascontiguousarray(z["L0_ffn_bq"].reshape(1, D)),
        "bk": np.ascontiguousarray(z["L0_ffn_bk"].reshape(1, D)),
        "bv": np.ascontiguousarray(z["L0_ffn_bv"].reshape(1, D)),
        "we": np.ascontiguousarray(z["L0_ffn_We"].reshape(E * D, D)),
        "be": np.ascontiguousarray(z["L0_ffn_be"]),
        "wsw": z["L0_ffn_Wsw"],
        "bsw": np.ascontiguousarray(z["L0_ffn_bsw"].reshape(1, E)),
        "gamma": np.ascontiguousarray(z["L0_ffn_gamma"].reshape(1, D)),
        "beta": np.ascontiguousarray(z["L0_ffn_beta"].reshape(1, D)),
        "rankmask": (np.arange(8) < core).astype(np.float32).reshape(8, 1),
    }
    m.update(consts)
    return m


_NC_CACHE = {}


def run(inputs, stage="full"):
    if stage not in _NC_CACHE:
        _NC_CACHE[stage] = build(stage)
    nc = _NC_CACHE[stage]
    consts = host_consts()
    in_maps = [shard_inputs(inputs, c, consts) for c in range(8)]
    return run_bass_kernel_spmd(nc, in_maps, core_ids=list(range(8)))


def kernel(**inputs):
    res = run(inputs, "full")
    return np.stack([res.results[c]["out"] for c in range(8)], axis=0)
